# version 03
# baseline (speedup 1.0000x reference)
import numpy as np
import contextlib
import concourse.bass as bass
import concourse.mybir as mybir
from concourse.bass_utils import run_bass_kernel_spmd

F32 = mybir.dt.float32
BF16 = mybir.dt.bfloat16
I32 = mybir.dt.int32
AF = mybir.ActivationFunctionType
ALU = mybir.AluOpType
AX = mybir.AxisListType

P = 128
D = 1024
PO = 8
T = 2048
TT = 256
NT = T // TT
CH = 128
CPT = TT // CH
NCH = T // CH
NH = 16
HD = 64
FF = 4096
NFC = FF // P
NEXP = 16
MI = 2048

ENGINES = ("sync", "scalar", "vector", "tensor", "gpsimd")

PH1A_PARTS = {"stats", "xn", "proj", "gk", "chunk", "spill"}


class Prog:

    def __init__(self, nc, sems, cnt):
        self.nc = nc
        self.sems = sems
        self.cnt = cnt
        self.steps = {e: [] for e in ENGINES}

    def c(self, key):
        return self.cnt.get(key, 0)

    def op(self, engine, fn, inc=None, n=1, waits=()):
        st = self.steps[engine]
        for (k, v) in waits:
            if v > 0:
                st.append(("w", k, v))
        if inc is not None:
            self.cnt[inc] = self.cnt.get(inc, 0) + n
            st.append(("o", fn, inc, n))
            return self.cnt[inc]
        st.append(("o", fn, None, 0))
        return None

    def dma(self, q, out, in_, waits=(), **kw):
        key = "d_" + q
        return self.op(
            q,
            (lambda e, o=out, i=in_, kw=dict(kw): e.dma_start(out=o, in_=i, **kw)),
            inc=key, n=16, waits=waits,
        )

    def nop(self, engine, waits=(), inc=None):
        assert inc is None
        for (k, v) in waits:
            if v > 0:
                self.steps[engine].append(("w", k, v))
        return None

    def mark(self, engine, inc, n=1):
        st = self.steps[engine]
        for i in range(len(st) - 1, -1, -1):
            if st[i][0] == "o":
                assert st[i][2] is None, "last op already carries an inc"
                self.cnt[inc] = self.cnt.get(inc, 0) + n
                st[i] = ("o", st[i][1], inc, n)
                return self.cnt[inc]
        raise AssertionError("no op to mark")

    def emit(self, enabled=True):
        if not enabled:
            return
        with self.nc.Block() as block:
            for e in ENGINES:
                steps = self.steps[e]
                if not steps:
                    continue

                def mk(steps=steps):
                    def body(eng):
                        for s in steps:
                            if s[0] == "w":
                                eng.wait_ge(self.sems[s[1]], s[2])
                            else:
                                ins = s[1](eng)
                                if s[2] is not None:
                                    ins.then_inc(self.sems[s[2]], s[3])
                    return body
                getattr(block, e)(mk())


def po_view(ap_or_handle, width):
    a = ap_or_handle if isinstance(ap_or_handle, bass.AP) else ap_or_handle.ap()
    return a.rearrange("p (po t) -> p po t", po=PO)



def build_l0(detect_races=True, phases=("ph0", "ph1a", "ph1b", "ph2", "ph3a", "ph3b", "ph4")):
    phases = set(phases)
    nc = bass.Bass("TRN2", target_bir_lowering=False, detect_race_conditions=detect_races)

    x_fm = nc.dram_tensor("x_fm", [P, PO * T], F32, kind="ExternalInput").ap()
    wq = nc.dram_tensor("wq", [P, PO * D], F32, kind="ExternalInput").ap()
    wk = nc.dram_tensor("wk", [P, PO * D], F32, kind="ExternalInput").ap()
    wv = nc.dram_tensor("wv", [P, PO * D], F32, kind="ExternalInput").ap()
    wg = nc.dram_tensor("wg", [P, PO * D], F32, kind="ExternalInput").ap()
    wout = nc.dram_tensor("wout", [P, PO * D], F32, kind="ExternalInput").ap()
    wgk1 = nc.dram_tensor("wgk1", [P, PO * 16], F32, kind="ExternalInput").ap()
    wgk2 = nc.dram_tensor("wgk2", [16, D], F32, kind="ExternalInput").ap()
    bgk2_col = nc.dram_tensor("bgk2_col", [P, PO], F32, kind="ExternalInput").ap()
    gamma_col = nc.dram_tensor("gamma_col", [P, PO], F32, kind="ExternalInput").ap()
    beta_col = nc.dram_tensor("beta_col", [P, PO], F32, kind="ExternalInput").ap()
    lng_col = nc.dram_tensor("lng_col", [P, PO], F32, kind="ExternalInput").ap()
    lnb_col = nc.dram_tensor("lnb_col", [P, PO], F32, kind="ExternalInput").ap()
    fc1b_col = nc.dram_tensor("fc1b_col", [P, NFC], F32, kind="ExternalInput").ap()
    fc2b_col = nc.dram_tensor("fc2b_col", [P, PO], F32, kind="ExternalInput").ap()
    fc1_q = nc.dram_tensor("fc1_q", [D, FF], I32, kind="ExternalInput").ap()
    fc1_sc = nc.dram_tensor("fc1_sc", [8, FF], F32, kind="ExternalInput").ap()
    fc2_q = nc.dram_tensor("fc2_q", [FF, D], I32, kind="ExternalInput").ap()
    fc2_sc = nc.dram_tensor("fc2_sc", [32, D], F32, kind="ExternalInput").ap()
    sbeta = nc.dram_tensor("sbeta", [P, 4], F32, kind="ExternalInput").ap()

    x1_out = nc.dram_tensor("x1_fm", [P, PO * T], F32, kind="ExternalOutput").ap()

    sp_kfm = nc.dram_tensor("sp_kfm", [NT, P, PO * TT], BF16)
    sp_qfm = nc.dram_tensor("sp_qfm", [NT, P, PO * TT], BF16)
    sp_gate = nc.dram_tensor("sp_gate", [NT, P, PO * TT], BF16)
    sp_An = nc.dram_tensor("sp_An", [NT, P, PO * TT], F32)
    sp_ktm = nc.dram_tensor("sp_ktm", [NT, P, CPT * D], BF16)
    sp_vt = nc.dram_tensor("sp_vt", [NT, P, CPT * D], BF16)
    w1bf = nc.dram_tensor("w1bf", [NFC, 8, P, P], BF16)
    w2bf = nc.dram_tensor("w2bf", [32, P, D], BF16)
    cc_in = nc.dram_tensor("cc_in", [P, PO * (HD + 1)], F32)
    cc_out = nc.dram_tensor("cc_out", [4 * P, PO * (HD + 1)], F32)

    sems = {}
    cnt = {}
    stack = contextlib.ExitStack()
    for k in ("d_sync", "d_gpsimd", "d_scalar", "pe", "act", "dve", "pool", "cc"):
        sems[k] = stack.enter_context(nc.semaphore(k))

    with stack:
        _st1 = contextlib.ExitStack()
        with _st1:
            xn_bf = _st1.enter_context(nc.sbuf_tensor("xn_bf", [P, PO * T], BF16))
            Call = _st1.enter_context(nc.sbuf_tensor("Call", [P, PO * NCH * HD], F32))
            Eall = _st1.enter_context(nc.sbuf_tensor("Eall", [P, PO * NCH], F32))
            cumE = _st1.enter_context(nc.sbuf_tensor("cumE", [P, PO * NCH], F32))
            Sinit = _st1.enter_context(nc.sbuf_tensor("Sinit", [P, PO * HD], F32))
            murow = _st1.enter_context(nc.sbuf_tensor("murow", [1, T], F32))
            rstdrow = _st1.enter_context(nc.sbuf_tensor("rstdrow", [1, T], F32))
            mrsrow = _st1.enter_context(nc.sbuf_tensor("mrsrow", [1, T], F32))
            ident_bf = _st1.enter_context(nc.sbuf_tensor("ident_bf", [P, P], BF16))
            ident_f = _st1.enter_context(nc.sbuf_tensor("ident_f", [P, P], F32))
            trimask = _st1.enter_context(nc.sbuf_tensor("trimask", [P, P], F32))
            zeros_f = _st1.enter_context(nc.sbuf_tensor("zeros_f", [P, P], F32))
            ccol = _st1.enter_context(nc.sbuf_tensor("ccol", [P, 8], F32))
            ones_row = _st1.enter_context(nc.sbuf_tensor("ones_row", [1, P], F32))
            gammas = _st1.enter_context(nc.sbuf_tensor("gammas", [P, PO * 2], F32))
            lns = _st1.enter_context(nc.sbuf_tensor("lns", [P, PO * 2], F32))
            fcb = _st1.enter_context(nc.sbuf_tensor("fcb", [P, NFC + PO], F32))
            bgk2s = _st1.enter_context(nc.sbuf_tensor("bgk2s", [P, PO], F32))
            sbetas = _st1.enter_context(nc.sbuf_tensor("sbetas", [P, 4], F32))
            rowtmp = _st1.enter_context(nc.sbuf_tensor("rowtmp", [1, TT], F32))
            rowtmp2 = _st1.enter_context(nc.sbuf_tensor("rowtmp2", [1, TT], F32))
            NEG1 = ccol[:, 0:1]
            NEG128 = ccol[:, 1:2]
            EPS = ccol[:, 2:3]
            ONEC = ccol[:, 4:5]

            p = Prog(nc, sems, cnt)
            for (buf, cmp) in ((ident_bf, ALU.not_equal), (ident_f, ALU.not_equal),
                               (trimask, ALU.is_gt)):
                p.op("gpsimd", (lambda g, b=buf: g.memset(b[:], 0.0)))
                p.op("gpsimd", (lambda g, b=buf, cm=cmp: g.affine_select(
                    out=b[:], in_=b[:], compare_op=cm,
                    fill=1.0, base=0, pattern=[[-1, P]], channel_multiplier=1)))
            p.op("gpsimd", lambda g: g.memset(zeros_f[:], 0.0))
            for (i, v) in ((0, -1.0), (1, -128.0), (2, 1e-5), (3, 1.0 / 1024.0), (4, 1.0)):
                p.op("gpsimd", (lambda g, i=i, v=v: g.memset(ccol[:, i:i + 1], v)))
            p.op("gpsimd", lambda g: g.memset(ones_row[:], 1.0))
            p.dma("sync", gammas[:, 0:PO], gamma_col)
            p.dma("sync", gammas[:, PO:2 * PO], beta_col)
            p.dma("sync", lns[:, 0:PO], lng_col)
            p.dma("sync", lns[:, PO:2 * PO], lnb_col)
            p.dma("sync", fcb[:, 0:NFC], fc1b_col)
            p.dma("sync", fcb[:, NFC:], fc2b_col)
            p.dma("sync", bgk2s[:], bgk2_col)
            p.dma("sync", sbetas[:], sbeta)
            p.emit("ph0" in phases)

            _st2 = contextlib.ExitStack()
            with _st2:
                wk_s = _st2.enter_context(nc.sbuf_tensor("wk_s", [P, PO * D], BF16))
                wv_s = _st2.enter_context(nc.sbuf_tensor("wv_s", [P, PO * D], BF16))
                wgk1_s = _st2.enter_context(nc.sbuf_tensor("wgk1_s", [P, PO * 16], BF16))
                wgk2_s = _st2.enter_context(nc.sbuf_tensor("wgk2_s", [16, D], BF16))
                xbuf = _st2.enter_context(nc.sbuf_tensor("xbuf", [P, PO * TT], F32))
                xsq = _st2.enter_context(nc.sbuf_tensor("xsq", [P, TT], F32))
                B1 = _st2.enter_context(nc.sbuf_tensor("B1", [P, TT], F32))
                B2 = _st2.enter_context(nc.sbuf_tensor("B2", [P, TT], F32))
                t1f = _st2.enter_context(nc.sbuf_tensor("t1f", [P, TT], F32))
                kfm_t = _st2.enter_context(nc.sbuf_tensor("kfm_t", [P, PO * TT], BF16))
                v32 = _st2.enter_context(nc.sbuf_tensor("v32", [P, PO * TT], F32))
                g32 = _st2.enter_context(nc.sbuf_tensor("g32", [P, PO * TT], F32))
                An_t = _st2.enter_context(nc.sbuf_tensor("An_t", [P, PO * TT], F32))
                mid_bf = _st2.enter_context(nc.sbuf_tensor("mid_bf", [16, TT], BF16))
                zf = _st2.enter_context(nc.sbuf_tensor("zf", [P, TT], F32))
                uf = _st2.enter_context(nc.sbuf_tensor("uf", [P, TT], F32))
                uf2 = _st2.enter_context(nc.sbuf_tensor("uf2", [P, TT], F32))
                expnA = _st2.enter_context(nc.sbuf_tensor("expnA", [P, PO * CH], F32))
                vt_fm = _st2.enter_context(nc.sbuf_tensor("vt_fm", [P, PO * CH], BF16))
                ktm_t = _st2.enter_context(nc.sbuf_tensor("ktm_t", [P, CPT * D], BF16))
                vt_t = _st2.enter_context(nc.sbuf_tensor("vt_t", [P, CPT * D], BF16))
                ps_a = _st2.enter_context(nc.psum_tensor("ps_a", [P, TT], F32))
                ps_b = _st2.enter_context(nc.psum_tensor("ps_b", [P, TT], F32))
                ps_s1 = _st2.enter_context(nc.psum_tensor("ps_s1", [1, TT], F32))
                ps_s2 = _st2.enter_context(nc.psum_tensor("ps_s2", [1, TT], F32))
                ps_mid = _st2.enter_context(nc.psum_tensor("ps_mid", [16, TT], F32))
                ps_t1 = _st2.enter_context(nc.psum_tensor("ps_t1", [P, CH], BF16))
                ps_t2 = _st2.enter_context(nc.psum_tensor("ps_t2", [P, CH], BF16))
                ps_c = _st2.enter_context(nc.psum_tensor("ps_c", [HD, HD], F32))
                p = Prog(nc, sems, cnt)
                p.dma("gpsimd", wk_s[:], wk)
                p.dma("gpsimd", wv_s[:], wv)
                p.dma("gpsimd", wgk1_s[:], wgk1)
                p.dma("gpsimd", wgk2_s[:], wgk2)
                wld = p.c("d_gpsimd")

                for t in range(NT):
                    t0 = t * TT
                    g_pe, g_act, g_dve = p.c("pe"), p.c("act"), p.c("dve")
                    xw = p.dma("sync", po_view(xbuf, TT), po_view(x_fm, T)[:, :, t0:t0 + TT],
                               waits=[("pe", g_pe), ("act", g_act), ("dve", g_dve)])
                    if "stats" not in PH1A_PARTS:
                        continue
                    for po in range(PO):
                        p.op("tensor", (lambda e, po=po: e.matmul(
                            ps_s1[:], ONEC, xbuf[:, po * TT:(po + 1) * TT],
                            start=(po == 0), stop=(po == PO - 1))),
                            waits=([("d_sync", xw)] if po == 0 else ()),
                            inc=("pe" if po == PO - 1 else None))
                    for po in range(PO):
                        aw = p.op("scalar", (lambda e, po=po: e.activation(
                            out=xsq[:], in_=xbuf[:, po * TT:(po + 1) * TT], func=AF.Square)),
                            waits=([("d_sync", xw), ("pe", g_pe)] if po == 0 else [("pe", p.c("pe"))]),
                            inc="act")
                        p.op("tensor", (lambda e, po=po: e.matmul(
                            ps_s2[:], ONEC, xsq[:],
                            start=(po == 0), stop=(po == PO - 1))),
                            waits=[("act", aw)], inc="pe")
                    sq_done = p.c("pe")
                    p.op("vector", (lambda e, t0=t0: e.tensor_scalar(
                        out=murow[0:1, t0:t0 + TT], in0=ps_s1[:], scalar1=1.0 / 1024.0,
                        scalar2=None, op0=ALU.mult)), waits=[("pe", sq_done)])
                    p.op("vector", lambda e: e.tensor_scalar(
                        out=rowtmp[:], in0=ps_s2[:], scalar1=1.0 / 1024.0,
                        scalar2=None, op0=ALU.mult))
                    p.op("vector", (lambda e, t0=t0: e.tensor_tensor(
                        out=rowtmp2[:], in0=murow[0:1, t0:t0 + TT],
                        in1=murow[0:1, t0:t0 + TT], op=ALU.mult)))
                    dv = p.op("vector", lambda e: e.tensor_tensor(
                        out=rowtmp[:], in0=rowtmp[:], in1=rowtmp2[:], op=ALU.subtract),
                        inc="dve")
                    av = p.op("scalar", lambda e: e.activation(
                        out=rowtmp2[:], in_=rowtmp[:], func=AF.Sqrt, bias=EPS[0:1, :], scale=1.0),
                        waits=[("dve", dv)], inc="act")
                    rcp = p.op("vector", (lambda e, t0=t0: e.reciprocal(
                        out=rstdrow[0:1, t0:t0 + TT], in_=rowtmp2[:])),
                        waits=[("act", av)], inc="dve")
                    dv = p.op("vector", (lambda e, t0=t0: e.tensor_tensor(
                        out=mrsrow[0:1, t0:t0 + TT], in0=murow[0:1, t0:t0 + TT],
                        in1=rstdrow[0:1, t0:t0 + TT], op=ALU.mult)),
                        waits=[("dve", rcp)], inc="dve")
                    bm = p.op("tensor", (lambda e, t0=t0: e.matmul(
                        ps_b[:], ones_row[:], rstdrow[0:1, t0:t0 + TT], start=True, stop=True)),
                        waits=[("dve", dv)], inc="pe")
                    ev = p.op("vector", lambda e: e.tensor_copy(out=B1[:], in_=ps_b[:]),
                              waits=[("pe", bm)], inc="dve")
                    bm = p.op("tensor", (lambda e, t0=t0: e.matmul(
                        ps_b[:], ones_row[:], mrsrow[0:1, t0:t0 + TT], start=True, stop=True)),
                        waits=[("dve", ev)], inc="pe")
                    ev = p.op("vector", lambda e: e.tensor_copy(out=B2[:], in_=ps_b[:]),
                              waits=[("pe", bm)], inc="dve")
                    for po in range(PO):
                        p.op("vector", (lambda e, po=po: e.tensor_tensor(
                            out=t1f[:], in0=xbuf[:, po * TT:(po + 1) * TT], in1=B1[:], op=ALU.mult)))
                        p.op("vector", (lambda e, po=po, t0=t0: e.tensor_tensor(
                            out=xn_bf[:, po * T + t0: po * T + t0 + TT], in0=t1f[:], in1=B2[:],
                            op=ALU.subtract)))
                    xn_done = p.mark("vector", "dve")
                    if "proj" not in PH1A_PARTS:
                        continue

                    for (wbuf, isk) in ((wk_s, True), (wv_s, False)):
                        for po in range(PO):
                            for kc in range(PO):
                                p.op("tensor", (lambda e, wbuf=wbuf, po=po, kc=kc, t0=t0: e.matmul(
                                    ps_a[:], wbuf[:, kc * D + po * P: kc * D + (po + 1) * P],
                                    xn_bf[:, kc * T + t0: kc * T + t0 + TT],
                                    start=(kc == 0), stop=(kc == PO - 1))),
                                    waits=([("dve", xn_done), ("d_gpsimd", wld)]
                                           if po == 0 and kc == 0 else ()))
                            mmv = p.mark("tensor", "pe")
                            if isk:
                                aw = p.op("scalar", (lambda e, po=po: e.activation(
                                    out=kfm_t[:, po * TT:(po + 1) * TT], in_=ps_a[:], func=AF.Copy)),
                                    waits=[("pe", mmv)], inc="act")
                                p.nop("tensor", waits=[("act", aw)])
                            else:
                                dw = p.op("vector", (lambda e, po=po: e.tensor_copy(
                                    out=v32[:, po * TT:(po + 1) * TT], in_=ps_a[:])),
                                    waits=[("pe", mmv)], inc="dve")
                                p.nop("tensor", waits=[("dve", dw)])
                    kv_done_act = p.c("act")
                    if "gk" not in PH1A_PARTS:
                        continue

                    for kc in range(PO):
                        p.op("tensor", (lambda e, kc=kc, t0=t0: e.matmul(
                            ps_mid[:], wgk1_s[:, kc * 16:(kc + 1) * 16],
                            xn_bf[:, kc * T + t0: kc * T + t0 + TT],
                            start=(kc == 0), stop=(kc == PO - 1))))
                    mmv = p.mark("tensor", "pe")
                    am = p.op("scalar", lambda e: e.activation(
                        out=mid_bf[:], in_=ps_mid[:], func=AF.Copy),
                        waits=[("pe", mmv)], inc="act")
                    for po in range(PO):
                        mm = p.op("tensor", (lambda e, po=po: e.matmul(
                            ps_b[:], wgk2_s[:, po * P:(po + 1) * P], mid_bf[:],
                            start=True, stop=True)),
                            waits=([("act", am)] if po == 0 else [("dve", p.c("dve"))]),
                            inc="pe")
                        p.op("scalar", (lambda e, po=po: e.activation(
                            out=zf[:], in_=ps_b[:], func=AF.Identity,
                            bias=bgk2s[:, po:po + 1], scale=1.0)), waits=[("pe", mm)])
                        p.op("scalar", lambda e: e.activation(out=uf[:], in_=zf[:], func=AF.Abs))
                        p.op("scalar", lambda e: e.activation(out=uf2[:], in_=uf[:], func=AF.Exp, scale=NEG1))
                        a2 = p.op("scalar", lambda e: e.activation(
                            out=uf[:], in_=uf2[:], func=AF.Ln, bias=1.0, scale=1.0), inc="act")
                        p.op("vector", lambda e: e.tensor_scalar(
                            out=zf[:], in0=zf[:], scalar1=0.0, scalar2=None, op0=ALU.min),
                            waits=[("act", a2)])
                        p.op("vector", lambda e: e.tensor_tensor(
                            out=uf[:], in0=uf[:], in1=zf[:], op=ALU.subtract))
                        p.op("vector", (lambda e, po=po: e.tensor_scalar(
                            out=g32[:, po * TT:(po + 1) * TT], in0=uf[:], scalar1=0.0625,
                            scalar2=None, op0=ALU.mult)), inc="dve")
                    g_done = p.c("dve")
                    if "chunk" not in PH1A_PARTS:
                        continue

                    for c in range(CPT):
                        ci = t * CPT + c
                        for po in range(PO):
                            p.op("vector", (lambda e, po=po, c=c: e.tensor_tensor_scan(
                                out=An_t[:, po * TT + c * CH: po * TT + (c + 1) * CH],
                                data0=g32[:, po * TT + c * CH: po * TT + (c + 1) * CH],
                                data1=zeros_f[:, 0:CH], initial=0.0,
                                op0=ALU.add, op1=ALU.add)),
                                waits=([("dve", g_done)] if po == 0 and c == 0 else ()))
                        sc_done = p.mark("vector", "dve")
                        p.op("scalar", (lambda e, c=c, ci=ci: e.activation(
                            out=po_view(Eall, NCH)[:, :, ci:ci + 1],
                            in_=po_view(An_t, TT)[:, :, c * CH + CH - 1: c * CH + CH],
                            func=AF.Exp, scale=NEG1)), waits=[("dve", sc_done)], inc="act")
                        a4 = p.op("scalar", (lambda e, c=c: e.activation(
                            out=po_view(expnA, CH),
                            in_=po_view(An_t, TT)[:, :, c * CH:(c + 1) * CH],
                            func=AF.Exp)), inc="act")
                        for po in range(PO):
                            p.op("vector", (lambda e, po=po, c=c: e.tensor_tensor(
                                out=vt_fm[:, po * CH:(po + 1) * CH],
                                in0=expnA[:, po * CH:(po + 1) * CH],
                                in1=v32[:, po * TT + c * CH: po * TT + (c + 1) * CH],
                                op=ALU.mult)), waits=([("act", a4)] if po == 0 else ()))
                        vt_done = p.mark("vector", "dve")
                        for po in range(PO):
                            pst = ps_t1 if po % 2 == 0 else ps_t2
                            mm = p.op("tensor", (lambda e, po=po, pst=pst: e.transpose(
                                pst[:], vt_fm[:, po * CH:(po + 1) * CH], ident_bf[:])),
                                waits=[("dve", (vt_done if po == 0 else p.c("dve")))], inc="pe")
                            p.op("vector", (lambda e, po=po, c=c, pst=pst: e.tensor_copy(
                                out=vt_t[:, c * D + po * CH: c * D + (po + 1) * CH], in_=pst[:])),
                                waits=[("pe", mm)], inc="dve")
                        for po in range(PO):
                            pst = ps_t1 if po % 2 == 0 else ps_t2
                            mm = p.op("tensor", (lambda e, po=po, c=c, pst=pst: e.transpose(
                                pst[:], kfm_t[:, po * TT + c * CH: po * TT + (c + 1) * CH],
                                ident_bf[:])),
                                waits=[("dve", p.c("dve")), ("act", kv_done_act)], inc="pe")
                            p.op("vector", (lambda e, po=po, c=c, pst=pst: e.tensor_copy(
                                out=ktm_t[:, c * D + po * CH: c * D + (po + 1) * CH], in_=pst[:])),
                                waits=[("pe", mm)], inc="dve")
                        tr_done = p.c("dve")
                        for h in range(NH):
                            hb = (h % 2) * HD
                            po = h // 2
                            mm = p.op("tensor", (lambda e, h=h, c=c: e.matmul(
                                ps_c[:], vt_t[:, c * D + h * HD: c * D + (h + 1) * HD],
                                ktm_t[:, c * D + h * HD: c * D + (h + 1) * HD],
                                start=True, stop=True)),
                                waits=[("dve", (tr_done if h == 0 else p.c("dve")))], inc="pe")
                            p.op("vector", (lambda e, hb=hb, po=po, ci=ci: e.tensor_scalar(
                                out=Call[hb:hb + HD, (po * NCH + ci) * HD:(po * NCH + ci + 1) * HD],
                                in0=ps_c[:],
                                scalar1=Eall[hb:hb + HD, po * NCH + ci: po * NCH + ci + 1],
                                scalar2=None, op0=ALU.mult)),
                                waits=[("pe", mm), ("act", p.c("act"))], inc="dve")
                    if "spill" not in PH1A_PARTS:
                        continue
                    p.dma("scalar", sp_An[t], An_t[:], waits=[("dve", p.c("dve")), ("act", p.c("act"))])
                    p.dma("scalar", sp_kfm[t], kfm_t[:])
                    p.dma("scalar", sp_ktm[t], ktm_t[:])
                    p.dma("scalar", sp_vt[t], vt_t[:])
                    spw = p.c("d_scalar")
                    p.nop("tensor", waits=[("d_scalar", spw)])
                    p.nop("vector", waits=[("d_scalar", spw)])
                    p.nop("scalar", waits=[("d_scalar", spw)])
                p.emit("ph1a" in phases)

            _st3 = contextlib.ExitStack()
            with _st3:
                wq_s = _st3.enter_context(nc.sbuf_tensor("wq_s", [P, PO * D], BF16))
                wg_s = _st3.enter_context(nc.sbuf_tensor("wg_s", [P, PO * D], BF16))
                qfm_t = _st3.enter_context(nc.sbuf_tensor("qfm_t", [P, PO * TT], BF16))
                gate_t = _st3.enter_context(nc.sbuf_tensor("gate_t", [P, PO * TT], BF16))
                ps_a1 = _st3.enter_context(nc.psum_tensor("ps_a1", [P, TT], F32))
                ps_b1 = _st3.enter_context(nc.psum_tensor("ps_b1", [P, TT], F32))
                p = Prog(nc, sems, cnt)
                p.dma("gpsimd", wq_s[:], wq)
                p.dma("gpsimd", wg_s[:], wg)
                wld = p.c("d_gpsimd")
                for t in range(NT):
                    t0 = t * TT
                    for po in range(PO):
                        for kc in range(PO):
                            p.op("tensor", (lambda e, po=po, kc=kc, t0=t0: e.matmul(
                                ps_a1[:], wq_s[:, kc * D + po * P: kc * D + (po + 1) * P],
                                xn_bf[:, kc * T + t0: kc * T + t0 + TT],
                                start=(kc == 0), stop=(kc == PO - 1))),
                                waits=([("d_gpsimd", wld)] if po == 0 and kc == 0 and t == 0 else ()))
                        m1 = p.mark("tensor", "pe")
                        a1 = p.op("scalar", (lambda e, po=po: e.activation(
                            out=qfm_t[:, po * TT:(po + 1) * TT], in_=ps_a1[:], func=AF.Copy)),
                            waits=[("pe", m1), ("d_scalar", p.c("d_scalar"))], inc="act")
                        for kc in range(PO):
                            p.op("tensor", (lambda e, po=po, kc=kc, t0=t0: e.matmul(
                                ps_b1[:], wg_s[:, kc * D + po * P: kc * D + (po + 1) * P],
                                xn_bf[:, kc * T + t0: kc * T + t0 + TT],
                                start=(kc == 0), stop=(kc == PO - 1))),
                                waits=([("act", a1)] if kc == 0 else ()))
                        m2 = p.mark("tensor", "pe")
                        a2 = p.op("scalar", (lambda e, po=po: e.activation(
                            out=gate_t[:, po * TT:(po + 1) * TT], in_=ps_b1[:], func=AF.Silu)),
                            waits=[("pe", m2)], inc="act")
                        p.nop("tensor", waits=[("act", a2)])
                    p.dma("scalar", sp_qfm[t], qfm_t[:], waits=[("act", p.c("act"))])
                    p.dma("scalar", sp_gate[t], gate_t[:])
                p.emit("ph1b" in phases)

            _st4 = contextlib.ExitStack()
            with _st4:
                tmp_s = _st4.enter_context(nc.sbuf_tensor("tmp_s", [P, HD], F32))
                gath = _st4.enter_context(nc.sbuf_tensor("gath", [P, PO * (HD + 1)], F32))
                p = Prog(nc, sems, cnt)
                for po in range(PO):
                    p.op("vector", (lambda e, po=po: e.tensor_copy(
                        out=cumE[:, po * NCH: po * NCH + 1], in_=Eall[:, po * NCH: po * NCH + 1])))
                for j in range(1, NCH):
                    for po in range(PO):
                        p.op("vector", (lambda e, po=po, j=j: e.tensor_scalar(
                            out=tmp_s[:],
                            in0=Call[:, (po * NCH + j - 1) * HD:(po * NCH + j) * HD],
                            scalar1=Eall[:, po * NCH + j: po * NCH + j + 1],
                            scalar2=None, op0=ALU.mult)))
                        p.op("vector", (lambda e, po=po, j=j: e.tensor_tensor(
                            out=Call[:, (po * NCH + j) * HD:(po * NCH + j + 1) * HD],
                            in0=Call[:, (po * NCH + j) * HD:(po * NCH + j + 1) * HD],
                            in1=tmp_s[:], op=ALU.add)))
                        p.op("vector", (lambda e, po=po, j=j: e.tensor_tensor(
                            out=cumE[:, po * NCH + j: po * NCH + j + 1],
                            in0=cumE[:, po * NCH + j - 1: po * NCH + j],
                            in1=Eall[:, po * NCH + j: po * NCH + j + 1], op=ALU.mult)))
                for po in range(PO):
                    p.op("vector", (lambda e, po=po: e.tensor_copy(
                        out=gath[:, po * (HD + 1): po * (HD + 1) + HD],
                        in_=Call[:, (po * NCH + NCH - 1) * HD:(po * NCH + NCH) * HD])))
                    p.op("vector", (lambda e, po=po: e.tensor_copy(
                        out=gath[:, po * (HD + 1) + HD: (po + 1) * (HD + 1)],
                        in_=cumE[:, po * NCH + NCH - 1: po * NCH + NCH])))
                dv = p.mark("vector", "dve")
                p.dma("gpsimd", cc_in.ap(), gath[:], waits=[("dve", dv)])
                ccw = p.c("d_gpsimd")
                p.op("gpsimd", lambda e: e.collective_compute(
                    "AllGather", ALU.bypass,
                    replica_groups=[[0, 1, 2, 3], [4, 5, 6, 7]],
                    ins=[cc_in.ap().opt()], outs=[cc_out.ap().opt()]),
                    waits=[("d_gpsimd", ccw)], inc="cc")
                p.emit("ph2" in phases)

                p = Prog(nc, sems, cnt)
                mz = p.op("gpsimd", lambda e: e.memset(Sinit[:], 0.0), inc="pool")
                for r in range(4):
                    ld = p.dma("sync", gath[:], cc_out[r * P:(r + 1) * P, :].opt(),
                               waits=([("cc", p.c("cc"))] if r == 0 else [("dve", p.c("dve"))]))
                    p.nop("vector", waits=[("d_sync", ld), ("pool", mz)])
                    for po in range(PO):
                        p.op("vector", (lambda e, po=po: e.tensor_scalar(
                            out=tmp_s[:],
                            in0=Sinit[:, po * HD:(po + 1) * HD],
                            scalar1=gath[:, po * (HD + 1) + HD: (po + 1) * (HD + 1)],
                            scalar2=None, op0=ALU.mult)))
                        p.op("vector", (lambda e, po=po: e.tensor_tensor(
                            out=tmp_s[:], in0=tmp_s[:],
                            in1=gath[:, po * (HD + 1): po * (HD + 1) + HD], op=ALU.add)))
                        p.op("vector", (lambda e, po=po: e.tensor_tensor(
                            out=tmp_s[:], in0=tmp_s[:],
                            in1=Sinit[:, po * HD:(po + 1) * HD], op=ALU.subtract)))
                        p.op("vector", (lambda e, po=po, r=r: e.scalar_tensor_tensor(
                            out=Sinit[:, po * HD:(po + 1) * HD],
                            in0=tmp_s[:], scalar=sbetas[:, r:r + 1],
                            in1=Sinit[:, po * HD:(po + 1) * HD],
                            op0=ALU.mult, op1=ALU.add)), inc="dve")
                for cch in range(NCH - 1, 0, -1):
                    for po in range(PO):
                        p.op("vector", (lambda e, po=po, cch=cch: e.tensor_scalar(
                            out=tmp_s[:], in0=Sinit[:, po * HD:(po + 1) * HD],
                            scalar1=cumE[:, po * NCH + cch - 1: po * NCH + cch],
                            scalar2=None, op0=ALU.mult)))
                        p.op("vector", (lambda e, po=po, cch=cch: e.tensor_tensor(
                            out=Call[:, (po * NCH + cch) * HD:(po * NCH + cch + 1) * HD],
                            in0=Call[:, (po * NCH + cch - 1) * HD:(po * NCH + cch) * HD],
                            in1=tmp_s[:], op=ALU.add)))
                for po in range(PO):
                    p.op("vector", (lambda e, po=po: e.tensor_copy(
                        out=Call[:, (po * NCH) * HD:(po * NCH + 1) * HD],
                        in_=Sinit[:, po * HD:(po + 1) * HD])))
                p.emit("ph2" in phases)

            _gffn_cm = nc.sbuf_tensor("gffn", [P, PO * T], BF16)
            gffn = _gffn_cm.__enter__()

            _st5 = contextlib.ExitStack()
            with _st5:
                qi_b = _st5.enter_context(nc.sbuf_tensor("qi_b", [P, FF], I32))
                scb = _st5.enter_context(nc.sbuf_tensor("scb", [P, FF], F32))
                qf_b = _st5.enter_context(nc.sbuf_tensor("qf_b", [P, FF], F32))
                wrow = _st5.enter_context(nc.sbuf_tensor("wrow", [P, FF], BF16))
                p = Prog(nc, sems, cnt)
                for kc in range(8):
                    lq = p.dma("sync", qi_b[:], fc1_q[kc * P:(kc + 1) * P, :],
                               waits=[("d_scalar", p.c("d_scalar"))])
                    ls = p.dma("gpsimd", scb[:], fc1_sc.partition_broadcast(P)[:, kc, :],
                               waits=[("dve", p.c("dve"))])
                    aw = p.op("scalar", lambda e: e.activation(
                        out=qf_b[:], in_=qi_b[:], func=AF.Identity, bias=NEG128, scale=1.0),
                        waits=[("d_sync", lq)], inc="act")
                    dw = p.op("vector", lambda e: e.tensor_tensor(
                        out=wrow[:], in0=qf_b[:], in1=scb[:], op=ALU.mult),
                        waits=[("act", aw), ("d_gpsimd", ls)], inc="dve")
                    sw = p.dma("scalar",
                               w1bf.ap().rearrange("n k p q -> p n k q")[:, :, kc, :],
                               wrow.ap().rearrange("p (n q) -> p n q", n=NFC),
                               waits=[("dve", dw)])
                    p.nop("sync", waits=[("d_scalar", sw)])
                for kc in range(32):
                    lq = p.dma("sync", qi_b[:, 0:D], fc2_q[kc * P:(kc + 1) * P, :],
                               waits=[("d_scalar", p.c("d_scalar"))])
                    ls = p.dma("gpsimd", scb[:, 0:D], fc2_sc.partition_broadcast(P)[:, kc, :],
                               waits=[("dve", p.c("dve"))])
                    aw = p.op("scalar", lambda e: e.activation(
                        out=qf_b[:, 0:D], in_=qi_b[:, 0:D], func=AF.Identity, bias=NEG128, scale=1.0),
                        waits=[("d_sync", lq)], inc="act")
                    dw = p.op("vector", lambda e: e.tensor_tensor(
                        out=wrow[:, 0:D], in0=qf_b[:, 0:D], in1=scb[:, 0:D], op=ALU.mult),
                        waits=[("act", aw), ("d_gpsimd", ls)], inc="dve")
                    sw = p.dma("scalar", w2bf[kc], wrow[:, 0:D], waits=[("dve", dw)])
                    p.nop("sync", waits=[("d_scalar", sw)])
                p.emit("ph3a" in phases)

            _st6 = contextlib.ExitStack()
            with _st6:
                w1sl = _st6.enter_context(nc.sbuf_tensor("w1sl", [P, 8 * 8 * P], BF16))
                w2sl = _st6.enter_context(nc.sbuf_tensor("w2sl", [P, 8 * D], BF16))
                ymid = _st6.enter_context(nc.sbuf_tensor("ymid", [P, NFC * TT], BF16))
                lnin = _st6.enter_context(nc.sbuf_tensor("lnin", [P, PO * TT], BF16))
                zacc = _st6.enter_context(nc.sbuf_tensor("zacc", [P, PO * TT], F32))
                xchk = _st6.enter_context(nc.sbuf_tensor("xchk", [P, TT], F32))
                B13 = _st6.enter_context(nc.sbuf_tensor("B13", [P, TT], F32))
                B23 = _st6.enter_context(nc.sbuf_tensor("B23", [P, TT], F32))
                t3a = _st6.enter_context(nc.sbuf_tensor("t3a", [P, TT], F32))
                t3b = _st6.enter_context(nc.sbuf_tensor("t3b", [P, TT], F32))
                ps_f1 = _st6.enter_context(nc.psum_tensor("ps_f1", [P, TT], F32))
                ps_f2 = _st6.enter_context(nc.psum_tensor("ps_f2", [P, TT], F32))
                ps_bc3 = _st6.enter_context(nc.psum_tensor("ps_bc3", [P, TT], F32))
                p = Prog(nc, sems, cnt)
                for t in range(NT):
                    t0 = t * TT
                    g_dve = p.c("dve")
                    bm = p.op("tensor", (lambda e, t0=t0: e.matmul(
                        ps_bc3[:], ones_row[:], rstdrow[0:1, t0:t0 + TT], start=True, stop=True)),
                        waits=[("dve", g_dve), ("act", p.c("act"))], inc="pe")
                    ev1 = p.op("vector", lambda e: e.tensor_copy(out=B13[:], in_=ps_bc3[:]),
                               waits=[("pe", bm)], inc="dve")
                    bm = p.op("tensor", (lambda e, t0=t0: e.matmul(
                        ps_bc3[:], ones_row[:], mrsrow[0:1, t0:t0 + TT], start=True, stop=True)),
                        waits=[("dve", ev1)], inc="pe")
                    ev2 = p.op("vector", lambda e: e.tensor_copy(out=B23[:], in_=ps_bc3[:]),
                               waits=[("pe", bm)], inc="dve")
                    for kc in range(PO):
                        p.op("vector", (lambda e, kc=kc, t0=t0: e.tensor_scalar(
                            out=lnin[:, kc * TT:(kc + 1) * TT],
                            in0=xn_bf[:, kc * T + t0: kc * T + t0 + TT],
                            scalar1=lns[:, kc:kc + 1], scalar2=lns[:, PO + kc:PO + kc + 1],
                            op0=ALU.mult, op1=ALU.add)))
                    lnin_done = p.mark("vector", "dve")

                    for slab in range(4):
                        lw = p.dma("gpsimd",
                                   w1sl.ap().rearrange("p (n k q) -> p n k q", n=8, k=8),
                                   w1bf[slab * 8:(slab + 1) * 8].rearrange("n k p q -> p n k q"),
                                   waits=[("pe", p.c("pe"))])
                        for nch in range(8):
                            n_g = slab * 8 + nch
                            for kc in range(PO):
                                p.op("tensor", (lambda e, nch=nch, kc=kc: e.matmul(
                                    ps_f1[:], w1sl[:, (nch * 8 + kc) * P:(nch * 8 + kc + 1) * P],
                                    lnin[:, kc * TT:(kc + 1) * TT],
                                    start=(kc == 0), stop=(kc == PO - 1))),
                                    waits=([("d_gpsimd", lw), ("dve", lnin_done)]
                                           if nch == 0 and kc == 0 else ()))
                            mmv = p.mark("tensor", "pe")
                            aw = p.op("scalar", (lambda e, n_g=n_g: e.activation(
                                out=ymid[:, n_g * TT:(n_g + 1) * TT], in_=ps_f1[:], func=AF.Silu,
                                bias=fcb[:, n_g:n_g + 1], scale=1.0)),
                                waits=[("pe", mmv)], inc="act")
                            p.nop("tensor", waits=[("act", aw)])
                    for slab2 in range(4):
                        lw2 = p.dma("gpsimd",
                                    w2sl.ap().rearrange("p (k m) -> p k m", k=8),
                                    w2bf[slab2 * 8:(slab2 + 1) * 8].rearrange("k p m -> p k m"),
                                    waits=[("pe", p.c("pe"))])
                        for mpo in range(PO):
                            for kk in range(8):
                                p.op("tensor", (lambda e, kk=kk, mpo=mpo, slab2=slab2: e.matmul(
                                    ps_f2[:], w2sl[:, kk * D + mpo * P: kk * D + (mpo + 1) * P],
                                    ymid[:, (slab2 * 8 + kk) * TT:(slab2 * 8 + kk + 1) * TT],
                                    start=(kk == 0), stop=(kk == 7))),
                                    waits=([("d_gpsimd", lw2), ("act", p.c("act"))]
                                           if kk == 0 and mpo == 0 else ()))
                            mmv = p.mark("tensor", "pe")
                            if slab2 == 0:
                                dw = p.op("vector", (lambda e, mpo=mpo: e.tensor_copy(
                                    out=zacc[:, mpo * TT:(mpo + 1) * TT], in_=ps_f2[:])),
                                    waits=[("pe", mmv)], inc="dve")
                            else:
                                dw = p.op("vector", (lambda e, mpo=mpo: e.tensor_tensor(
                                    out=zacc[:, mpo * TT:(mpo + 1) * TT],
                                    in0=zacc[:, mpo * TT:(mpo + 1) * TT],
                                    in1=ps_f2[:], op=ALU.add)),
                                    waits=[("pe", mmv)], inc="dve")
                            p.nop("tensor", waits=[("dve", dw)])
                    for mpo in range(PO):
                        lx = p.dma("sync", xchk[:],
                                   po_view(x_fm, T)[:, mpo, t0:t0 + TT],
                                   waits=[("dve", p.c("dve")), ("act", p.c("act"))])
                        p.op("vector", (lambda e, mpo=mpo: e.tensor_scalar(
                            out=t3a[:], in0=zacc[:, mpo * TT:(mpo + 1) * TT],
                            scalar1=fcb[:, NFC + mpo: NFC + mpo + 1], scalar2=None,
                            op0=ALU.add)))
                        p.op("vector", (lambda e, mpo=mpo: e.tensor_tensor(
                            out=t3b[:], in0=xchk[:], in1=B13[:], op=ALU.mult)),
                            waits=[("d_sync", lx)])
                        p.op("vector", lambda e: e.tensor_tensor(
                            out=t3b[:], in0=t3b[:], in1=B23[:], op=ALU.subtract))
                        d1 = p.op("vector", lambda e: e.scalar_tensor_tensor(
                            out=t3b[:], in0=t3a[:], scalar=0.5, in1=t3b[:],
                            op0=ALU.mult, op1=ALU.add), inc="dve")
                        a2 = p.op("scalar", (lambda e, mpo=mpo, t0=t0: e.activation(
                            out=gffn[:, mpo * T + t0: mpo * T + t0 + TT], in_=t3b[:], func=AF.Silu)),
                            waits=[("dve", d1)], inc="act")
                        p.nop("vector", waits=[("act", a2)])
                p.emit("ph3b" in phases)

            _st7 = contextlib.ExitStack()
            with _st7:
                wout_s = _st7.enter_context(nc.sbuf_tensor("wout_s", [P, PO * D], BF16))
                qg = _st7.enter_context(nc.sbuf_tensor("qg", [P, PO * TT], BF16))
                kg = _st7.enter_context(nc.sbuf_tensor("kg", [P, PO * TT], BF16))
                gateg = _st7.enter_context(nc.sbuf_tensor("gateg", [P, PO * TT], BF16))
                Ang = _st7.enter_context(nc.sbuf_tensor("Ang", [P, PO * TT], F32))
                ktg = _st7.enter_context(nc.sbuf_tensor("ktg", [P, CPT * D], BF16))
                vtg = _st7.enter_context(nc.sbuf_tensor("vtg", [P, CPT * D], BF16))
                Skv = _st7.enter_context(nc.sbuf_tensor("Skv", [P, (NH // 2) * HD], BF16))
                expA = _st7.enter_context(nc.sbuf_tensor("expA", [P, PO * CH], F32))
                mask_s = _st7.enter_context(nc.sbuf_tensor("mask_s", [P, CH], BF16))
                outb = _st7.enter_context(nc.sbuf_tensor("outb", [P, PO * CH], F32))
                sqb = _st7.enter_context(nc.sbuf_tensor("sqb", [P, CH], F32))
                B14 = _st7.enter_context(nc.sbuf_tensor("B14", [P, CH], F32))
                B24 = _st7.enter_context(nc.sbuf_tensor("B24", [P, CH], F32))
                lng4 = _st7.enter_context(nc.sbuf_tensor("lng4", [P, PO * TT], BF16))
                g32b = _st7.enter_context(nc.sbuf_tensor("g32b", [P, TT], F32))
                xob = _st7.enter_context(nc.sbuf_tensor("xob", [P, TT], F32))
                xg4 = _st7.enter_context(nc.sbuf_tensor("xg4", [P, PO * TT], F32))
                row4 = _st7.enter_context(nc.sbuf_tensor("row4", [1, CH], F32))
                row4b = _st7.enter_context(nc.sbuf_tensor("row4b", [1, CH], F32))
                ps_sc4 = _st7.enter_context(nc.psum_tensor("ps_sc4", [P, CH], F32))
                ps_mi4 = _st7.enter_context(nc.psum_tensor("ps_mi4", [HD, CH], F32))
                ps_st4 = _st7.enter_context(nc.psum_tensor("ps_st4", [HD, HD], F32))
                ps_s14 = _st7.enter_context(nc.psum_tensor("ps_s14", [1, CH], F32))
                ps_s24 = _st7.enter_context(nc.psum_tensor("ps_s24", [1, CH], F32))
                ps_bc4 = _st7.enter_context(nc.psum_tensor("ps_bc4", [P, CH], F32))
                ps_w4 = _st7.enter_context(nc.psum_tensor("ps_w4", [P, TT], F32))
                p = Prog(nc, sems, cnt)
                p.dma("gpsimd", wout_s[:], wout)
                wld = p.c("d_gpsimd")
                for t in range(NT):
                    t0 = t * TT
                    guard = [("pe", p.c("pe")), ("act", p.c("act")), ("dve", p.c("dve"))]
                    p.dma("sync", qg[:], sp_qfm[t], waits=guard)
                    p.dma("sync", kg[:], sp_kfm[t])
                    p.dma("sync", gateg[:], sp_gate[t])
                    lall_sync = p.c("d_sync")
                    p.dma("scalar", Ang[:], sp_An[t], waits=guard)
                    p.dma("scalar", ktg[:], sp_ktm[t])
                    p.dma("scalar", vtg[:], sp_vt[t])
                    lall_sc = p.c("d_scalar")
                    lxo = p.dma("gpsimd", xg4[:].rearrange("p (po t) -> p po t", po=PO),
                                po_view(x_fm, T)[:, :, t0:t0 + TT], waits=guard)
                    for c in range(CPT):
                        ci = t * CPT + c
                        ea = p.op("scalar", (lambda e, c=c: e.activation(
                            out=po_view(expA, CH),
                            in_=po_view(Ang, TT)[:, :, c * CH:(c + 1) * CH],
                            func=AF.Exp, scale=NEG1)),
                            waits=[("d_scalar", lall_sc), ("dve", p.c("dve"))], inc="act")
                        for h in range(NH):
                            hb = (h % 2) * HD
                            po = h // 2
                            mm = p.op("tensor", (lambda e, hb=hb, po=po, ci=ci: e.transpose(
                                ps_st4[:], Call[hb:hb + HD, (po * NCH + ci) * HD:(po * NCH + ci + 1) * HD],
                                ident_f[hb:hb + HD, hb:hb + HD])),
                                waits=[("act", p.c("act"))], inc="pe")
                            p.op("scalar", (lambda e, hb=hb, po=po: e.activation(
                                out=Skv[hb:hb + HD, po * HD:(po + 1) * HD], in_=ps_st4[:], func=AF.Copy)),
                                waits=[("pe", mm)], inc="act")
                        st_done = p.c("act")
                        for h in range(NH):
                            hb = (h % 2) * HD
                            po = h // 2
                            mm = p.op("tensor", (lambda e, hb=hb, po=po, c=c: e.matmul(
                                ps_sc4[:], kg[hb:hb + HD, po * TT + c * CH: po * TT + (c + 1) * CH],
                                qg[hb:hb + HD, po * TT + c * CH: po * TT + (c + 1) * CH],
                                start=True, stop=True)),
                                waits=([("d_sync", lall_sync), ("act", st_done)] if h == 0 else
                                       [("dve", p.c("dve"))]),
                                inc="pe")
                            msk = p.op("vector", (lambda e: e.tensor_tensor(
                                out=mask_s[:], in0=ps_sc4[:], in1=trimask[:], op=ALU.mult)),
                                waits=[("pe", mm)], inc="dve")
                            p.op("tensor", (lambda e, h=h, c=c: e.matmul(
                                ps_mi4[:], vtg[:, c * D + h * HD: c * D + (h + 1) * HD],
                                mask_s[:], start=True, stop=False)),
                                waits=[("dve", msk)])
                            mm2 = p.op("tensor", (lambda e, hb=hb, po=po, c=c: e.matmul(
                                ps_mi4[:], Skv[hb:hb + HD, po * HD:(po + 1) * HD],
                                qg[hb:hb + HD, po * TT + c * CH: po * TT + (c + 1) * CH],
                                start=False, stop=True)), inc="pe")
                            p.op("vector", (lambda e, hb=hb, po=po: e.tensor_tensor(
                                out=outb[hb:hb + HD, po * CH:(po + 1) * CH],
                                in0=ps_mi4[:], in1=expA[hb:hb + HD, po * CH:(po + 1) * CH],
                                op=ALU.mult)),
                                waits=[("pe", mm2), ("act", ea)], inc="dve")
                        out_done = p.c("dve")
                        for po in range(PO):
                            p.op("tensor", (lambda e, po=po: e.matmul(
                                ps_s14[:], ONEC, outb[:, po * CH:(po + 1) * CH],
                                start=(po == 0), stop=(po == PO - 1))),
                                waits=([("dve", out_done)] if po == 0 else ()))
                        p.mark("tensor", "pe")
                        for po in range(PO):
                            aw = p.op("scalar", (lambda e, po=po: e.activation(
                                out=sqb[:], in_=outb[:, po * CH:(po + 1) * CH], func=AF.Square)),
                                waits=([("dve", out_done)] if po == 0 else [("pe", p.c("pe"))]),
                                inc="act")
                            p.op("tensor", (lambda e, po=po: e.matmul(
                                ps_s24[:], ONEC, sqb[:],
                                start=(po == 0), stop=(po == PO - 1))),
                                waits=[("act", aw)], inc="pe")
                        stat_done = p.c("pe")
                        p.op("vector", lambda e: e.tensor_scalar(
                            out=row4[:], in0=ps_s14[:], scalar1=1.0 / 1024.0, scalar2=None,
                            op0=ALU.mult), waits=[("pe", stat_done)])
                        p.op("vector", lambda e: e.tensor_scalar(
                            out=row4b[:], in0=ps_s24[:], scalar1=1.0 / 1024.0, scalar2=None,
                            op0=ALU.mult))
                        p.op("vector", lambda e: e.tensor_tensor(
                            out=g32b[0:1, 0:CH], in0=row4[:], in1=row4[:], op=ALU.mult))
                        dv = p.op("vector", lambda e: e.tensor_tensor(
                            out=row4b[:], in0=row4b[:], in1=g32b[0:1, 0:CH], op=ALU.subtract),
                            inc="dve")
                        av = p.op("scalar", lambda e: e.activation(
                            out=g32b[0:1, 0:CH], in_=row4b[:], func=AF.Sqrt, bias=EPS[0:1, :], scale=1.0),
                            waits=[("dve", dv)], inc="act")
                        dv = p.op("vector", lambda e: e.reciprocal(out=row4b[:], in_=g32b[0:1, 0:CH]),
                                  waits=[("act", av)], inc="dve")
                        p.op("vector", lambda e: e.tensor_tensor(
                            out=row4[:], in0=row4[:], in1=row4b[:], op=ALU.mult),
                            waits=[("dve", dv)], inc="dve")
                        bm = p.op("tensor", lambda e: e.matmul(
                            ps_bc4[:], ones_row[:], row4b[:], start=True, stop=True),
                            waits=[("dve", p.c("dve"))], inc="pe")
                        e1 = p.op("vector", lambda e: e.tensor_copy(out=B14[:], in_=ps_bc4[:]),
                                  waits=[("pe", bm)], inc="dve")
                        bm = p.op("tensor", lambda e: e.matmul(
                            ps_bc4[:], ones_row[:], row4[:], start=True, stop=True),
                            waits=[("dve", e1)], inc="pe")
                        e2 = p.op("vector", lambda e: e.tensor_copy(out=B24[:], in_=ps_bc4[:]),
                                  waits=[("pe", bm)], inc="dve")
                        for po in range(PO):
                            p.op("vector", (lambda e, po=po: e.tensor_tensor(
                                out=sqb[:], in0=outb[:, po * CH:(po + 1) * CH], in1=B14[:],
                                op=ALU.mult)),
                                waits=([("dve", e2), ("pe", stat_done)] if po == 0 else ()))
                            p.op("vector", lambda e: e.tensor_tensor(
                                out=sqb[:], in0=sqb[:], in1=B24[:], op=ALU.subtract))
                            p.op("vector", (lambda e, po=po: e.tensor_scalar(
                                out=sqb[:], in0=sqb[:],
                                scalar1=gammas[:, po:po + 1], scalar2=gammas[:, PO + po:PO + po + 1],
                                op0=ALU.mult, op1=ALU.add)))
                            aw = p.op("scalar", (lambda e, po=po, c=c: e.activation(
                                out=g32b[:, 0:CH],
                                in_=gateg[:, po * TT + c * CH: po * TT + (c + 1) * CH],
                                func=AF.Copy)),
                                waits=[("d_sync", lall_sync), ("dve", p.c("dve"))], inc="act")
                            p.op("vector", (lambda e, po=po, c=c: e.tensor_tensor(
                                out=lng4[:, po * TT + c * CH: po * TT + (c + 1) * CH],
                                in0=sqb[:], in1=g32b[:, 0:CH], op=ALU.mult)),
                                waits=[("act", aw)], inc="dve")
                    ln_done = p.c("dve")
                    for po in range(PO):
                        for kc in range(PO):
                            p.op("tensor", (lambda e, po=po, kc=kc: e.matmul(
                                ps_w4[:], wout_s[:, kc * D + po * P: kc * D + (po + 1) * P],
                                lng4[:, kc * TT:(kc + 1) * TT],
                                start=(kc == 0), stop=(kc == PO - 1))),
                                waits=([("dve", ln_done), ("d_gpsimd", wld)]
                                       if po == 0 and kc == 0 else ()))
                        mmv = p.mark("tensor", "pe")
                        aw = p.op("scalar", (lambda e, po=po, t0=t0: e.activation(
                            out=g32b[:], in_=gffn[:, po * T + t0: po * T + t0 + TT], func=AF.Copy)),
                            waits=[("dve", p.c("dve"))], inc="act")
                        p.op("vector", (lambda e: e.tensor_tensor(
                            out=xob[:], in0=ps_w4[:], in1=g32b[:], op=ALU.mult)),
                            waits=[("pe", mmv), ("act", aw), ("d_gpsimd", lxo)])
                        dv = p.op("vector", (lambda e, po=po: e.tensor_tensor(
                            out=xob[:], in0=xob[:], in1=xg4[:, po * TT:(po + 1) * TT], op=ALU.add)),
                            inc="dve")
                        sw = p.dma("scalar", po_view(x1_out, T)[:, po, t0:t0 + TT],
                                   xob[:], waits=[("dve", dv)])
                        p.nop("vector", waits=[("d_scalar", sw)])
                        p.nop("tensor", waits=[("d_scalar", sw)])
                p.emit("ph4" in phases)

            _gffn_cm.__exit__(None, None, None)

            p = Prog(nc, sems, cnt)
            p.nop("sync", waits=[("d_scalar", cnt.get("d_scalar", 0))])
            p.emit("ph4" in phases)

    return nc


def _fm_layout(a2d):
    Tc = a2d.shape[0]
    return np.ascontiguousarray(
        a2d.T.reshape(PO, P, Tc).transpose(1, 0, 2).reshape(P, PO * Tc))


def _w_layout(w):
    return np.ascontiguousarray(
        w.reshape(PO, P, w.shape[1]).transpose(1, 0, 2).reshape(P, PO * w.shape[1]))


def _col_layout(v):
    n = v.shape[0] // P
    return np.ascontiguousarray(v.reshape(n, P).T)


def unfm_layout(a):
    Tc = a.shape[1] // PO
    return a.reshape(P, PO, Tc).transpose(1, 0, 2).reshape(D, Tc).T


def prep_l0_inputs(inputs):
    x = np.asarray(inputs["L0_attn_x"])
    base = {
        "wq": _w_layout(np.asarray(inputs["L0_attn_Wq"])),
        "wk": _w_layout(np.asarray(inputs["L0_attn_Wk"])),
        "wv": _w_layout(np.asarray(inputs["L0_attn_Wv"])),
        "wg": _w_layout(np.asarray(inputs["L0_attn_Wg"])),
        "wout": _w_layout(np.asarray(inputs["L0_attn_Wout"])),
        "wgk1": _w_layout(np.asarray(inputs["L0_attn_Wgk1"])),
        "wgk2": np.ascontiguousarray(np.asarray(inputs["L0_attn_Wgk2"])),
        "bgk2_col": _col_layout(np.asarray(inputs["L0_attn_bgk2"])),
        "gamma_col": _col_layout(np.asarray(inputs["L0_attn_gamma"])),
        "beta_col": _col_layout(np.asarray(inputs["L0_attn_beta"])),
        "lng_col": _col_layout(np.asarray(inputs["L0_ffn_ln_gamma"])),
        "lnb_col": _col_layout(np.asarray(inputs["L0_ffn_ln_beta"])),
        "fc1b_col": _col_layout(np.asarray(inputs["L0_ffn_fc1_bias"])),
        "fc2b_col": _col_layout(np.asarray(inputs["L0_ffn_fc2_bias"])),
        "fc1_q": np.ascontiguousarray(np.asarray(inputs["L0_ffn_fc1_q"])),
        "fc1_sc": np.ascontiguousarray(np.asarray(inputs["L0_ffn_fc1_scales"])),
        "fc2_q": np.ascontiguousarray(np.asarray(inputs["L0_ffn_fc2_q"])),
        "fc2_sc": np.ascontiguousarray(np.asarray(inputs["L0_ffn_fc2_scales"])),
    }
    maps = []
    for core in range(8):
        b, r = core // 4, core % 4
        m = dict(base)
        m["x_fm"] = _fm_layout(x[b, r * T:(r + 1) * T, :])
        sb = np.zeros((P, 4), np.float32)
        sb[:, :r] = 1.0
        m["sbeta"] = sb
        maps.append(m)
    return maps


QUAT_TAB = {
    (0, 0): (0, 1.0), (1, 0): (1, -1.0), (2, 0): (2, -1.0), (3, 0): (3, -1.0),
    (0, 1): (1, 1.0), (1, 1): (0, 1.0), (2, 1): (3, -1.0), (3, 1): (2, 1.0),
    (0, 2): (2, 1.0), (1, 2): (3, 1.0), (2, 2): (0, 1.0), (3, 2): (1, -1.0),
    (0, 3): (3, 1.0), (1, 3): (2, -1.0), (2, 3): (1, 1.0), (3, 3): (0, 1.0),
}

TS5 = 512
NT5 = T // TS5
NTC = T // P


def build_l1(detect_races=True):
    nc = bass.Bass("TRN2", target_bir_lowering=False, detect_race_conditions=detect_races)

    x1_fm = nc.dram_tensor("x1_fm", [P, PO * T], F32, kind="ExternalInput").ap()
    qw_r = nc.dram_tensor("qw_r", [P, 8 * 256], F32, kind="ExternalInput").ap()
    kw_r = nc.dram_tensor("kw_r", [P, 8 * 256], F32, kind="ExternalInput").ap()
    vw_r = nc.dram_tensor("vw_r", [P, 8 * 256], F32, kind="ExternalInput").ap()
    ow_r = nc.dram_tensor("ow_r", [P, 8 * 256], F32, kind="ExternalInput").ap()
    qb_r = nc.dram_tensor("qb_r", [1, 4 * 256], F32, kind="ExternalInput").ap()
    kb_r = nc.dram_tensor("kb_r", [1, 4 * 256], F32, kind="ExternalInput").ap()
    vb_r = nc.dram_tensor("vb_r", [1, 4 * 256], F32, kind="ExternalInput").ap()
    ob_r = nc.dram_tensor("ob_r", [1, 4 * 256], F32, kind="ExternalInput").ap()
    rw = nc.dram_tensor("rw", [D, 16], F32, kind="ExternalInput").ap()
    rb = nc.dram_tensor("rb", [1, 16], F32, kind="ExternalInput").ap()
    w1m = nc.dram_tensor("w1m", [NEXP, D, MI], F32, kind="ExternalInput").ap()
    b1m = nc.dram_tensor("b1m", [NEXP, P, MI // P], F32, kind="ExternalInput").ap()
    w2m = nc.dram_tensor("w2m", [NEXP, MI, D], F32, kind="ExternalInput").ap()
    b2m = nc.dram_tensor("b2m", [1, NEXP * D], F32, kind="ExternalInput").ap()
    jcol = nc.dram_tensor("jcol", [P, PO], F32, kind="ExternalInput").ap()

    out_fm = nc.dram_tensor("out_fm", [P, PO * T], F32, kind="ExternalOutput").ap()
    if DEBUG_L1:
        dbg_xn2 = nc.dram_tensor("dbg_xn2", [P, PO * T], BF16, kind="ExternalOutput").ap()
        dbg_ctm = nc.dram_tensor("dbg_ctm", [P, NTC * NEXP], F32, kind="ExternalOutput").ap()
        dbg_k = nc.dram_tensor("dbg_k", [P, PO * T], BF16, kind="ExternalOutput").ap()
        dbg_v = nc.dram_tensor("dbg_v", [P, NTC * NH * 65], BF16, kind="ExternalOutput").ap()
        dbg_q = nc.dram_tensor("dbg_q", [P, PO * T], BF16, kind="ExternalOutput").ap()
        dbg_oacc = nc.dram_tensor("dbg_oacc", [P, NH * 4 * 4 * 65], F32, kind="ExternalOutput").ap()
        dbg_s = nc.dram_tensor("dbg_s", [P, PO * T], BF16, kind="ExternalOutput").ap()
        dbg_z = nc.dram_tensor("dbg_z", [P, NTC * D], F32, kind="ExternalOutput").ap()

    sp_xn2 = nc.dram_tensor("sp_xn2", [P, PO * T], BF16)
    sp_s = nc.dram_tensor("sp_s", [P, PO * T], BF16)
    bias_sc = nc.dram_tensor("bias_sc", [4, PO, P], F32)
    ck_in = nc.dram_tensor("ck_in", [P, PO * T], BF16)
    ck_out = nc.dram_tensor("ck_out", [4 * P, PO * T], BF16)
    cv_in = nc.dram_tensor("cv_in", [P, NTC * NH * 65], BF16)
    cv_out = nc.dram_tensor("cv_out", [4 * P, NTC * NH * 65], BF16)

    sems = {}
    cnt = {}
    stack = contextlib.ExitStack()
    for k in ("d_sync", "d_gpsimd", "d_scalar", "pe", "act", "dve", "pool", "cc"):
        sems[k] = stack.enter_context(nc.semaphore(k))

    with stack:
        _sl1 = contextlib.ExitStack()
        with _sl1:
            ident_bf = _sl1.enter_context(nc.sbuf_tensor("ident_bf", [P, P], BF16))
            ccol = _sl1.enter_context(nc.sbuf_tensor("ccol", [P, 8], F32))
            ones_row = _sl1.enter_context(nc.sbuf_tensor("ones_row", [1, P], F32))
            c_tm = _sl1.enter_context(nc.sbuf_tensor("c_tm", [P, NTC * NEXP], F32))
            murow2 = _sl1.enter_context(nc.sbuf_tensor("murow2", [1, T], F32))
            rstdrow2 = _sl1.enter_context(nc.sbuf_tensor("rstdrow2", [1, T], F32))
            bcols = _sl1.enter_context(nc.sbuf_tensor("bcols", [P, 4 * PO], F32))
            rbs = _sl1.enter_context(nc.sbuf_tensor("rbs", [P, 16], F32))
            ps_j = _sl1.enter_context(nc.psum_tensor("ps_j", [1, PO], F32))
            ps_j2 = _sl1.enter_context(nc.psum_tensor("ps_j2", [P, 1], F32))

            NEG1 = ccol[:, 0:1]
            EPSL = ccol[:, 2:3]
            EPSQ = ccol[:, 3:4]
            ONEC = ccol[:, 4:5]
            NEG8 = ccol[:, 5:6]
            JBIAS = ccol[:, 6:7]

            p = Prog(nc, sems, cnt)
            p.op("gpsimd", lambda g: g.memset(ident_bf[:], 0.0), inc="pool")
            p.op("gpsimd", lambda g: g.affine_select(
                out=ident_bf[:], in_=ident_bf[:], compare_op=ALU.not_equal,
                fill=1.0, base=0, pattern=[[-1, P]], channel_multiplier=1),
                waits=[("pool", p.c("pool"))], inc="pool")
            for (i, v) in ((0, -1.0), (2, 1e-5), (3, 1e-6), (4, 1.0), (5, -8.0)):
                p.op("gpsimd", (lambda g, i=i, v=v: g.memset(ccol[:, i:i + 1], v)))
            p.op("gpsimd", lambda g: g.memset(ones_row[:], 1.0))
            p.dma("sync", rbs[:], rb.partition_broadcast(P)[:, 0, :])
            jw = p.dma("sync", bcols[:, 0:PO], jcol)
            aj = p.op("scalar", lambda e: e.activation(
                out=bcols[:, PO:2 * PO], in_=bcols[:, 0:PO], func=AF.Sin),
                waits=[("d_sync", jw), ("pool", p.c("pool"))], inc="act")
            mj = p.op("tensor", lambda e: e.matmul(
                ps_j[:], ccol[:, 4:5], bcols[:, PO:2 * PO], start=True, stop=True),
                waits=[("act", aj)], inc="pe")
            dj = p.op("vector", lambda e: e.reduce_sum(
                out=ccol[0:1, 6:7], in_=ps_j[:], axis=AX.X), waits=[("pe", mj)], inc="dve")
            dj = p.op("vector", lambda e: e.tensor_scalar(
                out=ccol[0:1, 6:7], in0=ccol[0:1, 6:7], scalar1=0.1 / 1024.0, scalar2=-8.0,
                op0=ALU.mult, op1=ALU.add), waits=[("dve", dj)], inc="dve")
            mj = p.op("tensor", lambda e: e.matmul(
                ps_j2[:], ones_row[:], ccol[0:1, 6:7], start=True, stop=True),
                waits=[("dve", dj)], inc="pe")
            p.op("vector", (lambda e: e.tensor_copy(out=ccol[:, 6:7], in_=ps_j2[:])),
                 waits=[("pe", mj)], inc="dve")
            p.emit("ph5" in PHASES_L1)

            _s5 = contextlib.ExitStack()
            with _s5:
                brow = _s5.enter_context(nc.sbuf_tensor("brow", [1, 4 * 256], F32))
                bcomb = _s5.enter_context(nc.sbuf_tensor("bcomb", [1, D], F32))
                p = Prog(nc, sems, cnt)
                for (pi, bsrc) in enumerate((qb_r, kb_r, vb_r, ob_r)):
                    ld = p.dma("sync", brow[:], bsrc, waits=[("d_scalar", p.c("d_scalar"))])
                    p.nop("vector", waits=[("d_sync", ld)])
                    for comp in range(4):
                        for ci in range(4):
                            w, sg = QUAT_TAB[(ci, comp)]
                            if ci == 0:
                                p.op("vector", (lambda e, comp=comp, w=w, sg=sg: e.tensor_scalar(
                                    out=bcomb[0:1, comp * 256:(comp + 1) * 256],
                                    in0=brow[0:1, w * 256:(w + 1) * 256],
                                    scalar1=sg, scalar2=None, op0=ALU.mult)))
                            elif sg > 0:
                                p.op("vector", (lambda e, comp=comp, w=w: e.tensor_tensor(
                                    out=bcomb[0:1, comp * 256:(comp + 1) * 256],
                                    in0=bcomb[0:1, comp * 256:(comp + 1) * 256],
                                    in1=brow[0:1, w * 256:(w + 1) * 256], op=ALU.add)))
                            else:
                                p.op("vector", (lambda e, comp=comp, w=w: e.tensor_tensor(
                                    out=bcomb[0:1, comp * 256:(comp + 1) * 256],
                                    in0=bcomb[0:1, comp * 256:(comp + 1) * 256],
                                    in1=brow[0:1, w * 256:(w + 1) * 256], op=ALU.subtract)))
                    dv = p.mark("vector", "dve")
                    p.dma("scalar", bias_sc[pi].rearrange("po q -> (po q)").unsqueeze(0), bcomb[:],
                          waits=[("dve", dv)])
                sw = p.c("d_scalar")
                ld = p.dma("sync", bcols[:],
                           bias_sc.rearrange("w po q -> q (w po)"),
                           waits=[("d_scalar", sw)], allow_slow_non_contiguous=True)
                p.emit("ph5" in PHASES_L1)

            _s5b = contextlib.ExitStack()
            with _s5b:
                xbuf = _s5b.enter_context(nc.sbuf_tensor("xbuf5", [P, PO * TS5], F32))
                xsq = _s5b.enter_context(nc.sbuf_tensor("xsq5", [P, TS5], F32))
                B1 = _s5b.enter_context(nc.sbuf_tensor("B15", [P, TS5], F32))
                B2 = _s5b.enter_context(nc.sbuf_tensor("B25", [P, TS5], F32))
                t1f = _s5b.enter_context(nc.sbuf_tensor("t1f5", [P, TS5], F32))
                xn2t = _s5b.enter_context(nc.sbuf_tensor("xn2t", [P, PO * TS5], BF16))
                rwsb = _s5b.enter_context(nc.sbuf_tensor("rwsb", [P, PO * 16], BF16))
                logit = _s5b.enter_context(nc.sbuf_tensor("logit", [P, 16], F32))
                pwork = _s5b.enter_context(nc.sbuf_tensor("pwork", [P, 16], F32))
                macc = _s5b.enter_context(nc.sbuf_tensor("macc", [P, 16], F32))
                eqt = _s5b.enter_context(nc.sbuf_tensor("eqt", [P, 16], F32))
                eqt2 = _s5b.enter_context(nc.sbuf_tensor("eqt2", [P, 16], F32))
                ones16 = _s5b.enter_context(nc.sbuf_tensor("ones16", [P, 16], F32))
                col1 = _s5b.enter_context(nc.sbuf_tensor("col1", [P, 1], F32))
                col2 = _s5b.enter_context(nc.sbuf_tensor("col2", [P, 1], F32))
                rowt = _s5b.enter_context(nc.sbuf_tensor("rowt5", [1, TS5], F32))
                rowt2 = _s5b.enter_context(nc.sbuf_tensor("rowt5b", [1, TS5], F32))
                ps_a = _s5b.enter_context(nc.psum_tensor("ps_a5", [P, TS5], F32))
                ps_s1 = _s5b.enter_context(nc.psum_tensor("ps_s15", [1, TS5], F32))
                ps_s2 = _s5b.enter_context(nc.psum_tensor("ps_s25", [1, TS5], F32))
                ps_r = _s5b.enter_context(nc.psum_tensor("ps_r5", [P, 16], F32))

                p = Prog(nc, sems, cnt)
                p.dma("gpsimd", rwsb.ap().rearrange("p (po e) -> p po e", po=PO),
                       rw.rearrange("(po p) e -> p po e", p=P))
                p.op("gpsimd", lambda g: g.memset(ones16[:], 1.0), inc="pool")
                wld = p.c("d_gpsimd")
                for t in range(NT5):
                    t0 = t * TS5
                    guard = [("pe", p.c("pe")), ("act", p.c("act")), ("dve", p.c("dve"))]
                    xw = p.dma("sync", po_view(xbuf, TS5), po_view(x1_fm, T)[:, :, t0:t0 + TS5],
                               waits=guard)
                    for po in range(PO):
                        p.op("tensor", (lambda e, po=po: e.matmul(
                            ps_s1[:], ONEC, xbuf[:, po * TS5:(po + 1) * TS5],
                            start=(po == 0), stop=(po == PO - 1))),
                            waits=([("d_sync", xw)] if po == 0 else ()),
                            inc=("pe" if po == PO - 1 else None))
                    for po in range(PO):
                        aw = p.op("scalar", (lambda e, po=po: e.activation(
                            out=xsq[:], in_=xbuf[:, po * TS5:(po + 1) * TS5], func=AF.Square)),
                            waits=([("d_sync", xw), ("pe", guard[0][1])] if po == 0
                                   else [("pe", p.c("pe"))]),
                            inc="act")
                        p.op("tensor", (lambda e, po=po: e.matmul(
                            ps_s2[:], ONEC, xsq[:],
                            start=(po == 0), stop=(po == PO - 1))),
                            waits=[("act", aw)], inc="pe")
                    sq_done = p.c("pe")
                    p.op("vector", (lambda e, t0=t0: e.tensor_scalar(
                        out=murow2[0:1, t0:t0 + TS5], in0=ps_s1[:], scalar1=1.0 / 1024.0,
                        scalar2=None, op0=ALU.mult)), waits=[("pe", sq_done)])
                    p.op("vector", lambda e: e.tensor_scalar(
                        out=rowt[:], in0=ps_s2[:], scalar1=1.0 / 1024.0, scalar2=None, op0=ALU.mult))
                    p.op("vector", (lambda e, t0=t0: e.tensor_tensor(
                        out=rowt2[:], in0=murow2[0:1, t0:t0 + TS5], in1=murow2[0:1, t0:t0 + TS5],
                        op=ALU.mult)))
                    dv = p.op("vector", lambda e: e.tensor_tensor(
                        out=rowt[:], in0=rowt[:], in1=rowt2[:], op=ALU.subtract), inc="dve")
                    av = p.op("scalar", lambda e: e.activation(
                        out=rowt2[:], in_=rowt[:], func=AF.Sqrt, bias=EPSL[0:1, :], scale=1.0),
                        waits=[("dve", dv)], inc="act")
                    rcp = p.op("vector", (lambda e, t0=t0: e.reciprocal(
                        out=rstdrow2[0:1, t0:t0 + TS5], in_=rowt2[:])),
                        waits=[("act", av)], inc="dve")
                    dv = p.op("vector", (lambda e, t0=t0: e.tensor_tensor(
                        out=rowt2[:], in0=murow2[0:1, t0:t0 + TS5],
                        in1=rstdrow2[0:1, t0:t0 + TS5], op=ALU.mult)),
                        waits=[("dve", rcp)], inc="dve")
                    bm = p.op("tensor", (lambda e, t0=t0: e.matmul(
                        ps_a[:], ones_row[:], rstdrow2[0:1, t0:t0 + TS5], start=True, stop=True)),
                        waits=[("dve", dv)], inc="pe")
                    ev = p.op("vector", lambda e: e.tensor_copy(out=B1[:], in_=ps_a[:]),
                              waits=[("pe", bm)], inc="dve")
                    bm = p.op("tensor", lambda e: e.matmul(
                        ps_a[:], ones_row[:], rowt2[:], start=True, stop=True),
                        waits=[("dve", ev)], inc="pe")
                    ev = p.op("vector", lambda e: e.tensor_copy(out=B2[:], in_=ps_a[:]),
                              waits=[("pe", bm)], inc="dve")
                    for po in range(PO):
                        p.op("vector", (lambda e, po=po: e.tensor_tensor(
                            out=t1f[:], in0=xbuf[:, po * TS5:(po + 1) * TS5], in1=B1[:],
                            op=ALU.mult)))
                        p.op("vector", (lambda e, po=po: e.tensor_tensor(
                            out=xn2t[:, po * TS5:(po + 1) * TS5], in0=t1f[:], in1=B2[:],
                            op=ALU.subtract)))
                    xn_done = p.mark("vector", "dve")
                    sw = p.dma("scalar", po_view(sp_xn2.ap(), T)[:, :, t0:t0 + TS5],
                               po_view(xn2t, TS5), waits=[("dve", xn_done)])
                    for tc in range(TS5 // P):
                        tg = (t0 + tc * P) // P
                        for kc in range(PO):
                            p.op("tensor", (lambda e, kc=kc, tc=tc: e.matmul(
                                ps_r[:], xn2t[:, kc * TS5 + tc * P: kc * TS5 + (tc + 1) * P],
                                rwsb[:, kc * 16:(kc + 1) * 16],
                                start=(kc == 0), stop=(kc == PO - 1))),
                                waits=([("dve", xn_done), ("d_gpsimd", wld),
                                        ("dve", p.c("dve"))] if kc == 0 else ()))
                        rm = p.mark("tensor", "pe")
                        dv0 = p.op("vector", lambda e: e.tensor_tensor(
                            out=logit[:], in0=ps_r[:], in1=rbs[:], op=ALU.add),
                            waits=[("pe", rm)], inc="dve")
                        rmx = p.op("vector", lambda e: e.reduce_max(col1[:], logit[:], axis=AX.X),
                                   inc="dve")
                        p.op("vector", lambda e: e.tensor_scalar(
                            out=col1[:], in0=col1[:], scalar1=-1.0, scalar2=None, op0=ALU.mult),
                            waits=[("dve", rmx)])
                        dv1 = p.mark("vector", "dve")
                        a1 = p.op("scalar", lambda e: e.activation(
                            out=pwork[:], in_=logit[:], func=AF.Exp, bias=col1[:, 0:1], scale=1.0),
                            waits=[("dve", dv1)], inc="act")
                        p.op("vector", lambda e: e.tensor_copy(out=logit[:], in_=pwork[:]),
                             waits=[("act", a1)])
                        p.op("gpsimd", lambda e: e.memset(macc[:], 0.0),
                             waits=[("dve", p.c("dve"))], inc="pool")
                        p.op("gpsimd", lambda e: e.memset(col2[:], 0.0), inc="pool")
                        p.nop("vector", waits=[("pool", p.c("pool"))])
                        for it in range(4):
                            rmx = p.op("vector", lambda e: e.reduce_max(col1[:], pwork[:], axis=AX.X),
                                       inc="dve")
                            p.op("vector", lambda e: e.tensor_tensor(
                                out=col2[:], in0=col2[:], in1=col1[:], op=ALU.add),
                                waits=[("dve", rmx)])
                            p.op("vector", lambda e: e.tensor_scalar(
                                out=eqt[:], in0=ones16[:], scalar1=col1[:, 0:1],
                                scalar2=None, op0=ALU.mult))
                            p.op("vector", lambda e: e.tensor_tensor(
                                out=eqt[:], in0=pwork[:], in1=eqt[:], op=ALU.subtract))
                            dsg = p.mark("vector", "dve")
                            asg = p.op("scalar", lambda e: e.activation(
                                out=eqt2[:], in_=eqt[:], func=AF.Sign),
                                waits=[("dve", dsg)], inc="act")
                            p.op("vector", lambda e: e.tensor_scalar(
                                out=eqt2[:], in0=eqt2[:], scalar1=1.0, scalar2=None, op0=ALU.add),
                                waits=[("act", asg)])
                            p.op("vector", lambda e: e.tensor_tensor(
                                out=macc[:], in0=macc[:], in1=eqt2[:], op=ALU.max))
                            p.op("vector", lambda e: e.scalar_tensor_tensor(
                                out=pwork[:], in0=eqt2[:], scalar=-1e30, in1=pwork[:],
                                op0=ALU.mult, op1=ALU.add))
                        rcp = p.op("vector", lambda e: e.reciprocal(col1[:], col2[:]), inc="dve")
                        p.op("vector", lambda e: e.tensor_tensor(
                            out=logit[:], in0=logit[:], in1=macc[:], op=ALU.mult),
                            waits=[("dve", rcp)])
                        p.op("vector", (lambda e, tg=tg: e.tensor_scalar(
                            out=c_tm[:, tg * NEXP:(tg + 1) * NEXP], in0=logit[:],
                            scalar1=col1[:, 0:1], scalar2=None, op0=ALU.mult)), inc="dve")
                    p.nop("scalar", waits=[("d_scalar", p.c("d_scalar"))])
                p.emit("ph5" in PHASES_L1)

            _s6 = contextlib.ExitStack()
            with _s6:
                q_fm = _s6.enter_context(nc.sbuf_tensor("q_fm", [P, PO * T], BF16))
                _s6i = contextlib.ExitStack()
                wqb = _s6i.enter_context(nc.sbuf_tensor("wqb", [P, PO * D], BF16))
                wkb = _s6i.enter_context(nc.sbuf_tensor("wkb", [P, PO * D], BF16))
                wvb = _s6i.enter_context(nc.sbuf_tensor("wvb", [P, PO * D], BF16))
                qraw = _s6i.enter_context(nc.sbuf_tensor("qraw", [P, 8 * 256], BF16))
                k_fm = _s6i.enter_context(nc.sbuf_tensor("k_fm", [P, PO * T], BF16))
                vtc = _s6i.enter_context(nc.sbuf_tensor("vtc", [P, NTC * NH * 65], BF16))
                xn2t6 = _s6i.enter_context(nc.sbuf_tensor("xn2t6", [P, PO * TS5], BF16))
                pre = _s6i.enter_context(nc.sbuf_tensor("pre", [P, PO * TS5], F32))
                sqt = _s6i.enter_context(nc.sbuf_tensor("sqt", [P, TS5], F32))
                nsq = _s6i.enter_context(nc.sbuf_tensor("nsq", [P, 2 * TS5], F32))
                nsq2 = _s6i.enter_context(nc.sbuf_tensor("nsq2", [P, 2 * TS5], F32))
                ps_a = _s6i.enter_context(nc.psum_tensor("ps_a6", [P, TS5], F32))
                ps_t = _s6i.enter_context(nc.psum_tensor("ps_t6", [P, P], BF16))
                ps_t2 = _s6i.enter_context(nc.psum_tensor("ps_t6b", [P, P], BF16))

                p = Prog(nc, sems, cnt)
                for (wsrc, wdst) in ((qw_r, wqb), (kw_r, wkb), (vw_r, wvb)):
                    lw = p.dma("gpsimd", qraw[:], wsrc, waits=[("dve", p.c("dve"))])
                    p.nop("vector", waits=[("d_gpsimd", lw)])
                    for kc in range(PO):
                        ci, hc = kc // 2, kc % 2
                        for comp in range(4):
                            w, sg = QUAT_TAB[(ci, comp)]
                            p.op("vector", (lambda e, kc=kc, comp=comp, w=w, sg=sg, hc=hc,
                                            wdst=wdst: e.tensor_scalar(
                                out=wdst[:, kc * D + comp * 256: kc * D + (comp + 1) * 256],
                                in0=qraw[:, (w * 2 + hc) * 256:(w * 2 + hc + 1) * 256],
                                scalar1=sg, scalar2=None, op0=ALU.mult)))
                    p.mark("vector", "dve")
                p.op("gpsimd", lambda e: e.memset(
                    vtc.ap().rearrange("p (tc h o) -> p tc h o", tc=NTC, h=NH)[:, :, :, 64:65], 1.0),
                    inc="pool")
                wasm = p.c("dve")
                for t in range(NT5):
                    t0 = t * TS5
                    guard = [("pe", p.c("pe")), ("act", p.c("act")), ("dve", max(p.c("dve"), wasm))]
                    lx = p.dma("sync", po_view(xn2t6, TS5), po_view(sp_xn2.ap(), T)[:, :, t0:t0 + TS5],
                               waits=guard)
                    for (pi, wdst, dst) in ((0, wqb, q_fm), (1, wkb, k_fm), (2, wvb, None)):
                        for po in range(PO):
                            for kc in range(PO):
                                p.op("tensor", (lambda e, wdst=wdst, po=po, kc=kc: e.matmul(
                                    ps_a[:], wdst[:, kc * D + po * P: kc * D + (po + 1) * P],
                                    xn2t6[:, kc * TS5:(kc + 1) * TS5],
                                    start=(kc == 0), stop=(kc == PO - 1))),
                                    waits=([("d_sync", lx), ("dve", max(wasm, p.c("dve"))),
                                            ("act", p.c("act"))]
                                           if po == 0 and kc == 0 else ()))
                            mmv = p.mark("tensor", "pe")
                            aw = p.op("scalar", (lambda e, pi=pi, po=po: e.activation(
                                out=pre[:, po * TS5:(po + 1) * TS5], in_=ps_a[:], func=AF.Identity,
                                bias=bcols[:, pi * PO + po: pi * PO + po + 1], scale=1.0)),
                                waits=[("pe", mmv)], inc="act")
                            p.nop("tensor", waits=[("act", aw)])
                        pre_done = p.c("act")
                        if pi < 2:
                            for half in range(2):
                                for ci2 in range(4):
                                    po = 2 * ci2 + half
                                    aw = p.op("scalar", (lambda e, po=po: e.activation(
                                        out=sqt[:], in_=pre[:, po * TS5:(po + 1) * TS5],
                                        func=AF.Square)),
                                        waits=([("act", pre_done), ("dve", p.c("dve"))]
                                               if half == 0 and ci2 == 0
                                               else [("dve", p.c("dve"))]), inc="act")
                                    if ci2 == 0:
                                        p.op("vector", (lambda e, half=half: e.tensor_copy(
                                            out=nsq[:, half * TS5:(half + 1) * TS5], in_=sqt[:])),
                                            waits=[("act", aw)], inc="dve")
                                    else:
                                        p.op("vector", (lambda e, half=half: e.tensor_tensor(
                                            out=nsq[:, half * TS5:(half + 1) * TS5],
                                            in0=nsq[:, half * TS5:(half + 1) * TS5],
                                            in1=sqt[:], op=ALU.add)),
                                            waits=[("act", aw)], inc="dve")
                            dvn = p.c("dve")
                            a2 = p.op("scalar", lambda e: e.activation(
                                out=nsq2[:], in_=nsq[:], func=AF.Sqrt, bias=EPSQ, scale=1.0),
                                waits=[("dve", dvn)], inc="act")
                            dv2 = p.op("vector", lambda e: e.reciprocal(out=nsq[:], in_=nsq2[:]),
                                       waits=[("act", a2)], inc="dve")
                            if pi == 0:
                                dv2 = p.op("vector", lambda e: e.tensor_scalar(
                                    out=nsq[:], in0=nsq[:], scalar1=0.125, scalar2=None,
                                    op0=ALU.mult), waits=[("dve", dv2)], inc="dve")
                            for po in range(PO):
                                p.op("vector", (lambda e, po=po, t0=t0, dst=dst: e.tensor_tensor(
                                    out=dst[:, po * T + t0: po * T + t0 + TS5],
                                    in0=pre[:, po * TS5:(po + 1) * TS5],
                                    in1=nsq[:, (po % 2) * TS5:((po % 2) + 1) * TS5],
                                    op=ALU.mult)),
                                    waits=([("dve", dv2)] if po == 0 else ()))
                            p.mark("vector", "dve")
                        else:
                            for po in range(PO):
                                for tc2 in range(TS5 // P):
                                    pst = ps_t if (po * 4 + tc2) % 2 == 0 else ps_t2
                                    aw = p.op("scalar", (lambda e, po=po, tc2=tc2: e.activation(
                                        out=sqt.ap().bitcast(BF16)[:, 0:P],
                                        in_=pre[:, po * TS5 + tc2 * P: po * TS5 + (tc2 + 1) * P],
                                        func=AF.Copy)),
                                        waits=[("act", pre_done), ("pe", p.c("pe"))], inc="act")
                                    mm = p.op("tensor", (lambda e, pst=pst: e.transpose(
                                        pst[:], sqt.ap().bitcast(BF16)[:, 0:P], ident_bf[:])),
                                        waits=[("act", aw)], inc="pe")
                                    tcg = t * (TS5 // P) + tc2
                                    p.op("vector", (lambda e, po=po, tcg=tcg, pst=pst: e.tensor_copy(
                                        out=vtc.ap().rearrange(
                                            "p (tc h o) -> p tc h o", tc=NTC, h=NH)[
                                            :, tcg, 2 * po: 2 * po + 2, 0:64],
                                        in_=pst.ap().rearrange("p (a b) -> p a b", a=2))),
                                        waits=[("pe", mm)], inc="dve")
                p.emit("ph6" in PHASES_L1)

                p = Prog(nc, sems, cnt)
                dv = p.c("dve")
                p.dma("gpsimd", ck_in.ap(), k_fm[:], waits=[("dve", dv), ("pool", p.c("pool"))])
                p.dma("gpsimd", cv_in.ap(), vtc[:])
                ccw = p.c("d_gpsimd")
                p.op("gpsimd", lambda e: e.collective_compute(
                    "AllGather", ALU.bypass,
                    replica_groups=[[0, 1, 2, 3], [4, 5, 6, 7]],
                    ins=[ck_in.ap().opt()], outs=[ck_out.ap().opt()]),
                    waits=[("d_gpsimd", ccw)], inc="cc")
                p.op("gpsimd", lambda e: e.collective_compute(
                    "AllGather", ALU.bypass,
                    replica_groups=[[0, 1, 2, 3], [4, 5, 6, 7]],
                    ins=[cv_in.ap().opt()], outs=[cv_out.ap().opt()]),
                    inc="cc")
                p.emit("ph6" in PHASES_L1)
                _s6i.close()

                _s7 = contextlib.ExitStack()
                with _s7:
                    k_sb = _s7.enter_context(nc.sbuf_tensor("k_sb", [P, PO * T], BF16))
                    v_sb = _s7.enter_context(nc.sbuf_tensor("v_sb", [P, NTC * NH * 65], BF16))
                    oacc = _s7.enter_context(nc.sbuf_tensor("oacc", [P, NH * 4 * 4 * 65], F32))
                    probs = _s7.enter_context(nc.sbuf_tensor("probs", [P, 2 * 512], BF16))
                    lcol = _s7.enter_context(nc.sbuf_tensor("lcol", [P, 1], F32))
                    ps_sc1 = _s7.enter_context(nc.psum_tensor("ps_sc1", [P, 512], F32))
                    ps_sc2 = _s7.enter_context(nc.psum_tensor("ps_sc2", [P, 512], F32))
                    ps_o1 = _s7.enter_context(nc.psum_tensor("ps_o1", [P, 4 * 65], F32))
                    ps_o2 = _s7.enter_context(nc.psum_tensor("ps_o2", [P, 4 * 65], F32))
                    QB = 512

                    p = Prog(nc, sems, cnt)
                    p.nop("sync", waits=[("cc", p.c("cc"))])
                    exp_log = []
                    for r in range(4):
                        lk = p.dma("sync", k_sb[:], ck_out[r * P:(r + 1) * P, :].opt(),
                                   waits=([("cc", p.c("cc"))] if r == 0 else
                                          [("pe", p.c("pe")), ("act", p.c("act"))]))
                        lv = p.dma("sync", v_sb[:], cv_out[r * P:(r + 1) * P, :].opt())
                        lkv = p.c("d_sync")
                        for h in range(NH):
                            hb = (h % 2) * HD
                            hp = h // 2
                            for qb in range(4):
                                exps = []
                                for kt in range(NTC):
                                    pss = ps_sc1 if kt % 2 == 0 else ps_sc2
                                    wts = [("act", p.c("act"))] if len(exp_log) >= 2 else []
                                    if kt == 0 and h == 0 and qb == 0:
                                        wts.append(("d_sync", lkv))
                                    if kt == 0:
                                        wts.append(("dve", p.c("dve")))
                                    mm = p.op("tensor", (lambda e, hb=hb, hp=hp, kt=kt, qb=qb,
                                                         pss=pss: e.matmul(
                                        pss[:], k_sb[hb:hb + HD, hp * T + kt * P: hp * T + (kt + 1) * P],
                                        q_fm[hb:hb + HD, hp * T + qb * QB: hp * T + (qb + 1) * QB],
                                        start=True, stop=True)),
                                        waits=([("act", exp_log[-2])] if len(exp_log) >= 2 else
                                               wts),
                                        inc="pe")
                                    pb = probs[:, (kt % 2) * 512:(kt % 2 + 1) * 512]
                                    a1 = p.op("scalar", (lambda e, pss=pss, pb=pb: e.activation(
                                        out=pb, in_=pss[:], func=AF.Exp, bias=JBIAS, scale=1.0)),
                                        waits=[("pe", mm)], inc="act")
                                    exp_log.append(a1)
                                    pso = ps_o1 if (h * 4 + qb) % 2 == 0 else ps_o2
                                    for qt in range(4):
                                        p.op("tensor", (lambda e, pb=pb, qt=qt, kt=kt, h=h,
                                                        pso=pso: e.matmul(
                                            pso[:, qt * 65:(qt + 1) * 65],
                                            pb[:, qt * P:(qt + 1) * P],
                                            v_sb[:, (kt * NH + h) * 65:(kt * NH + h + 1) * 65],
                                            start=(kt == 0 and qt == 0),
                                            stop=(kt == NTC - 1 and qt == 3))),
                                            waits=([("act", a1)] if qt == 0 else ()),
                                            inc=("pe" if kt == NTC - 1 and qt == 3 else None))
                                pvd = p.c("pe")
                                if r == 0:
                                    p.op("vector", (lambda e, h=h, qb=qb, pso=pso: e.tensor_copy(
                                        out=oacc[:, (h * 4 + qb) * 260:(h * 4 + qb + 1) * 260],
                                        in_=pso[:])), waits=[("pe", pvd)], inc="dve")
                                else:
                                    p.op("vector", (lambda e, h=h, qb=qb, pso=pso: e.tensor_tensor(
                                        out=oacc[:, (h * 4 + qb) * 260:(h * 4 + qb + 1) * 260],
                                        in0=oacc[:, (h * 4 + qb) * 260:(h * 4 + qb + 1) * 260],
                                        in1=pso[:], op=ALU.add)), waits=[("pe", pvd)], inc="dve")
                    p.emit("ph7" in PHASES_L1)

                    p = Prog(nc, sems, cnt)
                    if DEBUG_L1:
                        p.dma("scalar", dbg_oacc, oacc[:], waits=[("dve", p.c("dve"))])
                        p.dma("scalar", dbg_xn2, sp_xn2.ap())
                        p.dma("scalar", dbg_ctm, c_tm[:])
                        p.dma("scalar", dbg_k, ck_in.ap())
                        p.dma("scalar", dbg_v, cv_in.ap())
                        p.nop("sync", waits=[("d_scalar", p.c("d_scalar"))])
                    for h in range(NH):
                        for tcg in range(NTC):
                            qb, qt = tcg // 4, tcg % 4
                            base = (h * 4 + qb) * 260 + qt * 65
                            rcp = p.op("vector", (lambda e, base=base: e.reciprocal(
                                out=lcol[:], in_=oacc[:, base + 64: base + 65])), inc="dve")
                            p.op("vector", (lambda e, base=base, tcg=tcg, h=h: e.tensor_scalar(
                                out=q_fm.ap().bitcast(BF16)[:, tcg * D + h * HD: tcg * D + (h + 1) * HD]
                                if False else
                                q_fm[:, tcg * D + h * HD: tcg * D + (h + 1) * HD],
                                in0=oacc[:, base: base + 64],
                                scalar1=lcol[:, 0:1], scalar2=None, op0=ALU.mult)),
                                waits=[("dve", rcp)])
                    p.mark("vector", "dve")
                    p.emit("ph7" in PHASES_L1)
                p.emit if False else None

                _s7b = contextlib.ExitStack()
                with _s7b:
                    afm = _s7b.enter_context(nc.sbuf_tensor("afm", [P, PO * T], BF16))
                    wob = _s7b.enter_context(nc.sbuf_tensor("wob", [P, PO * D], BF16))
                    qraw2 = _s7b.enter_context(nc.sbuf_tensor("qraw2", [P, 8 * 256], BF16))
                    sfm = _s7b.enter_context(nc.sbuf_tensor("sfm", [P, PO * TS5], BF16))
                    ps_t7 = _s7b.enter_context(nc.psum_tensor("ps_t7", [P, P], BF16))
                    ps_t7b = _s7b.enter_context(nc.psum_tensor("ps_t7b", [P, P], BF16))
                    ps_a7 = _s7b.enter_context(nc.psum_tensor("ps_a7", [P, TS5], F32))

                    p = Prog(nc, sems, cnt)
                    if DEBUG_L1:
                        p.dma("scalar", dbg_q, q_fm[:], waits=[("dve", p.c("dve"))])
                    lw = p.dma("gpsimd", qraw2[:], ow_r)
                    p.nop("vector", waits=[("d_gpsimd", p.c("d_gpsimd"))])
                    for kc in range(PO):
                        ci, hc = kc // 2, kc % 2
                        for comp in range(4):
                            w, sg = QUAT_TAB[(ci, comp)]
                            p.op("vector", (lambda e, kc=kc, comp=comp, w=w, sg=sg, hc=hc:
                                            e.tensor_scalar(
                                out=wob[:, kc * D + comp * 256: kc * D + (comp + 1) * 256],
                                in0=qraw2[:, (w * 2 + hc) * 256:(w * 2 + hc + 1) * 256],
                                scalar1=sg, scalar2=None, op0=ALU.mult)))
                    wdone = p.mark("vector", "dve")
                    for tcg in range(NTC):
                        for j in range(PO):
                            pst = ps_t7 if (tcg * PO + j) % 2 == 0 else ps_t7b
                            mm = p.op("tensor", (lambda e, tcg=tcg, j=j, pst=pst: e.transpose(
                                pst[:], q_fm[:, tcg * D + j * P: tcg * D + (j + 1) * P],
                                ident_bf[:])),
                                waits=[("dve", p.c("dve"))], inc="pe")
                            p.op("vector", (lambda e, tcg=tcg, j=j, pst=pst: e.tensor_copy(
                                out=afm[:, j * T + tcg * P: j * T + (tcg + 1) * P], in_=pst[:])),
                                waits=[("pe", mm)], inc="dve")
                    tr_done = p.c("dve")
                    for t in range(NT5):
                        t0 = t * TS5
                        for po in range(PO):
                            for kc in range(PO):
                                p.op("tensor", (lambda e, po=po, kc=kc, t0=t0: e.matmul(
                                    ps_a7[:], wob[:, kc * D + po * P: kc * D + (po + 1) * P],
                                    afm[:, kc * T + t0: kc * T + t0 + TS5],
                                    start=(kc == 0), stop=(kc == PO - 1))),
                                    waits=([("dve", tr_done)] if t == 0 and po == 0 and kc == 0
                                           else ()))
                            mmv = p.mark("tensor", "pe")
                            aw = p.op("scalar", (lambda e, po=po: e.activation(
                                out=sfm[:, po * TS5:(po + 1) * TS5], in_=ps_a7[:],
                                func=AF.Identity, bias=bcols[:, 3 * PO + po: 3 * PO + po + 1],
                                scale=1.0)),
                                waits=[("pe", mmv), ("d_scalar", p.c("d_scalar"))], inc="act")
                            p.nop("tensor", waits=[("act", aw)])
                        p.dma("scalar", po_view(sp_s.ap(), T)[:, :, t0:t0 + TS5],
                              po_view(sfm, TS5), waits=[("act", p.c("act"))])
                    p.nop("sync", waits=[("d_scalar", p.c("d_scalar"))])
                    p.emit("ph7" in PHASES_L1)

            _s8 = contextlib.ExitStack()
            with _s8:
                zacc = _s8.enter_context(nc.sbuf_tensor("zacc8", [P, NTC * D], F32))
                _s8i = contextlib.ExitStack()
                w1e = _s8i.enter_context(nc.sbuf_tensor("w1e", [P, PO * MI], BF16))
                w2e = _s8i.enter_context(nc.sbuf_tensor("w2e", [P, (MI // P) * D], BF16))
                midb = _s8i.enter_context(nc.sbuf_tensor("midb", [P, (MI // P) * 512], BF16))
                xn2t8 = _s8i.enter_context(nc.sbuf_tensor("xn2t8", [P, PO * 512], BF16))
                b1c = _s8i.enter_context(nc.sbuf_tensor("b1c", [P, MI // P], F32))
                b2row = _s8i.enter_context(nc.sbuf_tensor("b2row", [1, D], F32))
                ps_f1 = _s8i.enter_context(nc.psum_tensor("ps_f18", [P, 512], F32))
                ps_z = [_s8i.enter_context(nc.psum_tensor(f"ps_z{i}", [P, 512], F32))
                        for i in range(4)]

                p = Prog(nc, sems, cnt)
                if DEBUG_L1:
                    p.dma("scalar", dbg_s, sp_s.ap())
                    p.nop("sync", waits=[("d_scalar", p.c("d_scalar"))])
                for e_i in range(NEXP):
                    lw1 = p.dma("gpsimd", w1e.ap().rearrange("p (k n) -> p k n", k=PO),
                                w1m[e_i].rearrange("(k p) n -> p k n", p=P),
                                waits=[("pe", p.c("pe"))])
                    lw2 = p.dma("gpsimd", w2e.ap().rearrange("p (k m) -> p k m", k=MI // P),
                                w2m[e_i].rearrange("(k p) m -> p k m", p=P))
                    lb1 = p.dma("sync", b1c[:], b1m[e_i], waits=[("act", p.c("act"))])
                    lb2 = p.dma("sync", b2row[:], b2m[0:1, e_i * D:(e_i + 1) * D],
                                waits=[("pe", p.c("pe"))])
                    wld8 = p.c("d_gpsimd")
                    for ts in range(4):
                        t0 = ts * 512
                        lx = p.dma("sync", po_view(xn2t8, 512),
                                   po_view(sp_xn2.ap(), T)[:, :, t0:t0 + 512],
                                   waits=[("pe", p.c("pe"))])
                        for nch in range(MI // P):
                            for kc in range(PO):
                                p.op("tensor", (lambda e, nch=nch, kc=kc: e.matmul(
                                    ps_f1[:], w1e[:, kc * MI + nch * P: kc * MI + (nch + 1) * P],
                                    xn2t8[:, kc * 512:(kc + 1) * 512],
                                    start=(kc == 0), stop=(kc == PO - 1))),
                                    waits=([("d_gpsimd", wld8), ("d_sync", p.c("d_sync")),
                                            ("act", p.c("act"))]
                                           if nch == 0 and kc == 0 else ()))
                            mmv = p.mark("tensor", "pe")
                            aw = p.op("scalar", (lambda e, nch=nch: e.activation(
                                out=midb[:, nch * 512:(nch + 1) * 512], in_=ps_f1[:], func=AF.Gelu,
                                bias=b1c[:, nch:nch + 1], scale=1.0)),
                                waits=[("pe", mmv), ("d_sync", p.c("d_sync"))], inc="act")
                            p.nop("tensor", waits=[("act", aw)])
                        mid_done = p.c("act")
                        for mh in range(2):
                            for t128 in range(4):
                                tg = (t0 + t128 * P) // P
                                psz = ps_z[t128]
                                p.op("tensor", (lambda e, psz=psz, mh=mh: e.matmul(
                                    psz[:], ones_row[:],
                                    b2row[0:1, mh * 512: mh * 512 + 512],
                                    start=True, stop=False)),
                                    waits=([("act", mid_done), ("dve", p.c("dve")),
                                            ("d_sync", p.c("d_sync"))]
                                           if mh == 0 and t128 == 0 else
                                           [("dve", p.c("dve"))]))
                                for nch in range(MI // P):
                                    p.op("tensor", (lambda e, nch=nch, t128=t128, mh=mh,
                                                    psz=psz: e.matmul(
                                        psz[:],
                                        midb[:, nch * 512 + t128 * P: nch * 512 + (t128 + 1) * P],
                                        w2e[:, nch * D + mh * 512: nch * D + mh * 512 + 512],
                                        start=False, stop=(nch == MI // P - 1))))
                                mmz = p.mark("tensor", "pe")
                                tg2 = ts * 4 + t128
                                if e_i == 0:
                                    p.op("vector", (lambda e, tg2=tg2, mh=mh, psz=psz, e_i=e_i:
                                                    e.tensor_scalar(
                                        out=zacc[:, tg2 * D + mh * 512: tg2 * D + mh * 512 + 512],
                                        in0=psz[:],
                                        scalar1=c_tm[:, tg2 * NEXP + e_i: tg2 * NEXP + e_i + 1],
                                        scalar2=None, op0=ALU.mult)),
                                        waits=[("pe", mmz)], inc="dve")
                                else:
                                    p.op("vector", (lambda e, tg2=tg2, mh=mh, psz=psz, e_i=e_i:
                                                    e.scalar_tensor_tensor(
                                        out=zacc[:, tg2 * D + mh * 512: tg2 * D + mh * 512 + 512],
                                        in0=psz[:],
                                        scalar=c_tm[:, tg2 * NEXP + e_i: tg2 * NEXP + e_i + 1],
                                        in1=zacc[:, tg2 * D + mh * 512: tg2 * D + mh * 512 + 512],
                                        op0=ALU.mult, op1=ALU.add)),
                                        waits=[("pe", mmz)], inc="dve")
                p.emit("ph8" in PHASES_L1)
                _s8i.close()

                _s9 = contextlib.ExitStack()
                with _s9:
                    gz = _s9.enter_context(nc.sbuf_tensor("gz", [P, NTC * D], BF16))
                    gfm = _s9.enter_context(nc.sbuf_tensor("gfm", [P, PO * T], BF16))
                    sfm9 = _s9.enter_context(nc.sbuf_tensor("sfm9", [P, PO * TS5], BF16))
                    x9 = _s9.enter_context(nc.sbuf_tensor("x9", [P, TS5], F32))
                    o9 = _s9.enter_context(nc.sbuf_tensor("o9", [P, TS5], F32))
                    g9 = _s9.enter_context(nc.sbuf_tensor("g9", [P, TS5], F32))
                    ps_t9 = _s9.enter_context(nc.psum_tensor("ps_t9", [P, P], BF16))
                    ps_t9b = _s9.enter_context(nc.psum_tensor("ps_t9b", [P, P], BF16))

                    p = Prog(nc, sems, cnt)
                    if DEBUG_L1:
                        p.dma("scalar", dbg_z, zacc[:])
                        p.nop("sync", waits=[("d_scalar", p.c("d_scalar"))])
                    for tcg in range(NTC):
                        p.op("scalar", (lambda e, tcg=tcg: e.activation(
                            out=gz[:, tcg * D:(tcg + 1) * D], in_=zacc[:, tcg * D:(tcg + 1) * D],
                            func=AF.Silu)), inc="act")
                    silu_done = p.c("act")
                    for tcg in range(NTC):
                        for j in range(PO):
                            pst = ps_t9 if (tcg * PO + j) % 2 == 0 else ps_t9b
                            mm = p.op("tensor", (lambda e, tcg=tcg, j=j, pst=pst: e.transpose(
                                pst[:], gz[:, tcg * D + j * P: tcg * D + (j + 1) * P],
                                ident_bf[:])),
                                waits=([("act", silu_done)] if tcg == 0 and j == 0 else
                                       [("dve", p.c("dve"))]), inc="pe")
                            p.op("vector", (lambda e, tcg=tcg, j=j, pst=pst: e.tensor_copy(
                                out=gfm[:, j * T + tcg * P: j * T + (tcg + 1) * P], in_=pst[:])),
                                waits=[("pe", mm)], inc="dve")
                    tr9 = p.c("dve")
                    for t in range(NT5):
                        t0 = t * TS5
                        ls = p.dma("sync", po_view(sfm9, TS5),
                                   po_view(sp_s.ap(), T)[:, :, t0:t0 + TS5],
                                   waits=[("dve", p.c("dve"))])
                        for po in range(PO):
                            lx = p.dma("scalar", x9[:], po_view(x1_fm, T)[:, po, t0:t0 + TS5],
                                       waits=[("dve", p.c("dve"))])
                            a1 = p.op("scalar", (lambda e, po=po, t0=t0: e.activation(
                                out=g9[:], in_=gfm[:, po * T + t0: po * T + t0 + TS5],
                                func=AF.Copy)), waits=[("dve", tr9)], inc="act")
                            a2 = p.op("scalar", (lambda e, po=po: e.activation(
                                out=o9[:], in_=sfm9[:, po * TS5:(po + 1) * TS5], func=AF.Copy)),
                                waits=[("d_sync", ls)], inc="act")
                            p.op("vector", lambda e: e.tensor_tensor(
                                out=o9[:], in0=o9[:], in1=g9[:], op=ALU.mult),
                                waits=[("act", a2), ("d_scalar", lx)])
                            dv = p.op("vector", lambda e: e.tensor_tensor(
                                out=o9[:], in0=o9[:], in1=x9[:], op=ALU.add), inc="dve")
                            sw = p.dma("scalar", po_view(out_fm, T)[:, po, t0:t0 + TS5], o9[:],
                                       waits=[("dve", dv)])
                            p.nop("scalar", waits=[("d_scalar", sw)])
                            p.nop("vector", waits=[("d_scalar", sw)])
                    p.nop("sync", waits=[("d_scalar", p.c("d_scalar"))])
                    p.emit("ph9" in PHASES_L1)

    return nc


PHASES_L1 = {"ph5", "ph6", "ph7", "ph8", "ph9"}
DEBUG_L1 = False


def prep_l1_inputs(inputs, x1_fm_per_core):
    def quat_w_layout(w):
        return np.ascontiguousarray(
            np.asarray(w).reshape(4, 2, P, 256).transpose(2, 0, 1, 3).reshape(P, 2048))

    base = {
        "qw_r": quat_w_layout(inputs["L1_attn_qW"]),
        "kw_r": quat_w_layout(inputs["L1_attn_kW"]),
        "vw_r": quat_w_layout(inputs["L1_attn_vW"]),
        "ow_r": quat_w_layout(inputs["L1_attn_oW"]),
        "qb_r": np.ascontiguousarray(np.asarray(inputs["L1_attn_qb"]).reshape(1, 1024)),
        "kb_r": np.ascontiguousarray(np.asarray(inputs["L1_attn_kb"]).reshape(1, 1024)),
        "vb_r": np.ascontiguousarray(np.asarray(inputs["L1_attn_vb"]).reshape(1, 1024)),
        "ob_r": np.ascontiguousarray(np.asarray(inputs["L1_attn_ob"]).reshape(1, 1024)),
        "rw": np.ascontiguousarray(np.asarray(inputs["L1_ffn_router_w"])),
        "jcol": np.ascontiguousarray(np.asarray(inputs["L1_attn_J"]).reshape(PO, P).T),
        "rb": np.ascontiguousarray(np.asarray(inputs["L1_ffn_router_b"]).reshape(1, 16)),
        "w1m": np.ascontiguousarray(np.asarray(inputs["L1_ffn_w1"])),
        "b1m": np.ascontiguousarray(
            np.asarray(inputs["L1_ffn_b1"]).reshape(NEXP, MI // P, P).transpose(0, 2, 1)),
        "w2m": np.ascontiguousarray(np.asarray(inputs["L1_ffn_w2"])),
        "b2m": np.ascontiguousarray(np.asarray(inputs["L1_ffn_b2"]).reshape(1, NEXP * D)),
    }
    maps = []
    for core in range(8):
        m = dict(base)
        m["x1_fm"] = np.ascontiguousarray(x1_fm_per_core[core])
        maps.append(m)
    return maps


def kernel(**inputs):
    nc0 = build_l0()
    maps0 = prep_l0_inputs(inputs)
    res0 = run_bass_kernel_spmd(nc0, maps0, core_ids=list(range(8)))
    x1s = [res0.results[c]["x1_fm"] for c in range(8)]
    nc1 = build_l1()
    maps1 = prep_l1_inputs(inputs, x1s)
    res1 = run_bass_kernel_spmd(nc1, maps1, core_ids=list(range(8)))
    out = np.zeros((2, 8192, 1024), np.float32)
    for core in range(8):
        b, r = core // 4, core % 4
        out[b, r * T:(r + 1) * T, :] = unfm_layout(res1.results[core]["out_fm"])
    return out


if __name__ == "__main__":
    pass




# version 04
# speedup vs baseline: 1.0018x; 1.0018x over previous
import numpy as np
import contextlib
import concourse.bass as bass
import concourse.mybir as mybir
from concourse.bass_utils import run_bass_kernel_spmd

F32 = mybir.dt.float32
BF16 = mybir.dt.bfloat16
I32 = mybir.dt.int32
AF = mybir.ActivationFunctionType
ALU = mybir.AluOpType
AX = mybir.AxisListType

P = 128
D = 1024
PO = 8
T = 2048
TT = 256
NT = T // TT
CH = 128
CPT = TT // CH
NCH = T // CH
NH = 16
HD = 64
FF = 4096
NFC = FF // P
NEXP = 16
MI = 2048

ENGINES = ("sync", "scalar", "vector", "tensor", "gpsimd")

PH1A_PARTS = {"stats", "xn", "proj", "gk", "chunk", "spill"}


class Prog:

    def __init__(self, nc, sems, cnt):
        self.nc = nc
        self.sems = sems
        self.cnt = cnt
        self.steps = {e: [] for e in ENGINES}

    def c(self, key):
        return self.cnt.get(key, 0)

    def op(self, engine, fn, inc=None, n=1, waits=()):
        st = self.steps[engine]
        for (k, v) in waits:
            if v > 0:
                st.append(("w", k, v))
        if inc is not None:
            self.cnt[inc] = self.cnt.get(inc, 0) + n
            st.append(("o", fn, inc, n))
            return self.cnt[inc]
        st.append(("o", fn, None, 0))
        return None

    def dma(self, q, out, in_, waits=(), **kw):
        key = "d_" + q
        return self.op(
            q,
            (lambda e, o=out, i=in_, kw=dict(kw): e.dma_start(out=o, in_=i, **kw)),
            inc=key, n=16, waits=waits,
        )

    def nop(self, engine, waits=(), inc=None):
        assert inc is None
        for (k, v) in waits:
            if v > 0:
                self.steps[engine].append(("w", k, v))
        return None

    def mark(self, engine, inc, n=1):
        st = self.steps[engine]
        for i in range(len(st) - 1, -1, -1):
            if st[i][0] == "o":
                assert st[i][2] is None, "last op already carries an inc"
                self.cnt[inc] = self.cnt.get(inc, 0) + n
                st[i] = ("o", st[i][1], inc, n)
                return self.cnt[inc]
        raise AssertionError("no op to mark")

    def emit(self, enabled=True):
        if not enabled:
            return
        with self.nc.Block() as block:
            for e in ENGINES:
                steps = self.steps[e]
                if not steps:
                    continue

                def mk(steps=steps):
                    def body(eng):
                        for s in steps:
                            if s[0] == "w":
                                eng.wait_ge(self.sems[s[1]], s[2])
                            else:
                                ins = s[1](eng)
                                if s[2] is not None:
                                    ins.then_inc(self.sems[s[2]], s[3])
                    return body
                getattr(block, e)(mk())


def po_view(ap_or_handle, width):
    a = ap_or_handle if isinstance(ap_or_handle, bass.AP) else ap_or_handle.ap()
    return a.rearrange("p (po t) -> p po t", po=PO)



def _decl_l0(nc, x1_kind="ExternalOutput"):
    x_fm = nc.dram_tensor("x_fm", [P, PO * T], F32, kind="ExternalInput").ap()
    wq = nc.dram_tensor("wq", [P, PO * D], F32, kind="ExternalInput").ap()
    wk = nc.dram_tensor("wk", [P, PO * D], F32, kind="ExternalInput").ap()
    wv = nc.dram_tensor("wv", [P, PO * D], F32, kind="ExternalInput").ap()
    wg = nc.dram_tensor("wg", [P, PO * D], F32, kind="ExternalInput").ap()
    wout = nc.dram_tensor("wout", [P, PO * D], F32, kind="ExternalInput").ap()
    wgk1 = nc.dram_tensor("wgk1", [P, PO * 16], F32, kind="ExternalInput").ap()
    wgk2 = nc.dram_tensor("wgk2", [16, D], F32, kind="ExternalInput").ap()
    bgk2_col = nc.dram_tensor("bgk2_col", [P, PO], F32, kind="ExternalInput").ap()
    gamma_col = nc.dram_tensor("gamma_col", [P, PO], F32, kind="ExternalInput").ap()
    beta_col = nc.dram_tensor("beta_col", [P, PO], F32, kind="ExternalInput").ap()
    lng_col = nc.dram_tensor("lng_col", [P, PO], F32, kind="ExternalInput").ap()
    lnb_col = nc.dram_tensor("lnb_col", [P, PO], F32, kind="ExternalInput").ap()
    fc1b_col = nc.dram_tensor("fc1b_col", [P, NFC], F32, kind="ExternalInput").ap()
    fc2b_col = nc.dram_tensor("fc2b_col", [P, PO], F32, kind="ExternalInput").ap()
    fc1_q = nc.dram_tensor("fc1_q", [D, FF], I32, kind="ExternalInput").ap()
    fc1_sc = nc.dram_tensor("fc1_sc", [8, FF], F32, kind="ExternalInput").ap()
    fc2_q = nc.dram_tensor("fc2_q", [FF, D], I32, kind="ExternalInput").ap()
    fc2_sc = nc.dram_tensor("fc2_sc", [32, D], F32, kind="ExternalInput").ap()
    sbeta = nc.dram_tensor("sbeta", [P, 4], F32, kind="ExternalInput").ap()

    x1_out = nc.dram_tensor("x1_fm", [P, PO * T], F32, kind=x1_kind).ap()

    sp_kfm = nc.dram_tensor("sp_kfm", [NT, P, PO * TT], BF16)
    sp_qfm = nc.dram_tensor("sp_qfm", [NT, P, PO * TT], BF16)
    sp_gate = nc.dram_tensor("sp_gate", [NT, P, PO * TT], BF16)
    sp_An = nc.dram_tensor("sp_An", [NT, P, PO * TT], F32)
    sp_ktm = nc.dram_tensor("sp_ktm", [NT, P, CPT * D], BF16)
    sp_vt = nc.dram_tensor("sp_vt", [NT, P, CPT * D], BF16)
    w1bf = nc.dram_tensor("w1bf", [NFC, 8, P, P], BF16)
    w2bf = nc.dram_tensor("w2bf", [32, P, D], BF16)
    cc_in = nc.dram_tensor("cc_in", [P, PO * (HD + 1)], F32)
    cc_out = nc.dram_tensor("cc_out", [4 * P, PO * (HD + 1)], F32)

    return dict(x_fm=x_fm, wq=wq, wk=wk, wv=wv, wg=wg, wout=wout, wgk1=wgk1, wgk2=wgk2, bgk2_col=bgk2_col, gamma_col=gamma_col, beta_col=beta_col, lng_col=lng_col, lnb_col=lnb_col, fc1b_col=fc1b_col, fc2b_col=fc2b_col, fc1_q=fc1_q, fc1_sc=fc1_sc, fc2_q=fc2_q, fc2_sc=fc2_sc, sbeta=sbeta, x1_out=x1_out, sp_kfm=sp_kfm, sp_qfm=sp_qfm, sp_gate=sp_gate, sp_An=sp_An, sp_ktm=sp_ktm, sp_vt=sp_vt, w1bf=w1bf, w2bf=w2bf, cc_in=cc_in, cc_out=cc_out)


def _emit_l0(nc, sems, cnt, phases, A):
    x_fm = A["x_fm"]
    wq = A["wq"]
    wk = A["wk"]
    wv = A["wv"]
    wg = A["wg"]
    wout = A["wout"]
    wgk1 = A["wgk1"]
    wgk2 = A["wgk2"]
    bgk2_col = A["bgk2_col"]
    gamma_col = A["gamma_col"]
    beta_col = A["beta_col"]
    lng_col = A["lng_col"]
    lnb_col = A["lnb_col"]
    fc1b_col = A["fc1b_col"]
    fc2b_col = A["fc2b_col"]
    fc1_q = A["fc1_q"]
    fc1_sc = A["fc1_sc"]
    fc2_q = A["fc2_q"]
    fc2_sc = A["fc2_sc"]
    sbeta = A["sbeta"]
    x1_out = A["x1_out"]
    sp_kfm = A["sp_kfm"]
    sp_qfm = A["sp_qfm"]
    sp_gate = A["sp_gate"]
    sp_An = A["sp_An"]
    sp_ktm = A["sp_ktm"]
    sp_vt = A["sp_vt"]
    w1bf = A["w1bf"]
    w2bf = A["w2bf"]
    cc_in = A["cc_in"]
    cc_out = A["cc_out"]
    _st1 = contextlib.ExitStack()
    with _st1:
        xn_bf = _st1.enter_context(nc.sbuf_tensor("xn_bf", [P, PO * T], BF16))
        Call = _st1.enter_context(nc.sbuf_tensor("Call", [P, PO * NCH * HD], F32))
        Eall = _st1.enter_context(nc.sbuf_tensor("Eall", [P, PO * NCH], F32))
        cumE = _st1.enter_context(nc.sbuf_tensor("cumE", [P, PO * NCH], F32))
        Sinit = _st1.enter_context(nc.sbuf_tensor("Sinit", [P, PO * HD], F32))
        murow = _st1.enter_context(nc.sbuf_tensor("murow", [1, T], F32))
        rstdrow = _st1.enter_context(nc.sbuf_tensor("rstdrow", [1, T], F32))
        mrsrow = _st1.enter_context(nc.sbuf_tensor("mrsrow", [1, T], F32))
        ident_bf = _st1.enter_context(nc.sbuf_tensor("ident_bf", [P, P], BF16))
        ident_f = _st1.enter_context(nc.sbuf_tensor("ident_f", [P, P], F32))
        trimask = _st1.enter_context(nc.sbuf_tensor("trimask", [P, P], F32))
        zeros_f = _st1.enter_context(nc.sbuf_tensor("zeros_f", [P, P], F32))
        ccol = _st1.enter_context(nc.sbuf_tensor("ccol", [P, 8], F32))
        ones_row = _st1.enter_context(nc.sbuf_tensor("ones_row", [1, P], F32))
        gammas = _st1.enter_context(nc.sbuf_tensor("gammas", [P, PO * 2], F32))
        lns = _st1.enter_context(nc.sbuf_tensor("lns", [P, PO * 2], F32))
        fcb = _st1.enter_context(nc.sbuf_tensor("fcb", [P, NFC + PO], F32))
        bgk2s = _st1.enter_context(nc.sbuf_tensor("bgk2s", [P, PO], F32))
        sbetas = _st1.enter_context(nc.sbuf_tensor("sbetas", [P, 4], F32))
        rowtmp = _st1.enter_context(nc.sbuf_tensor("rowtmp", [1, TT], F32))
        rowtmp2 = _st1.enter_context(nc.sbuf_tensor("rowtmp2", [1, TT], F32))
        NEG1 = ccol[:, 0:1]
        NEG128 = ccol[:, 1:2]
        EPS = ccol[:, 2:3]
        ONEC = ccol[:, 4:5]

        p = Prog(nc, sems, cnt)
        for (buf, cmp) in ((ident_bf, ALU.not_equal), (ident_f, ALU.not_equal),
                           (trimask, ALU.is_gt)):
            p.op("gpsimd", (lambda g, b=buf: g.memset(b[:], 0.0)))
            p.op("gpsimd", (lambda g, b=buf, cm=cmp: g.affine_select(
                out=b[:], in_=b[:], compare_op=cm,
                fill=1.0, base=0, pattern=[[-1, P]], channel_multiplier=1)))
        p.op("gpsimd", lambda g: g.memset(zeros_f[:], 0.0))
        for (i, v) in ((0, -1.0), (1, -128.0), (2, 1e-5), (3, 1.0 / 1024.0), (4, 1.0)):
            p.op("gpsimd", (lambda g, i=i, v=v: g.memset(ccol[:, i:i + 1], v)))
        p.op("gpsimd", lambda g: g.memset(ones_row[:], 1.0))
        p.dma("sync", gammas[:, 0:PO], gamma_col)
        p.dma("sync", gammas[:, PO:2 * PO], beta_col)
        p.dma("sync", lns[:, 0:PO], lng_col)
        p.dma("sync", lns[:, PO:2 * PO], lnb_col)
        p.dma("sync", fcb[:, 0:NFC], fc1b_col)
        p.dma("sync", fcb[:, NFC:], fc2b_col)
        p.dma("sync", bgk2s[:], bgk2_col)
        p.dma("sync", sbetas[:], sbeta)
        p.emit("ph0" in phases)

        _st2 = contextlib.ExitStack()
        with _st2:
            wk_s = _st2.enter_context(nc.sbuf_tensor("wk_s", [P, PO * D], BF16))
            wv_s = _st2.enter_context(nc.sbuf_tensor("wv_s", [P, PO * D], BF16))
            wgk1_s = _st2.enter_context(nc.sbuf_tensor("wgk1_s", [P, PO * 16], BF16))
            wgk2_s = _st2.enter_context(nc.sbuf_tensor("wgk2_s", [16, D], BF16))
            xbuf = _st2.enter_context(nc.sbuf_tensor("xbuf", [P, PO * TT], F32))
            xsq = _st2.enter_context(nc.sbuf_tensor("xsq", [P, TT], F32))
            B1 = _st2.enter_context(nc.sbuf_tensor("B1", [P, TT], F32))
            B2 = _st2.enter_context(nc.sbuf_tensor("B2", [P, TT], F32))
            t1f = _st2.enter_context(nc.sbuf_tensor("t1f", [P, TT], F32))
            kfm_t = _st2.enter_context(nc.sbuf_tensor("kfm_t", [P, PO * TT], BF16))
            v32 = _st2.enter_context(nc.sbuf_tensor("v32", [P, PO * TT], F32))
            g32 = _st2.enter_context(nc.sbuf_tensor("g32", [P, PO * TT], F32))
            An_t = _st2.enter_context(nc.sbuf_tensor("An_t", [P, PO * TT], F32))
            mid_bf = _st2.enter_context(nc.sbuf_tensor("mid_bf", [16, TT], BF16))
            zf = _st2.enter_context(nc.sbuf_tensor("zf", [P, TT], F32))
            uf = _st2.enter_context(nc.sbuf_tensor("uf", [P, TT], F32))
            uf2 = _st2.enter_context(nc.sbuf_tensor("uf2", [P, TT], F32))
            expnA = _st2.enter_context(nc.sbuf_tensor("expnA", [P, PO * CH], F32))
            vt_fm = _st2.enter_context(nc.sbuf_tensor("vt_fm", [P, PO * CH], BF16))
            ktm_t = _st2.enter_context(nc.sbuf_tensor("ktm_t", [P, CPT * D], BF16))
            vt_t = _st2.enter_context(nc.sbuf_tensor("vt_t", [P, CPT * D], BF16))
            ps_a = _st2.enter_context(nc.psum_tensor("ps_a", [P, TT], F32))
            ps_b = _st2.enter_context(nc.psum_tensor("ps_b", [P, TT], F32))
            ps_s1 = _st2.enter_context(nc.psum_tensor("ps_s1", [1, TT], F32))
            ps_s2 = _st2.enter_context(nc.psum_tensor("ps_s2", [1, TT], F32))
            ps_mid = _st2.enter_context(nc.psum_tensor("ps_mid", [16, TT], F32))
            ps_t1 = _st2.enter_context(nc.psum_tensor("ps_t1", [P, CH], BF16))
            ps_t2 = _st2.enter_context(nc.psum_tensor("ps_t2", [P, CH], BF16))
            ps_c = _st2.enter_context(nc.psum_tensor("ps_c", [HD, HD], F32))
            p = Prog(nc, sems, cnt)
            p.dma("gpsimd", wk_s[:], wk)
            p.dma("gpsimd", wv_s[:], wv)
            p.dma("gpsimd", wgk1_s[:], wgk1)
            p.dma("gpsimd", wgk2_s[:], wgk2)
            wld = p.c("d_gpsimd")

            for t in range(NT):
                t0 = t * TT
                g_pe, g_act, g_dve = p.c("pe"), p.c("act"), p.c("dve")
                xw = p.dma("sync", po_view(xbuf, TT), po_view(x_fm, T)[:, :, t0:t0 + TT],
                           waits=[("pe", g_pe), ("act", g_act), ("dve", g_dve)])
                if "stats" not in PH1A_PARTS:
                    continue
                for po in range(PO):
                    p.op("tensor", (lambda e, po=po: e.matmul(
                        ps_s1[:], ONEC, xbuf[:, po * TT:(po + 1) * TT],
                        start=(po == 0), stop=(po == PO - 1))),
                        waits=([("d_sync", xw)] if po == 0 else ()),
                        inc=("pe" if po == PO - 1 else None))
                for po in range(PO):
                    aw = p.op("scalar", (lambda e, po=po: e.activation(
                        out=xsq[:], in_=xbuf[:, po * TT:(po + 1) * TT], func=AF.Square)),
                        waits=([("d_sync", xw), ("pe", g_pe)] if po == 0 else [("pe", p.c("pe"))]),
                        inc="act")
                    p.op("tensor", (lambda e, po=po: e.matmul(
                        ps_s2[:], ONEC, xsq[:],
                        start=(po == 0), stop=(po == PO - 1))),
                        waits=[("act", aw)], inc="pe")
                sq_done = p.c("pe")
                p.op("vector", (lambda e, t0=t0: e.tensor_scalar(
                    out=murow[0:1, t0:t0 + TT], in0=ps_s1[:], scalar1=1.0 / 1024.0,
                    scalar2=None, op0=ALU.mult)), waits=[("pe", sq_done)])
                p.op("vector", lambda e: e.tensor_scalar(
                    out=rowtmp[:], in0=ps_s2[:], scalar1=1.0 / 1024.0,
                    scalar2=None, op0=ALU.mult))
                p.op("vector", (lambda e, t0=t0: e.tensor_tensor(
                    out=rowtmp2[:], in0=murow[0:1, t0:t0 + TT],
                    in1=murow[0:1, t0:t0 + TT], op=ALU.mult)))
                dv = p.op("vector", lambda e: e.tensor_tensor(
                    out=rowtmp[:], in0=rowtmp[:], in1=rowtmp2[:], op=ALU.subtract),
                    inc="dve")
                av = p.op("scalar", lambda e: e.activation(
                    out=rowtmp2[:], in_=rowtmp[:], func=AF.Sqrt, bias=EPS[0:1, :], scale=1.0),
                    waits=[("dve", dv)], inc="act")
                rcp = p.op("vector", (lambda e, t0=t0: e.reciprocal(
                    out=rstdrow[0:1, t0:t0 + TT], in_=rowtmp2[:])),
                    waits=[("act", av)], inc="dve")
                dv = p.op("vector", (lambda e, t0=t0: e.tensor_tensor(
                    out=mrsrow[0:1, t0:t0 + TT], in0=murow[0:1, t0:t0 + TT],
                    in1=rstdrow[0:1, t0:t0 + TT], op=ALU.mult)),
                    waits=[("dve", rcp)], inc="dve")
                bm = p.op("tensor", (lambda e, t0=t0: e.matmul(
                    ps_b[:], ones_row[:], rstdrow[0:1, t0:t0 + TT], start=True, stop=True)),
                    waits=[("dve", dv)], inc="pe")
                ev = p.op("vector", lambda e: e.tensor_copy(out=B1[:], in_=ps_b[:]),
                          waits=[("pe", bm)], inc="dve")
                bm = p.op("tensor", (lambda e, t0=t0: e.matmul(
                    ps_b[:], ones_row[:], mrsrow[0:1, t0:t0 + TT], start=True, stop=True)),
                    waits=[("dve", ev)], inc="pe")
                ev = p.op("vector", lambda e: e.tensor_copy(out=B2[:], in_=ps_b[:]),
                          waits=[("pe", bm)], inc="dve")
                for po in range(PO):
                    p.op("vector", (lambda e, po=po: e.tensor_tensor(
                        out=t1f[:], in0=xbuf[:, po * TT:(po + 1) * TT], in1=B1[:], op=ALU.mult)))
                    p.op("vector", (lambda e, po=po, t0=t0: e.tensor_tensor(
                        out=xn_bf[:, po * T + t0: po * T + t0 + TT], in0=t1f[:], in1=B2[:],
                        op=ALU.subtract)))
                xn_done = p.mark("vector", "dve")
                if "proj" not in PH1A_PARTS:
                    continue

                for (wbuf, isk) in ((wk_s, True), (wv_s, False)):
                    for po in range(PO):
                        for kc in range(PO):
                            p.op("tensor", (lambda e, wbuf=wbuf, po=po, kc=kc, t0=t0: e.matmul(
                                ps_a[:], wbuf[:, kc * D + po * P: kc * D + (po + 1) * P],
                                xn_bf[:, kc * T + t0: kc * T + t0 + TT],
                                start=(kc == 0), stop=(kc == PO - 1))),
                                waits=([("dve", xn_done), ("d_gpsimd", wld)]
                                       if po == 0 and kc == 0 else ()))
                        mmv = p.mark("tensor", "pe")
                        if isk:
                            aw = p.op("scalar", (lambda e, po=po: e.activation(
                                out=kfm_t[:, po * TT:(po + 1) * TT], in_=ps_a[:], func=AF.Copy)),
                                waits=[("pe", mmv)], inc="act")
                            p.nop("tensor", waits=[("act", aw)])
                        else:
                            dw = p.op("vector", (lambda e, po=po: e.tensor_copy(
                                out=v32[:, po * TT:(po + 1) * TT], in_=ps_a[:])),
                                waits=[("pe", mmv)], inc="dve")
                            p.nop("tensor", waits=[("dve", dw)])
                kv_done_act = p.c("act")
                if "gk" not in PH1A_PARTS:
                    continue

                for kc in range(PO):
                    p.op("tensor", (lambda e, kc=kc, t0=t0: e.matmul(
                        ps_mid[:], wgk1_s[:, kc * 16:(kc + 1) * 16],
                        xn_bf[:, kc * T + t0: kc * T + t0 + TT],
                        start=(kc == 0), stop=(kc == PO - 1))))
                mmv = p.mark("tensor", "pe")
                am = p.op("scalar", lambda e: e.activation(
                    out=mid_bf[:], in_=ps_mid[:], func=AF.Copy),
                    waits=[("pe", mmv)], inc="act")
                for po in range(PO):
                    mm = p.op("tensor", (lambda e, po=po: e.matmul(
                        ps_b[:], wgk2_s[:, po * P:(po + 1) * P], mid_bf[:],
                        start=True, stop=True)),
                        waits=([("act", am)] if po == 0 else [("dve", p.c("dve"))]),
                        inc="pe")
                    p.op("scalar", (lambda e, po=po: e.activation(
                        out=zf[:], in_=ps_b[:], func=AF.Identity,
                        bias=bgk2s[:, po:po + 1], scale=1.0)), waits=[("pe", mm)])
                    p.op("scalar", lambda e: e.activation(out=uf[:], in_=zf[:], func=AF.Abs))
                    p.op("scalar", lambda e: e.activation(out=uf2[:], in_=uf[:], func=AF.Exp, scale=NEG1))
                    a2 = p.op("scalar", lambda e: e.activation(
                        out=uf[:], in_=uf2[:], func=AF.Ln, bias=1.0, scale=1.0), inc="act")
                    p.op("vector", lambda e: e.tensor_scalar(
                        out=zf[:], in0=zf[:], scalar1=0.0, scalar2=None, op0=ALU.min),
                        waits=[("act", a2)])
                    p.op("vector", lambda e: e.tensor_tensor(
                        out=uf[:], in0=uf[:], in1=zf[:], op=ALU.subtract))
                    p.op("vector", (lambda e, po=po: e.tensor_scalar(
                        out=g32[:, po * TT:(po + 1) * TT], in0=uf[:], scalar1=0.0625,
                        scalar2=None, op0=ALU.mult)), inc="dve")
                g_done = p.c("dve")
                if "chunk" not in PH1A_PARTS:
                    continue

                for c in range(CPT):
                    ci = t * CPT + c
                    for po in range(PO):
                        p.op("vector", (lambda e, po=po, c=c: e.tensor_tensor_scan(
                            out=An_t[:, po * TT + c * CH: po * TT + (c + 1) * CH],
                            data0=g32[:, po * TT + c * CH: po * TT + (c + 1) * CH],
                            data1=zeros_f[:, 0:CH], initial=0.0,
                            op0=ALU.add, op1=ALU.add)),
                            waits=([("dve", g_done)] if po == 0 and c == 0 else ()))
                    sc_done = p.mark("vector", "dve")
                    p.op("scalar", (lambda e, c=c, ci=ci: e.activation(
                        out=po_view(Eall, NCH)[:, :, ci:ci + 1],
                        in_=po_view(An_t, TT)[:, :, c * CH + CH - 1: c * CH + CH],
                        func=AF.Exp, scale=NEG1)), waits=[("dve", sc_done)], inc="act")
                    a4 = p.op("scalar", (lambda e, c=c: e.activation(
                        out=po_view(expnA, CH),
                        in_=po_view(An_t, TT)[:, :, c * CH:(c + 1) * CH],
                        func=AF.Exp)), inc="act")
                    for po in range(PO):
                        p.op("vector", (lambda e, po=po, c=c: e.tensor_tensor(
                            out=vt_fm[:, po * CH:(po + 1) * CH],
                            in0=expnA[:, po * CH:(po + 1) * CH],
                            in1=v32[:, po * TT + c * CH: po * TT + (c + 1) * CH],
                            op=ALU.mult)), waits=([("act", a4)] if po == 0 else ()))
                    vt_done = p.mark("vector", "dve")
                    for po in range(PO):
                        pst = ps_t1 if po % 2 == 0 else ps_t2
                        mm = p.op("tensor", (lambda e, po=po, pst=pst: e.transpose(
                            pst[:], vt_fm[:, po * CH:(po + 1) * CH], ident_bf[:])),
                            waits=[("dve", (vt_done if po == 0 else p.c("dve")))], inc="pe")
                        p.op("vector", (lambda e, po=po, c=c, pst=pst: e.tensor_copy(
                            out=vt_t[:, c * D + po * CH: c * D + (po + 1) * CH], in_=pst[:])),
                            waits=[("pe", mm)], inc="dve")
                    for po in range(PO):
                        pst = ps_t1 if po % 2 == 0 else ps_t2
                        mm = p.op("tensor", (lambda e, po=po, c=c, pst=pst: e.transpose(
                            pst[:], kfm_t[:, po * TT + c * CH: po * TT + (c + 1) * CH],
                            ident_bf[:])),
                            waits=[("dve", p.c("dve")), ("act", kv_done_act)], inc="pe")
                        p.op("vector", (lambda e, po=po, c=c, pst=pst: e.tensor_copy(
                            out=ktm_t[:, c * D + po * CH: c * D + (po + 1) * CH], in_=pst[:])),
                            waits=[("pe", mm)], inc="dve")
                    tr_done = p.c("dve")
                    for h in range(NH):
                        hb = (h % 2) * HD
                        po = h // 2
                        mm = p.op("tensor", (lambda e, h=h, c=c: e.matmul(
                            ps_c[:], vt_t[:, c * D + h * HD: c * D + (h + 1) * HD],
                            ktm_t[:, c * D + h * HD: c * D + (h + 1) * HD],
                            start=True, stop=True)),
                            waits=[("dve", (tr_done if h == 0 else p.c("dve")))], inc="pe")
                        p.op("vector", (lambda e, hb=hb, po=po, ci=ci: e.tensor_scalar(
                            out=Call[hb:hb + HD, (po * NCH + ci) * HD:(po * NCH + ci + 1) * HD],
                            in0=ps_c[:],
                            scalar1=Eall[hb:hb + HD, po * NCH + ci: po * NCH + ci + 1],
                            scalar2=None, op0=ALU.mult)),
                            waits=[("pe", mm), ("act", p.c("act"))], inc="dve")
                if "spill" not in PH1A_PARTS:
                    continue
                p.dma("scalar", sp_An[t], An_t[:], waits=[("dve", p.c("dve")), ("act", p.c("act"))])
                p.dma("scalar", sp_kfm[t], kfm_t[:])
                p.dma("scalar", sp_ktm[t], ktm_t[:])
                p.dma("scalar", sp_vt[t], vt_t[:])
                spw = p.c("d_scalar")
                p.nop("tensor", waits=[("d_scalar", spw)])
                p.nop("vector", waits=[("d_scalar", spw)])
                p.nop("scalar", waits=[("d_scalar", spw)])
            p.emit("ph1a" in phases)

        _st3 = contextlib.ExitStack()
        with _st3:
            wq_s = _st3.enter_context(nc.sbuf_tensor("wq_s", [P, PO * D], BF16))
            wg_s = _st3.enter_context(nc.sbuf_tensor("wg_s", [P, PO * D], BF16))
            qfm_t = _st3.enter_context(nc.sbuf_tensor("qfm_t", [P, PO * TT], BF16))
            gate_t = _st3.enter_context(nc.sbuf_tensor("gate_t", [P, PO * TT], BF16))
            ps_a1 = _st3.enter_context(nc.psum_tensor("ps_a1", [P, TT], F32))
            ps_b1 = _st3.enter_context(nc.psum_tensor("ps_b1", [P, TT], F32))
            p = Prog(nc, sems, cnt)
            p.dma("gpsimd", wq_s[:], wq)
            p.dma("gpsimd", wg_s[:], wg)
            wld = p.c("d_gpsimd")
            for t in range(NT):
                t0 = t * TT
                for po in range(PO):
                    for kc in range(PO):
                        p.op("tensor", (lambda e, po=po, kc=kc, t0=t0: e.matmul(
                            ps_a1[:], wq_s[:, kc * D + po * P: kc * D + (po + 1) * P],
                            xn_bf[:, kc * T + t0: kc * T + t0 + TT],
                            start=(kc == 0), stop=(kc == PO - 1))),
                            waits=([("d_gpsimd", wld)] if po == 0 and kc == 0 and t == 0 else ()))
                    m1 = p.mark("tensor", "pe")
                    a1 = p.op("scalar", (lambda e, po=po: e.activation(
                        out=qfm_t[:, po * TT:(po + 1) * TT], in_=ps_a1[:], func=AF.Copy)),
                        waits=[("pe", m1), ("d_scalar", p.c("d_scalar"))], inc="act")
                    for kc in range(PO):
                        p.op("tensor", (lambda e, po=po, kc=kc, t0=t0: e.matmul(
                            ps_b1[:], wg_s[:, kc * D + po * P: kc * D + (po + 1) * P],
                            xn_bf[:, kc * T + t0: kc * T + t0 + TT],
                            start=(kc == 0), stop=(kc == PO - 1))),
                            waits=([("act", a1)] if kc == 0 else ()))
                    m2 = p.mark("tensor", "pe")
                    a2 = p.op("scalar", (lambda e, po=po: e.activation(
                        out=gate_t[:, po * TT:(po + 1) * TT], in_=ps_b1[:], func=AF.Silu)),
                        waits=[("pe", m2)], inc="act")
                    p.nop("tensor", waits=[("act", a2)])
                p.dma("scalar", sp_qfm[t], qfm_t[:], waits=[("act", p.c("act"))])
                p.dma("scalar", sp_gate[t], gate_t[:])
            p.emit("ph1b" in phases)

        _st4 = contextlib.ExitStack()
        with _st4:
            tmp_s = _st4.enter_context(nc.sbuf_tensor("tmp_s", [P, HD], F32))
            gath = _st4.enter_context(nc.sbuf_tensor("gath", [P, PO * (HD + 1)], F32))
            p = Prog(nc, sems, cnt)
            for po in range(PO):
                p.op("vector", (lambda e, po=po: e.tensor_copy(
                    out=cumE[:, po * NCH: po * NCH + 1], in_=Eall[:, po * NCH: po * NCH + 1])))
            for j in range(1, NCH):
                for po in range(PO):
                    p.op("vector", (lambda e, po=po, j=j: e.tensor_scalar(
                        out=tmp_s[:],
                        in0=Call[:, (po * NCH + j - 1) * HD:(po * NCH + j) * HD],
                        scalar1=Eall[:, po * NCH + j: po * NCH + j + 1],
                        scalar2=None, op0=ALU.mult)))
                    p.op("vector", (lambda e, po=po, j=j: e.tensor_tensor(
                        out=Call[:, (po * NCH + j) * HD:(po * NCH + j + 1) * HD],
                        in0=Call[:, (po * NCH + j) * HD:(po * NCH + j + 1) * HD],
                        in1=tmp_s[:], op=ALU.add)))
                    p.op("vector", (lambda e, po=po, j=j: e.tensor_tensor(
                        out=cumE[:, po * NCH + j: po * NCH + j + 1],
                        in0=cumE[:, po * NCH + j - 1: po * NCH + j],
                        in1=Eall[:, po * NCH + j: po * NCH + j + 1], op=ALU.mult)))
            for po in range(PO):
                p.op("vector", (lambda e, po=po: e.tensor_copy(
                    out=gath[:, po * (HD + 1): po * (HD + 1) + HD],
                    in_=Call[:, (po * NCH + NCH - 1) * HD:(po * NCH + NCH) * HD])))
                p.op("vector", (lambda e, po=po: e.tensor_copy(
                    out=gath[:, po * (HD + 1) + HD: (po + 1) * (HD + 1)],
                    in_=cumE[:, po * NCH + NCH - 1: po * NCH + NCH])))
            dv = p.mark("vector", "dve")
            p.dma("gpsimd", cc_in.ap(), gath[:], waits=[("dve", dv)])
            ccw = p.c("d_gpsimd")
            p.op("gpsimd", lambda e: e.collective_compute(
                "AllGather", ALU.bypass,
                replica_groups=[[0, 1, 2, 3], [4, 5, 6, 7]],
                ins=[cc_in.ap().opt()], outs=[cc_out.ap().opt()]),
                waits=[("d_gpsimd", ccw)], inc="cc")
            p.emit("ph2" in phases)

            p = Prog(nc, sems, cnt)
            mz = p.op("gpsimd", lambda e: e.memset(Sinit[:], 0.0), inc="pool")
            for r in range(4):
                ld = p.dma("sync", gath[:], cc_out[r * P:(r + 1) * P, :].opt(),
                           waits=([("cc", p.c("cc"))] if r == 0 else [("dve", p.c("dve"))]))
                p.nop("vector", waits=[("d_sync", ld), ("pool", mz)])
                for po in range(PO):
                    p.op("vector", (lambda e, po=po: e.tensor_scalar(
                        out=tmp_s[:],
                        in0=Sinit[:, po * HD:(po + 1) * HD],
                        scalar1=gath[:, po * (HD + 1) + HD: (po + 1) * (HD + 1)],
                        scalar2=None, op0=ALU.mult)))
                    p.op("vector", (lambda e, po=po: e.tensor_tensor(
                        out=tmp_s[:], in0=tmp_s[:],
                        in1=gath[:, po * (HD + 1): po * (HD + 1) + HD], op=ALU.add)))
                    p.op("vector", (lambda e, po=po: e.tensor_tensor(
                        out=tmp_s[:], in0=tmp_s[:],
                        in1=Sinit[:, po * HD:(po + 1) * HD], op=ALU.subtract)))
                    p.op("vector", (lambda e, po=po, r=r: e.scalar_tensor_tensor(
                        out=Sinit[:, po * HD:(po + 1) * HD],
                        in0=tmp_s[:], scalar=sbetas[:, r:r + 1],
                        in1=Sinit[:, po * HD:(po + 1) * HD],
                        op0=ALU.mult, op1=ALU.add)), inc="dve")
            for cch in range(NCH - 1, 0, -1):
                for po in range(PO):
                    p.op("vector", (lambda e, po=po, cch=cch: e.tensor_scalar(
                        out=tmp_s[:], in0=Sinit[:, po * HD:(po + 1) * HD],
                        scalar1=cumE[:, po * NCH + cch - 1: po * NCH + cch],
                        scalar2=None, op0=ALU.mult)))
                    p.op("vector", (lambda e, po=po, cch=cch: e.tensor_tensor(
                        out=Call[:, (po * NCH + cch) * HD:(po * NCH + cch + 1) * HD],
                        in0=Call[:, (po * NCH + cch - 1) * HD:(po * NCH + cch) * HD],
                        in1=tmp_s[:], op=ALU.add)))
            for po in range(PO):
                p.op("vector", (lambda e, po=po: e.tensor_copy(
                    out=Call[:, (po * NCH) * HD:(po * NCH + 1) * HD],
                    in_=Sinit[:, po * HD:(po + 1) * HD])))
            p.emit("ph2" in phases)

        _gffn_cm = nc.sbuf_tensor("gffn", [P, PO * T], BF16)
        gffn = _gffn_cm.__enter__()

        _st5 = contextlib.ExitStack()
        with _st5:
            qi_b = _st5.enter_context(nc.sbuf_tensor("qi_b", [P, FF], I32))
            scb = _st5.enter_context(nc.sbuf_tensor("scb", [P, FF], F32))
            qf_b = _st5.enter_context(nc.sbuf_tensor("qf_b", [P, FF], F32))
            wrow = _st5.enter_context(nc.sbuf_tensor("wrow", [P, FF], BF16))
            p = Prog(nc, sems, cnt)
            for kc in range(8):
                lq = p.dma("sync", qi_b[:], fc1_q[kc * P:(kc + 1) * P, :],
                           waits=[("d_scalar", p.c("d_scalar"))])
                ls = p.dma("gpsimd", scb[:], fc1_sc.partition_broadcast(P)[:, kc, :],
                           waits=[("dve", p.c("dve"))])
                aw = p.op("scalar", lambda e: e.activation(
                    out=qf_b[:], in_=qi_b[:], func=AF.Identity, bias=NEG128, scale=1.0),
                    waits=[("d_sync", lq)], inc="act")
                dw = p.op("vector", lambda e: e.tensor_tensor(
                    out=wrow[:], in0=qf_b[:], in1=scb[:], op=ALU.mult),
                    waits=[("act", aw), ("d_gpsimd", ls)], inc="dve")
                sw = p.dma("scalar",
                           w1bf.ap().rearrange("n k p q -> p n k q")[:, :, kc, :],
                           wrow.ap().rearrange("p (n q) -> p n q", n=NFC),
                           waits=[("dve", dw)])
                p.nop("sync", waits=[("d_scalar", sw)])
            for kc in range(32):
                lq = p.dma("sync", qi_b[:, 0:D], fc2_q[kc * P:(kc + 1) * P, :],
                           waits=[("d_scalar", p.c("d_scalar"))])
                ls = p.dma("gpsimd", scb[:, 0:D], fc2_sc.partition_broadcast(P)[:, kc, :],
                           waits=[("dve", p.c("dve"))])
                aw = p.op("scalar", lambda e: e.activation(
                    out=qf_b[:, 0:D], in_=qi_b[:, 0:D], func=AF.Identity, bias=NEG128, scale=1.0),
                    waits=[("d_sync", lq)], inc="act")
                dw = p.op("vector", lambda e: e.tensor_tensor(
                    out=wrow[:, 0:D], in0=qf_b[:, 0:D], in1=scb[:, 0:D], op=ALU.mult),
                    waits=[("act", aw), ("d_gpsimd", ls)], inc="dve")
                sw = p.dma("scalar", w2bf[kc], wrow[:, 0:D], waits=[("dve", dw)])
                p.nop("sync", waits=[("d_scalar", sw)])
            p.emit("ph3a" in phases)

        _st6 = contextlib.ExitStack()
        with _st6:
            w1sl = _st6.enter_context(nc.sbuf_tensor("w1sl", [P, 8 * 8 * P], BF16))
            w2sl = _st6.enter_context(nc.sbuf_tensor("w2sl", [P, 8 * D], BF16))
            ymid = _st6.enter_context(nc.sbuf_tensor("ymid", [P, NFC * TT], BF16))
            lnin = _st6.enter_context(nc.sbuf_tensor("lnin", [P, PO * TT], BF16))
            zacc = _st6.enter_context(nc.sbuf_tensor("zacc", [P, PO * TT], F32))
            xchk = _st6.enter_context(nc.sbuf_tensor("xchk", [P, TT], F32))
            B13 = _st6.enter_context(nc.sbuf_tensor("B13", [P, TT], F32))
            B23 = _st6.enter_context(nc.sbuf_tensor("B23", [P, TT], F32))
            t3a = _st6.enter_context(nc.sbuf_tensor("t3a", [P, TT], F32))
            t3b = _st6.enter_context(nc.sbuf_tensor("t3b", [P, TT], F32))
            ps_f1 = _st6.enter_context(nc.psum_tensor("ps_f1", [P, TT], F32))
            ps_f2 = _st6.enter_context(nc.psum_tensor("ps_f2", [P, TT], F32))
            ps_bc3 = _st6.enter_context(nc.psum_tensor("ps_bc3", [P, TT], F32))
            p = Prog(nc, sems, cnt)
            for t in range(NT):
                t0 = t * TT
                g_dve = p.c("dve")
                bm = p.op("tensor", (lambda e, t0=t0: e.matmul(
                    ps_bc3[:], ones_row[:], rstdrow[0:1, t0:t0 + TT], start=True, stop=True)),
                    waits=[("dve", g_dve), ("act", p.c("act"))], inc="pe")
                ev1 = p.op("vector", lambda e: e.tensor_copy(out=B13[:], in_=ps_bc3[:]),
                           waits=[("pe", bm)], inc="dve")
                bm = p.op("tensor", (lambda e, t0=t0: e.matmul(
                    ps_bc3[:], ones_row[:], mrsrow[0:1, t0:t0 + TT], start=True, stop=True)),
                    waits=[("dve", ev1)], inc="pe")
                ev2 = p.op("vector", lambda e: e.tensor_copy(out=B23[:], in_=ps_bc3[:]),
                           waits=[("pe", bm)], inc="dve")
                for kc in range(PO):
                    p.op("vector", (lambda e, kc=kc, t0=t0: e.tensor_scalar(
                        out=lnin[:, kc * TT:(kc + 1) * TT],
                        in0=xn_bf[:, kc * T + t0: kc * T + t0 + TT],
                        scalar1=lns[:, kc:kc + 1], scalar2=lns[:, PO + kc:PO + kc + 1],
                        op0=ALU.mult, op1=ALU.add)))
                lnin_done = p.mark("vector", "dve")

                for slab in range(4):
                    lw = p.dma("gpsimd",
                               w1sl.ap().rearrange("p (n k q) -> p n k q", n=8, k=8),
                               w1bf[slab * 8:(slab + 1) * 8].rearrange("n k p q -> p n k q"),
                               waits=[("pe", p.c("pe"))])
                    for nch in range(8):
                        n_g = slab * 8 + nch
                        for kc in range(PO):
                            p.op("tensor", (lambda e, nch=nch, kc=kc: e.matmul(
                                ps_f1[:], w1sl[:, (nch * 8 + kc) * P:(nch * 8 + kc + 1) * P],
                                lnin[:, kc * TT:(kc + 1) * TT],
                                start=(kc == 0), stop=(kc == PO - 1))),
                                waits=([("d_gpsimd", lw), ("dve", lnin_done)]
                                       if nch == 0 and kc == 0 else ()))
                        mmv = p.mark("tensor", "pe")
                        aw = p.op("scalar", (lambda e, n_g=n_g: e.activation(
                            out=ymid[:, n_g * TT:(n_g + 1) * TT], in_=ps_f1[:], func=AF.Silu,
                            bias=fcb[:, n_g:n_g + 1], scale=1.0)),
                            waits=[("pe", mmv)], inc="act")
                        p.nop("tensor", waits=[("act", aw)])
                for slab2 in range(4):
                    lw2 = p.dma("gpsimd",
                                w2sl.ap().rearrange("p (k m) -> p k m", k=8),
                                w2bf[slab2 * 8:(slab2 + 1) * 8].rearrange("k p m -> p k m"),
                                waits=[("pe", p.c("pe"))])
                    for mpo in range(PO):
                        for kk in range(8):
                            p.op("tensor", (lambda e, kk=kk, mpo=mpo, slab2=slab2: e.matmul(
                                ps_f2[:], w2sl[:, kk * D + mpo * P: kk * D + (mpo + 1) * P],
                                ymid[:, (slab2 * 8 + kk) * TT:(slab2 * 8 + kk + 1) * TT],
                                start=(kk == 0), stop=(kk == 7))),
                                waits=([("d_gpsimd", lw2), ("act", p.c("act"))]
                                       if kk == 0 and mpo == 0 else ()))
                        mmv = p.mark("tensor", "pe")
                        if slab2 == 0:
                            dw = p.op("vector", (lambda e, mpo=mpo: e.tensor_copy(
                                out=zacc[:, mpo * TT:(mpo + 1) * TT], in_=ps_f2[:])),
                                waits=[("pe", mmv)], inc="dve")
                        else:
                            dw = p.op("vector", (lambda e, mpo=mpo: e.tensor_tensor(
                                out=zacc[:, mpo * TT:(mpo + 1) * TT],
                                in0=zacc[:, mpo * TT:(mpo + 1) * TT],
                                in1=ps_f2[:], op=ALU.add)),
                                waits=[("pe", mmv)], inc="dve")
                        p.nop("tensor", waits=[("dve", dw)])
                for mpo in range(PO):
                    lx = p.dma("sync", xchk[:],
                               po_view(x_fm, T)[:, mpo, t0:t0 + TT],
                               waits=[("dve", p.c("dve")), ("act", p.c("act"))])
                    p.op("vector", (lambda e, mpo=mpo: e.tensor_scalar(
                        out=t3a[:], in0=zacc[:, mpo * TT:(mpo + 1) * TT],
                        scalar1=fcb[:, NFC + mpo: NFC + mpo + 1], scalar2=None,
                        op0=ALU.add)))
                    p.op("vector", (lambda e, mpo=mpo: e.tensor_tensor(
                        out=t3b[:], in0=xchk[:], in1=B13[:], op=ALU.mult)),
                        waits=[("d_sync", lx)])
                    p.op("vector", lambda e: e.tensor_tensor(
                        out=t3b[:], in0=t3b[:], in1=B23[:], op=ALU.subtract))
                    d1 = p.op("vector", lambda e: e.scalar_tensor_tensor(
                        out=t3b[:], in0=t3a[:], scalar=0.5, in1=t3b[:],
                        op0=ALU.mult, op1=ALU.add), inc="dve")
                    a2 = p.op("scalar", (lambda e, mpo=mpo, t0=t0: e.activation(
                        out=gffn[:, mpo * T + t0: mpo * T + t0 + TT], in_=t3b[:], func=AF.Silu)),
                        waits=[("dve", d1)], inc="act")
                    p.nop("vector", waits=[("act", a2)])
            p.emit("ph3b" in phases)

        _st7 = contextlib.ExitStack()
        with _st7:
            wout_s = _st7.enter_context(nc.sbuf_tensor("wout_s", [P, PO * D], BF16))
            qg = _st7.enter_context(nc.sbuf_tensor("qg", [P, PO * TT], BF16))
            kg = _st7.enter_context(nc.sbuf_tensor("kg", [P, PO * TT], BF16))
            gateg = _st7.enter_context(nc.sbuf_tensor("gateg", [P, PO * TT], BF16))
            Ang = _st7.enter_context(nc.sbuf_tensor("Ang", [P, PO * TT], F32))
            ktg = _st7.enter_context(nc.sbuf_tensor("ktg", [P, CPT * D], BF16))
            vtg = _st7.enter_context(nc.sbuf_tensor("vtg", [P, CPT * D], BF16))
            Skv = _st7.enter_context(nc.sbuf_tensor("Skv", [P, (NH // 2) * HD], BF16))
            expA = _st7.enter_context(nc.sbuf_tensor("expA", [P, PO * CH], F32))
            mask_s = _st7.enter_context(nc.sbuf_tensor("mask_s", [P, CH], BF16))
            outb = _st7.enter_context(nc.sbuf_tensor("outb", [P, PO * CH], F32))
            sqb = _st7.enter_context(nc.sbuf_tensor("sqb", [P, CH], F32))
            B14 = _st7.enter_context(nc.sbuf_tensor("B14", [P, CH], F32))
            B24 = _st7.enter_context(nc.sbuf_tensor("B24", [P, CH], F32))
            lng4 = _st7.enter_context(nc.sbuf_tensor("lng4", [P, PO * TT], BF16))
            g32b = _st7.enter_context(nc.sbuf_tensor("g32b", [P, TT], F32))
            xob = _st7.enter_context(nc.sbuf_tensor("xob", [P, TT], F32))
            xg4 = _st7.enter_context(nc.sbuf_tensor("xg4", [P, PO * TT], F32))
            row4 = _st7.enter_context(nc.sbuf_tensor("row4", [1, CH], F32))
            row4b = _st7.enter_context(nc.sbuf_tensor("row4b", [1, CH], F32))
            ps_sc4 = _st7.enter_context(nc.psum_tensor("ps_sc4", [P, CH], F32))
            ps_mi4 = _st7.enter_context(nc.psum_tensor("ps_mi4", [HD, CH], F32))
            ps_st4 = _st7.enter_context(nc.psum_tensor("ps_st4", [HD, HD], F32))
            ps_s14 = _st7.enter_context(nc.psum_tensor("ps_s14", [1, CH], F32))
            ps_s24 = _st7.enter_context(nc.psum_tensor("ps_s24", [1, CH], F32))
            ps_bc4 = _st7.enter_context(nc.psum_tensor("ps_bc4", [P, CH], F32))
            ps_w4 = _st7.enter_context(nc.psum_tensor("ps_w4", [P, TT], F32))
            p = Prog(nc, sems, cnt)
            p.dma("gpsimd", wout_s[:], wout)
            wld = p.c("d_gpsimd")
            for t in range(NT):
                t0 = t * TT
                guard = [("pe", p.c("pe")), ("act", p.c("act")), ("dve", p.c("dve"))]
                p.dma("sync", qg[:], sp_qfm[t], waits=guard)
                p.dma("sync", kg[:], sp_kfm[t])
                p.dma("sync", gateg[:], sp_gate[t])
                lall_sync = p.c("d_sync")
                p.dma("scalar", Ang[:], sp_An[t], waits=guard)
                p.dma("scalar", ktg[:], sp_ktm[t])
                p.dma("scalar", vtg[:], sp_vt[t])
                lall_sc = p.c("d_scalar")
                lxo = p.dma("gpsimd", xg4[:].rearrange("p (po t) -> p po t", po=PO),
                            po_view(x_fm, T)[:, :, t0:t0 + TT], waits=guard)
                for c in range(CPT):
                    ci = t * CPT + c
                    ea = p.op("scalar", (lambda e, c=c: e.activation(
                        out=po_view(expA, CH),
                        in_=po_view(Ang, TT)[:, :, c * CH:(c + 1) * CH],
                        func=AF.Exp, scale=NEG1)),
                        waits=[("d_scalar", lall_sc), ("dve", p.c("dve"))], inc="act")
                    for h in range(NH):
                        hb = (h % 2) * HD
                        po = h // 2
                        mm = p.op("tensor", (lambda e, hb=hb, po=po, ci=ci: e.transpose(
                            ps_st4[:], Call[hb:hb + HD, (po * NCH + ci) * HD:(po * NCH + ci + 1) * HD],
                            ident_f[hb:hb + HD, hb:hb + HD])),
                            waits=[("act", p.c("act"))], inc="pe")
                        p.op("scalar", (lambda e, hb=hb, po=po: e.activation(
                            out=Skv[hb:hb + HD, po * HD:(po + 1) * HD], in_=ps_st4[:], func=AF.Copy)),
                            waits=[("pe", mm)], inc="act")
                    st_done = p.c("act")
                    for h in range(NH):
                        hb = (h % 2) * HD
                        po = h // 2
                        mm = p.op("tensor", (lambda e, hb=hb, po=po, c=c: e.matmul(
                            ps_sc4[:], kg[hb:hb + HD, po * TT + c * CH: po * TT + (c + 1) * CH],
                            qg[hb:hb + HD, po * TT + c * CH: po * TT + (c + 1) * CH],
                            start=True, stop=True)),
                            waits=([("d_sync", lall_sync), ("act", st_done)] if h == 0 else
                                   [("dve", p.c("dve"))]),
                            inc="pe")
                        msk = p.op("vector", (lambda e: e.tensor_tensor(
                            out=mask_s[:], in0=ps_sc4[:], in1=trimask[:], op=ALU.mult)),
                            waits=[("pe", mm)], inc="dve")
                        p.op("tensor", (lambda e, h=h, c=c: e.matmul(
                            ps_mi4[:], vtg[:, c * D + h * HD: c * D + (h + 1) * HD],
                            mask_s[:], start=True, stop=False)),
                            waits=[("dve", msk)])
                        mm2 = p.op("tensor", (lambda e, hb=hb, po=po, c=c: e.matmul(
                            ps_mi4[:], Skv[hb:hb + HD, po * HD:(po + 1) * HD],
                            qg[hb:hb + HD, po * TT + c * CH: po * TT + (c + 1) * CH],
                            start=False, stop=True)), inc="pe")
                        p.op("vector", (lambda e, hb=hb, po=po: e.tensor_tensor(
                            out=outb[hb:hb + HD, po * CH:(po + 1) * CH],
                            in0=ps_mi4[:], in1=expA[hb:hb + HD, po * CH:(po + 1) * CH],
                            op=ALU.mult)),
                            waits=[("pe", mm2), ("act", ea)], inc="dve")
                    out_done = p.c("dve")
                    for po in range(PO):
                        p.op("tensor", (lambda e, po=po: e.matmul(
                            ps_s14[:], ONEC, outb[:, po * CH:(po + 1) * CH],
                            start=(po == 0), stop=(po == PO - 1))),
                            waits=([("dve", out_done)] if po == 0 else ()))
                    p.mark("tensor", "pe")
                    for po in range(PO):
                        aw = p.op("scalar", (lambda e, po=po: e.activation(
                            out=sqb[:], in_=outb[:, po * CH:(po + 1) * CH], func=AF.Square)),
                            waits=([("dve", out_done)] if po == 0 else [("pe", p.c("pe"))]),
                            inc="act")
                        p.op("tensor", (lambda e, po=po: e.matmul(
                            ps_s24[:], ONEC, sqb[:],
                            start=(po == 0), stop=(po == PO - 1))),
                            waits=[("act", aw)], inc="pe")
                    stat_done = p.c("pe")
                    p.op("vector", lambda e: e.tensor_scalar(
                        out=row4[:], in0=ps_s14[:], scalar1=1.0 / 1024.0, scalar2=None,
                        op0=ALU.mult), waits=[("pe", stat_done)])
                    p.op("vector", lambda e: e.tensor_scalar(
                        out=row4b[:], in0=ps_s24[:], scalar1=1.0 / 1024.0, scalar2=None,
                        op0=ALU.mult))
                    p.op("vector", lambda e: e.tensor_tensor(
                        out=g32b[0:1, 0:CH], in0=row4[:], in1=row4[:], op=ALU.mult))
                    dv = p.op("vector", lambda e: e.tensor_tensor(
                        out=row4b[:], in0=row4b[:], in1=g32b[0:1, 0:CH], op=ALU.subtract),
                        inc="dve")
                    av = p.op("scalar", lambda e: e.activation(
                        out=g32b[0:1, 0:CH], in_=row4b[:], func=AF.Sqrt, bias=EPS[0:1, :], scale=1.0),
                        waits=[("dve", dv)], inc="act")
                    dv = p.op("vector", lambda e: e.reciprocal(out=row4b[:], in_=g32b[0:1, 0:CH]),
                              waits=[("act", av)], inc="dve")
                    p.op("vector", lambda e: e.tensor_tensor(
                        out=row4[:], in0=row4[:], in1=row4b[:], op=ALU.mult),
                        waits=[("dve", dv)], inc="dve")
                    bm = p.op("tensor", lambda e: e.matmul(
                        ps_bc4[:], ones_row[:], row4b[:], start=True, stop=True),
                        waits=[("dve", p.c("dve"))], inc="pe")
                    e1 = p.op("vector", lambda e: e.tensor_copy(out=B14[:], in_=ps_bc4[:]),
                              waits=[("pe", bm)], inc="dve")
                    bm = p.op("tensor", lambda e: e.matmul(
                        ps_bc4[:], ones_row[:], row4[:], start=True, stop=True),
                        waits=[("dve", e1)], inc="pe")
                    e2 = p.op("vector", lambda e: e.tensor_copy(out=B24[:], in_=ps_bc4[:]),
                              waits=[("pe", bm)], inc="dve")
                    for po in range(PO):
                        p.op("vector", (lambda e, po=po: e.tensor_tensor(
                            out=sqb[:], in0=outb[:, po * CH:(po + 1) * CH], in1=B14[:],
                            op=ALU.mult)),
                            waits=([("dve", e2), ("pe", stat_done)] if po == 0 else ()))
                        p.op("vector", lambda e: e.tensor_tensor(
                            out=sqb[:], in0=sqb[:], in1=B24[:], op=ALU.subtract))
                        p.op("vector", (lambda e, po=po: e.tensor_scalar(
                            out=sqb[:], in0=sqb[:],
                            scalar1=gammas[:, po:po + 1], scalar2=gammas[:, PO + po:PO + po + 1],
                            op0=ALU.mult, op1=ALU.add)))
                        aw = p.op("scalar", (lambda e, po=po, c=c: e.activation(
                            out=g32b[:, 0:CH],
                            in_=gateg[:, po * TT + c * CH: po * TT + (c + 1) * CH],
                            func=AF.Copy)),
                            waits=[("d_sync", lall_sync), ("dve", p.c("dve"))], inc="act")
                        p.op("vector", (lambda e, po=po, c=c: e.tensor_tensor(
                            out=lng4[:, po * TT + c * CH: po * TT + (c + 1) * CH],
                            in0=sqb[:], in1=g32b[:, 0:CH], op=ALU.mult)),
                            waits=[("act", aw)], inc="dve")
                ln_done = p.c("dve")
                for po in range(PO):
                    for kc in range(PO):
                        p.op("tensor", (lambda e, po=po, kc=kc: e.matmul(
                            ps_w4[:], wout_s[:, kc * D + po * P: kc * D + (po + 1) * P],
                            lng4[:, kc * TT:(kc + 1) * TT],
                            start=(kc == 0), stop=(kc == PO - 1))),
                            waits=([("dve", ln_done), ("d_gpsimd", wld)]
                                   if po == 0 and kc == 0 else ()))
                    mmv = p.mark("tensor", "pe")
                    aw = p.op("scalar", (lambda e, po=po, t0=t0: e.activation(
                        out=g32b[:], in_=gffn[:, po * T + t0: po * T + t0 + TT], func=AF.Copy)),
                        waits=[("dve", p.c("dve"))], inc="act")
                    p.op("vector", (lambda e: e.tensor_tensor(
                        out=xob[:], in0=ps_w4[:], in1=g32b[:], op=ALU.mult)),
                        waits=[("pe", mmv), ("act", aw), ("d_gpsimd", lxo)])
                    dv = p.op("vector", (lambda e, po=po: e.tensor_tensor(
                        out=xob[:], in0=xob[:], in1=xg4[:, po * TT:(po + 1) * TT], op=ALU.add)),
                        inc="dve")
                    sw = p.dma("scalar", po_view(x1_out, T)[:, po, t0:t0 + TT],
                               xob[:], waits=[("dve", dv)])
                    p.nop("vector", waits=[("d_scalar", sw)])
                    p.nop("tensor", waits=[("d_scalar", sw)])
            p.emit("ph4" in phases)

        _gffn_cm.__exit__(None, None, None)

        p = Prog(nc, sems, cnt)
        p.nop("sync", waits=[("d_scalar", cnt.get("d_scalar", 0))])
        p.emit("ph4" in phases)



def build_l0(detect_races=True, phases=("ph0", "ph1a", "ph1b", "ph2", "ph3a", "ph3b", "ph4")):
    phases = set(phases)
    nc = bass.Bass("TRN2", target_bir_lowering=False, detect_race_conditions=detect_races)
    A = _decl_l0(nc, "ExternalOutput")
    sems = {}
    cnt = {}
    stack = contextlib.ExitStack()
    for k in ("d_sync", "d_gpsimd", "d_scalar", "pe", "act", "dve", "pool", "cc"):
        sems[k] = stack.enter_context(nc.semaphore(k))
    with stack:
        _emit_l0(nc, sems, cnt, phases, A)
    return nc




def _fm_layout(a2d):
    Tc = a2d.shape[0]
    return np.ascontiguousarray(
        a2d.T.reshape(PO, P, Tc).transpose(1, 0, 2).reshape(P, PO * Tc))


def _w_layout(w):
    return np.ascontiguousarray(
        w.reshape(PO, P, w.shape[1]).transpose(1, 0, 2).reshape(P, PO * w.shape[1]))


def _col_layout(v):
    n = v.shape[0] // P
    return np.ascontiguousarray(v.reshape(n, P).T)


def unfm_layout(a):
    Tc = a.shape[1] // PO
    return a.reshape(P, PO, Tc).transpose(1, 0, 2).reshape(D, Tc).T


def prep_l0_inputs(inputs):
    x = np.asarray(inputs["L0_attn_x"])
    base = {
        "wq": _w_layout(np.asarray(inputs["L0_attn_Wq"])),
        "wk": _w_layout(np.asarray(inputs["L0_attn_Wk"])),
        "wv": _w_layout(np.asarray(inputs["L0_attn_Wv"])),
        "wg": _w_layout(np.asarray(inputs["L0_attn_Wg"])),
        "wout": _w_layout(np.asarray(inputs["L0_attn_Wout"])),
        "wgk1": _w_layout(np.asarray(inputs["L0_attn_Wgk1"])),
        "wgk2": np.ascontiguousarray(np.asarray(inputs["L0_attn_Wgk2"])),
        "bgk2_col": _col_layout(np.asarray(inputs["L0_attn_bgk2"])),
        "gamma_col": _col_layout(np.asarray(inputs["L0_attn_gamma"])),
        "beta_col": _col_layout(np.asarray(inputs["L0_attn_beta"])),
        "lng_col": _col_layout(np.asarray(inputs["L0_ffn_ln_gamma"])),
        "lnb_col": _col_layout(np.asarray(inputs["L0_ffn_ln_beta"])),
        "fc1b_col": _col_layout(np.asarray(inputs["L0_ffn_fc1_bias"])),
        "fc2b_col": _col_layout(np.asarray(inputs["L0_ffn_fc2_bias"])),
        "fc1_q": np.ascontiguousarray(np.asarray(inputs["L0_ffn_fc1_q"])),
        "fc1_sc": np.ascontiguousarray(np.asarray(inputs["L0_ffn_fc1_scales"])),
        "fc2_q": np.ascontiguousarray(np.asarray(inputs["L0_ffn_fc2_q"])),
        "fc2_sc": np.ascontiguousarray(np.asarray(inputs["L0_ffn_fc2_scales"])),
    }
    maps = []
    for core in range(8):
        b, r = core // 4, core % 4
        m = dict(base)
        m["x_fm"] = _fm_layout(x[b, r * T:(r + 1) * T, :])
        sb = np.zeros((P, 4), np.float32)
        sb[:, :r] = 1.0
        m["sbeta"] = sb
        maps.append(m)
    return maps


QUAT_TAB = {
    (0, 0): (0, 1.0), (1, 0): (1, -1.0), (2, 0): (2, -1.0), (3, 0): (3, -1.0),
    (0, 1): (1, 1.0), (1, 1): (0, 1.0), (2, 1): (3, -1.0), (3, 1): (2, 1.0),
    (0, 2): (2, 1.0), (1, 2): (3, 1.0), (2, 2): (0, 1.0), (3, 2): (1, -1.0),
    (0, 3): (3, 1.0), (1, 3): (2, -1.0), (2, 3): (1, 1.0), (3, 3): (0, 1.0),
}

TS5 = 512
NT5 = T // TS5
NTC = T // P


def _decl_l1(nc, x1_ap=None):
    x1_fm = x1_ap if x1_ap is not None else nc.dram_tensor("x1_fm", [P, PO * T], F32, kind="ExternalInput").ap()
    qw_r = nc.dram_tensor("qw_r", [P, 8 * 256], F32, kind="ExternalInput").ap()
    kw_r = nc.dram_tensor("kw_r", [P, 8 * 256], F32, kind="ExternalInput").ap()
    vw_r = nc.dram_tensor("vw_r", [P, 8 * 256], F32, kind="ExternalInput").ap()
    ow_r = nc.dram_tensor("ow_r", [P, 8 * 256], F32, kind="ExternalInput").ap()
    qb_r = nc.dram_tensor("qb_r", [1, 4 * 256], F32, kind="ExternalInput").ap()
    kb_r = nc.dram_tensor("kb_r", [1, 4 * 256], F32, kind="ExternalInput").ap()
    vb_r = nc.dram_tensor("vb_r", [1, 4 * 256], F32, kind="ExternalInput").ap()
    ob_r = nc.dram_tensor("ob_r", [1, 4 * 256], F32, kind="ExternalInput").ap()
    rw = nc.dram_tensor("rw", [D, 16], F32, kind="ExternalInput").ap()
    rb = nc.dram_tensor("rb", [1, 16], F32, kind="ExternalInput").ap()
    w1m = nc.dram_tensor("w1m", [NEXP, D, MI], F32, kind="ExternalInput").ap()
    b1m = nc.dram_tensor("b1m", [NEXP, P, MI // P], F32, kind="ExternalInput").ap()
    w2m = nc.dram_tensor("w2m", [NEXP, MI, D], F32, kind="ExternalInput").ap()
    b2m = nc.dram_tensor("b2m", [1, NEXP * D], F32, kind="ExternalInput").ap()
    jcol = nc.dram_tensor("jcol", [P, PO], F32, kind="ExternalInput").ap()

    out_fm = nc.dram_tensor("out_fm", [P, PO * T], F32, kind="ExternalOutput").ap()
    if DEBUG_L1:
        dbg_xn2 = nc.dram_tensor("dbg_xn2", [P, PO * T], BF16, kind="ExternalOutput").ap()
        dbg_ctm = nc.dram_tensor("dbg_ctm", [P, NTC * NEXP], F32, kind="ExternalOutput").ap()
        dbg_k = nc.dram_tensor("dbg_k", [P, PO * T], BF16, kind="ExternalOutput").ap()
        dbg_v = nc.dram_tensor("dbg_v", [P, NTC * NH * 65], BF16, kind="ExternalOutput").ap()
        dbg_q = nc.dram_tensor("dbg_q", [P, PO * T], BF16, kind="ExternalOutput").ap()
        dbg_oacc = nc.dram_tensor("dbg_oacc", [P, NH * 4 * 4 * 65], F32, kind="ExternalOutput").ap()
        dbg_s = nc.dram_tensor("dbg_s", [P, PO * T], BF16, kind="ExternalOutput").ap()
        dbg_z = nc.dram_tensor("dbg_z", [P, NTC * D], F32, kind="ExternalOutput").ap()

    sp_xn2 = nc.dram_tensor("sp_xn2", [P, PO * T], BF16)
    sp_s = nc.dram_tensor("sp_s", [P, PO * T], BF16)
    bias_sc = nc.dram_tensor("bias_sc", [4, PO, P], F32)
    ck_in = nc.dram_tensor("ck_in", [P, PO * T], BF16)
    ck_out = nc.dram_tensor("ck_out", [4 * P, PO * T], BF16)
    cv_in = nc.dram_tensor("cv_in", [P, NTC * NH * 65], BF16)
    cv_out = nc.dram_tensor("cv_out", [4 * P, NTC * NH * 65], BF16)

    d = dict(x1_fm=x1_fm, qw_r=qw_r, kw_r=kw_r, vw_r=vw_r, ow_r=ow_r, qb_r=qb_r, kb_r=kb_r, vb_r=vb_r, ob_r=ob_r, rw=rw, rb=rb, w1m=w1m, b1m=b1m, w2m=w2m, b2m=b2m, jcol=jcol, out_fm=out_fm, sp_xn2=sp_xn2, sp_s=sp_s, bias_sc=bias_sc, ck_in=ck_in, ck_out=ck_out, cv_in=cv_in, cv_out=cv_out)
    if DEBUG_L1:
        d.update(dbg_xn2=dbg_xn2, dbg_ctm=dbg_ctm, dbg_k=dbg_k, dbg_v=dbg_v, dbg_q=dbg_q, dbg_oacc=dbg_oacc, dbg_s=dbg_s, dbg_z=dbg_z)
    return d


def _emit_l1(nc, sems, cnt, A):
    x1_fm = A["x1_fm"]
    qw_r = A["qw_r"]
    kw_r = A["kw_r"]
    vw_r = A["vw_r"]
    ow_r = A["ow_r"]
    qb_r = A["qb_r"]
    kb_r = A["kb_r"]
    vb_r = A["vb_r"]
    ob_r = A["ob_r"]
    rw = A["rw"]
    rb = A["rb"]
    w1m = A["w1m"]
    b1m = A["b1m"]
    w2m = A["w2m"]
    b2m = A["b2m"]
    jcol = A["jcol"]
    out_fm = A["out_fm"]
    dbg_xn2 = A.get("dbg_xn2")
    dbg_ctm = A.get("dbg_ctm")
    dbg_k = A.get("dbg_k")
    dbg_v = A.get("dbg_v")
    dbg_q = A.get("dbg_q")
    dbg_oacc = A.get("dbg_oacc")
    dbg_s = A.get("dbg_s")
    dbg_z = A.get("dbg_z")
    sp_xn2 = A["sp_xn2"]
    sp_s = A["sp_s"]
    bias_sc = A["bias_sc"]
    ck_in = A["ck_in"]
    ck_out = A["ck_out"]
    cv_in = A["cv_in"]
    cv_out = A["cv_out"]
    _sl1 = contextlib.ExitStack()
    with _sl1:
        ident_bf = _sl1.enter_context(nc.sbuf_tensor("ident_bf_y", [P, P], BF16))
        ccol = _sl1.enter_context(nc.sbuf_tensor("ccol_y", [P, 8], F32))
        ones_row = _sl1.enter_context(nc.sbuf_tensor("ones_row_y", [1, P], F32))
        c_tm = _sl1.enter_context(nc.sbuf_tensor("c_tm_y", [P, NTC * NEXP], F32))
        murow2 = _sl1.enter_context(nc.sbuf_tensor("murow2_y", [1, T], F32))
        rstdrow2 = _sl1.enter_context(nc.sbuf_tensor("rstdrow2_y", [1, T], F32))
        bcols = _sl1.enter_context(nc.sbuf_tensor("bcols_y", [P, 4 * PO], F32))
        rbs = _sl1.enter_context(nc.sbuf_tensor("rbs_y", [P, 16], F32))
        ps_j = _sl1.enter_context(nc.psum_tensor("ps_j_y", [1, PO], F32))
        ps_j2 = _sl1.enter_context(nc.psum_tensor("ps_j2_y", [P, 1], F32))

        NEG1 = ccol[:, 0:1]
        EPSL = ccol[:, 2:3]
        EPSQ = ccol[:, 3:4]
        ONEC = ccol[:, 4:5]
        NEG8 = ccol[:, 5:6]
        JBIAS = ccol[:, 6:7]

        p = Prog(nc, sems, cnt)
        p.op("gpsimd", lambda g: g.memset(ident_bf[:], 0.0), inc="pool")
        p.op("gpsimd", lambda g: g.affine_select(
            out=ident_bf[:], in_=ident_bf[:], compare_op=ALU.not_equal,
            fill=1.0, base=0, pattern=[[-1, P]], channel_multiplier=1),
            waits=[("pool", p.c("pool"))], inc="pool")
        for (i, v) in ((0, -1.0), (2, 1e-5), (3, 1e-6), (4, 1.0), (5, -8.0)):
            p.op("gpsimd", (lambda g, i=i, v=v: g.memset(ccol[:, i:i + 1], v)))
        p.op("gpsimd", lambda g: g.memset(ones_row[:], 1.0))
        p.dma("sync", rbs[:], rb.partition_broadcast(P)[:, 0, :])
        jw = p.dma("sync", bcols[:, 0:PO], jcol)
        aj = p.op("scalar", lambda e: e.activation(
            out=bcols[:, PO:2 * PO], in_=bcols[:, 0:PO], func=AF.Sin),
            waits=[("d_sync", jw), ("pool", p.c("pool"))], inc="act")
        mj = p.op("tensor", lambda e: e.matmul(
            ps_j[:], ccol[:, 4:5], bcols[:, PO:2 * PO], start=True, stop=True),
            waits=[("act", aj)], inc="pe")
        dj = p.op("vector", lambda e: e.reduce_sum(
            out=ccol[0:1, 6:7], in_=ps_j[:], axis=AX.X), waits=[("pe", mj)], inc="dve")
        dj = p.op("vector", lambda e: e.tensor_scalar(
            out=ccol[0:1, 6:7], in0=ccol[0:1, 6:7], scalar1=0.1 / 1024.0, scalar2=-8.0,
            op0=ALU.mult, op1=ALU.add), waits=[("dve", dj)], inc="dve")
        mj = p.op("tensor", lambda e: e.matmul(
            ps_j2[:], ones_row[:], ccol[0:1, 6:7], start=True, stop=True),
            waits=[("dve", dj)], inc="pe")
        p.op("vector", (lambda e: e.tensor_copy(out=ccol[:, 6:7], in_=ps_j2[:])),
             waits=[("pe", mj)], inc="dve")
        p.emit("ph5" in PHASES_L1)

        _s5 = contextlib.ExitStack()
        with _s5:
            brow = _s5.enter_context(nc.sbuf_tensor("brow_y", [1, 4 * 256], F32))
            bcomb = _s5.enter_context(nc.sbuf_tensor("bcomb_y", [1, D], F32))
            p = Prog(nc, sems, cnt)
            for (pi, bsrc) in enumerate((qb_r, kb_r, vb_r, ob_r)):
                ld = p.dma("sync", brow[:], bsrc, waits=[("d_scalar", p.c("d_scalar"))])
                p.nop("vector", waits=[("d_sync", ld)])
                for comp in range(4):
                    for ci in range(4):
                        w, sg = QUAT_TAB[(ci, comp)]
                        if ci == 0:
                            p.op("vector", (lambda e, comp=comp, w=w, sg=sg: e.tensor_scalar(
                                out=bcomb[0:1, comp * 256:(comp + 1) * 256],
                                in0=brow[0:1, w * 256:(w + 1) * 256],
                                scalar1=sg, scalar2=None, op0=ALU.mult)))
                        elif sg > 0:
                            p.op("vector", (lambda e, comp=comp, w=w: e.tensor_tensor(
                                out=bcomb[0:1, comp * 256:(comp + 1) * 256],
                                in0=bcomb[0:1, comp * 256:(comp + 1) * 256],
                                in1=brow[0:1, w * 256:(w + 1) * 256], op=ALU.add)))
                        else:
                            p.op("vector", (lambda e, comp=comp, w=w: e.tensor_tensor(
                                out=bcomb[0:1, comp * 256:(comp + 1) * 256],
                                in0=bcomb[0:1, comp * 256:(comp + 1) * 256],
                                in1=brow[0:1, w * 256:(w + 1) * 256], op=ALU.subtract)))
                dv = p.mark("vector", "dve")
                p.dma("scalar", bias_sc[pi].rearrange("po q -> (po q)").unsqueeze(0), bcomb[:],
                      waits=[("dve", dv)])
            sw = p.c("d_scalar")
            ld = p.dma("sync", bcols[:],
                       bias_sc.rearrange("w po q -> q (w po)"),
                       waits=[("d_scalar", sw)], allow_slow_non_contiguous=True)
            p.emit("ph5" in PHASES_L1)

        _s5b = contextlib.ExitStack()
        with _s5b:
            xbuf = _s5b.enter_context(nc.sbuf_tensor("xbuf5_y", [P, PO * TS5], F32))
            xsq = _s5b.enter_context(nc.sbuf_tensor("xsq5_y", [P, TS5], F32))
            B1 = _s5b.enter_context(nc.sbuf_tensor("B15_y", [P, TS5], F32))
            B2 = _s5b.enter_context(nc.sbuf_tensor("B25_y", [P, TS5], F32))
            t1f = _s5b.enter_context(nc.sbuf_tensor("t1f5_y", [P, TS5], F32))
            xn2t = _s5b.enter_context(nc.sbuf_tensor("xn2t_y", [P, PO * TS5], BF16))
            rwsb = _s5b.enter_context(nc.sbuf_tensor("rwsb_y", [P, PO * 16], BF16))
            logit = _s5b.enter_context(nc.sbuf_tensor("logit_y", [P, 16], F32))
            pwork = _s5b.enter_context(nc.sbuf_tensor("pwork_y", [P, 16], F32))
            macc = _s5b.enter_context(nc.sbuf_tensor("macc_y", [P, 16], F32))
            eqt = _s5b.enter_context(nc.sbuf_tensor("eqt_y", [P, 16], F32))
            eqt2 = _s5b.enter_context(nc.sbuf_tensor("eqt2_y", [P, 16], F32))
            ones16 = _s5b.enter_context(nc.sbuf_tensor("ones16_y", [P, 16], F32))
            col1 = _s5b.enter_context(nc.sbuf_tensor("col1_y", [P, 1], F32))
            col2 = _s5b.enter_context(nc.sbuf_tensor("col2_y", [P, 1], F32))
            rowt = _s5b.enter_context(nc.sbuf_tensor("rowt5_y", [1, TS5], F32))
            rowt2 = _s5b.enter_context(nc.sbuf_tensor("rowt5b_y", [1, TS5], F32))
            ps_a = _s5b.enter_context(nc.psum_tensor("ps_a5_y", [P, TS5], F32))
            ps_s1 = _s5b.enter_context(nc.psum_tensor("ps_s15_y", [1, TS5], F32))
            ps_s2 = _s5b.enter_context(nc.psum_tensor("ps_s25_y", [1, TS5], F32))
            ps_r = _s5b.enter_context(nc.psum_tensor("ps_r5_y", [P, 16], F32))

            p = Prog(nc, sems, cnt)
            p.dma("gpsimd", rwsb.ap().rearrange("p (po e) -> p po e", po=PO),
                   rw.rearrange("(po p) e -> p po e", p=P))
            p.op("gpsimd", lambda g: g.memset(ones16[:], 1.0), inc="pool")
            wld = p.c("d_gpsimd")
            for t in range(NT5):
                t0 = t * TS5
                guard = [("pe", p.c("pe")), ("act", p.c("act")), ("dve", p.c("dve"))]
                xw = p.dma("sync", po_view(xbuf, TS5), po_view(x1_fm, T)[:, :, t0:t0 + TS5],
                           waits=guard)
                for po in range(PO):
                    p.op("tensor", (lambda e, po=po: e.matmul(
                        ps_s1[:], ONEC, xbuf[:, po * TS5:(po + 1) * TS5],
                        start=(po == 0), stop=(po == PO - 1))),
                        waits=([("d_sync", xw)] if po == 0 else ()),
                        inc=("pe" if po == PO - 1 else None))
                for po in range(PO):
                    aw = p.op("scalar", (lambda e, po=po: e.activation(
                        out=xsq[:], in_=xbuf[:, po * TS5:(po + 1) * TS5], func=AF.Square)),
                        waits=([("d_sync", xw), ("pe", guard[0][1])] if po == 0
                               else [("pe", p.c("pe"))]),
                        inc="act")
                    p.op("tensor", (lambda e, po=po: e.matmul(
                        ps_s2[:], ONEC, xsq[:],
                        start=(po == 0), stop=(po == PO - 1))),
                        waits=[("act", aw)], inc="pe")
                sq_done = p.c("pe")
                p.op("vector", (lambda e, t0=t0: e.tensor_scalar(
                    out=murow2[0:1, t0:t0 + TS5], in0=ps_s1[:], scalar1=1.0 / 1024.0,
                    scalar2=None, op0=ALU.mult)), waits=[("pe", sq_done)])
                p.op("vector", lambda e: e.tensor_scalar(
                    out=rowt[:], in0=ps_s2[:], scalar1=1.0 / 1024.0, scalar2=None, op0=ALU.mult))
                p.op("vector", (lambda e, t0=t0: e.tensor_tensor(
                    out=rowt2[:], in0=murow2[0:1, t0:t0 + TS5], in1=murow2[0:1, t0:t0 + TS5],
                    op=ALU.mult)))
                dv = p.op("vector", lambda e: e.tensor_tensor(
                    out=rowt[:], in0=rowt[:], in1=rowt2[:], op=ALU.subtract), inc="dve")
                av = p.op("scalar", lambda e: e.activation(
                    out=rowt2[:], in_=rowt[:], func=AF.Sqrt, bias=EPSL[0:1, :], scale=1.0),
                    waits=[("dve", dv)], inc="act")
                rcp = p.op("vector", (lambda e, t0=t0: e.reciprocal(
                    out=rstdrow2[0:1, t0:t0 + TS5], in_=rowt2[:])),
                    waits=[("act", av)], inc="dve")
                dv = p.op("vector", (lambda e, t0=t0: e.tensor_tensor(
                    out=rowt2[:], in0=murow2[0:1, t0:t0 + TS5],
                    in1=rstdrow2[0:1, t0:t0 + TS5], op=ALU.mult)),
                    waits=[("dve", rcp)], inc="dve")
                bm = p.op("tensor", (lambda e, t0=t0: e.matmul(
                    ps_a[:], ones_row[:], rstdrow2[0:1, t0:t0 + TS5], start=True, stop=True)),
                    waits=[("dve", dv)], inc="pe")
                ev = p.op("vector", lambda e: e.tensor_copy(out=B1[:], in_=ps_a[:]),
                          waits=[("pe", bm)], inc="dve")
                bm = p.op("tensor", lambda e: e.matmul(
                    ps_a[:], ones_row[:], rowt2[:], start=True, stop=True),
                    waits=[("dve", ev)], inc="pe")
                ev = p.op("vector", lambda e: e.tensor_copy(out=B2[:], in_=ps_a[:]),
                          waits=[("pe", bm)], inc="dve")
                for po in range(PO):
                    p.op("vector", (lambda e, po=po: e.tensor_tensor(
                        out=t1f[:], in0=xbuf[:, po * TS5:(po + 1) * TS5], in1=B1[:],
                        op=ALU.mult)))
                    p.op("vector", (lambda e, po=po: e.tensor_tensor(
                        out=xn2t[:, po * TS5:(po + 1) * TS5], in0=t1f[:], in1=B2[:],
                        op=ALU.subtract)))
                xn_done = p.mark("vector", "dve")
                sw = p.dma("scalar", po_view(sp_xn2.ap(), T)[:, :, t0:t0 + TS5],
                           po_view(xn2t, TS5), waits=[("dve", xn_done)])
                for tc in range(TS5 // P):
                    tg = (t0 + tc * P) // P
                    for kc in range(PO):
                        p.op("tensor", (lambda e, kc=kc, tc=tc: e.matmul(
                            ps_r[:], xn2t[:, kc * TS5 + tc * P: kc * TS5 + (tc + 1) * P],
                            rwsb[:, kc * 16:(kc + 1) * 16],
                            start=(kc == 0), stop=(kc == PO - 1))),
                            waits=([("dve", xn_done), ("d_gpsimd", wld),
                                    ("dve", p.c("dve"))] if kc == 0 else ()))
                    rm = p.mark("tensor", "pe")
                    dv0 = p.op("vector", lambda e: e.tensor_tensor(
                        out=logit[:], in0=ps_r[:], in1=rbs[:], op=ALU.add),
                        waits=[("pe", rm)], inc="dve")
                    rmx = p.op("vector", lambda e: e.reduce_max(col1[:], logit[:], axis=AX.X),
                               inc="dve")
                    p.op("vector", lambda e: e.tensor_scalar(
                        out=col1[:], in0=col1[:], scalar1=-1.0, scalar2=None, op0=ALU.mult),
                        waits=[("dve", rmx)])
                    dv1 = p.mark("vector", "dve")
                    a1 = p.op("scalar", lambda e: e.activation(
                        out=pwork[:], in_=logit[:], func=AF.Exp, bias=col1[:, 0:1], scale=1.0),
                        waits=[("dve", dv1)], inc="act")
                    p.op("vector", lambda e: e.tensor_copy(out=logit[:], in_=pwork[:]),
                         waits=[("act", a1)])
                    p.op("gpsimd", lambda e: e.memset(macc[:], 0.0),
                         waits=[("dve", p.c("dve"))], inc="pool")
                    p.op("gpsimd", lambda e: e.memset(col2[:], 0.0), inc="pool")
                    p.nop("vector", waits=[("pool", p.c("pool"))])
                    for it in range(4):
                        rmx = p.op("vector", lambda e: e.reduce_max(col1[:], pwork[:], axis=AX.X),
                                   inc="dve")
                        p.op("vector", lambda e: e.tensor_tensor(
                            out=col2[:], in0=col2[:], in1=col1[:], op=ALU.add),
                            waits=[("dve", rmx)])
                        p.op("vector", lambda e: e.tensor_scalar(
                            out=eqt[:], in0=ones16[:], scalar1=col1[:, 0:1],
                            scalar2=None, op0=ALU.mult))
                        p.op("vector", lambda e: e.tensor_tensor(
                            out=eqt[:], in0=pwork[:], in1=eqt[:], op=ALU.subtract))
                        dsg = p.mark("vector", "dve")
                        asg = p.op("scalar", lambda e: e.activation(
                            out=eqt2[:], in_=eqt[:], func=AF.Sign),
                            waits=[("dve", dsg)], inc="act")
                        p.op("vector", lambda e: e.tensor_scalar(
                            out=eqt2[:], in0=eqt2[:], scalar1=1.0, scalar2=None, op0=ALU.add),
                            waits=[("act", asg)])
                        p.op("vector", lambda e: e.tensor_tensor(
                            out=macc[:], in0=macc[:], in1=eqt2[:], op=ALU.max))
                        p.op("vector", lambda e: e.scalar_tensor_tensor(
                            out=pwork[:], in0=eqt2[:], scalar=-1e30, in1=pwork[:],
                            op0=ALU.mult, op1=ALU.add))
                    rcp = p.op("vector", lambda e: e.reciprocal(col1[:], col2[:]), inc="dve")
                    p.op("vector", lambda e: e.tensor_tensor(
                        out=logit[:], in0=logit[:], in1=macc[:], op=ALU.mult),
                        waits=[("dve", rcp)])
                    p.op("vector", (lambda e, tg=tg: e.tensor_scalar(
                        out=c_tm[:, tg * NEXP:(tg + 1) * NEXP], in0=logit[:],
                        scalar1=col1[:, 0:1], scalar2=None, op0=ALU.mult)), inc="dve")
                p.nop("scalar", waits=[("d_scalar", p.c("d_scalar"))])
            p.emit("ph5" in PHASES_L1)

        _s6 = contextlib.ExitStack()
        with _s6:
            q_fm = _s6.enter_context(nc.sbuf_tensor("q_fm_y", [P, PO * T], BF16))
            _s6i = contextlib.ExitStack()
            wqb = _s6i.enter_context(nc.sbuf_tensor("wqb_y", [P, PO * D], BF16))
            wkb = _s6i.enter_context(nc.sbuf_tensor("wkb_y", [P, PO * D], BF16))
            wvb = _s6i.enter_context(nc.sbuf_tensor("wvb_y", [P, PO * D], BF16))
            qraw = _s6i.enter_context(nc.sbuf_tensor("qraw_y", [P, 8 * 256], BF16))
            k_fm = _s6i.enter_context(nc.sbuf_tensor("k_fm_y", [P, PO * T], BF16))
            vtc = _s6i.enter_context(nc.sbuf_tensor("vtc_y", [P, NTC * NH * 65], BF16))
            xn2t6 = _s6i.enter_context(nc.sbuf_tensor("xn2t6_y", [P, PO * TS5], BF16))
            pre = _s6i.enter_context(nc.sbuf_tensor("pre_y", [P, PO * TS5], F32))
            sqt = _s6i.enter_context(nc.sbuf_tensor("sqt_y", [P, TS5], F32))
            nsq = _s6i.enter_context(nc.sbuf_tensor("nsq_y", [P, 2 * TS5], F32))
            nsq2 = _s6i.enter_context(nc.sbuf_tensor("nsq2_y", [P, 2 * TS5], F32))
            ps_a = _s6i.enter_context(nc.psum_tensor("ps_a6_y", [P, TS5], F32))
            ps_t = _s6i.enter_context(nc.psum_tensor("ps_t6_y", [P, P], BF16))
            ps_t2 = _s6i.enter_context(nc.psum_tensor("ps_t6b_y", [P, P], BF16))

            p = Prog(nc, sems, cnt)
            for (wsrc, wdst) in ((qw_r, wqb), (kw_r, wkb), (vw_r, wvb)):
                lw = p.dma("gpsimd", qraw[:], wsrc, waits=[("dve", p.c("dve"))])
                p.nop("vector", waits=[("d_gpsimd", lw)])
                for kc in range(PO):
                    ci, hc = kc // 2, kc % 2
                    for comp in range(4):
                        w, sg = QUAT_TAB[(ci, comp)]
                        p.op("vector", (lambda e, kc=kc, comp=comp, w=w, sg=sg, hc=hc,
                                        wdst=wdst: e.tensor_scalar(
                            out=wdst[:, kc * D + comp * 256: kc * D + (comp + 1) * 256],
                            in0=qraw[:, (w * 2 + hc) * 256:(w * 2 + hc + 1) * 256],
                            scalar1=sg, scalar2=None, op0=ALU.mult)))
                p.mark("vector", "dve")
            p.op("gpsimd", lambda e: e.memset(
                vtc.ap().rearrange("p (tc h o) -> p tc h o", tc=NTC, h=NH)[:, :, :, 64:65], 1.0),
                inc="pool")
            wasm = p.c("dve")
            for t in range(NT5):
                t0 = t * TS5
                guard = [("pe", p.c("pe")), ("act", p.c("act")), ("dve", max(p.c("dve"), wasm))]
                lx = p.dma("sync", po_view(xn2t6, TS5), po_view(sp_xn2.ap(), T)[:, :, t0:t0 + TS5],
                           waits=guard)
                for (pi, wdst, dst) in ((0, wqb, q_fm), (1, wkb, k_fm), (2, wvb, None)):
                    for po in range(PO):
                        for kc in range(PO):
                            p.op("tensor", (lambda e, wdst=wdst, po=po, kc=kc: e.matmul(
                                ps_a[:], wdst[:, kc * D + po * P: kc * D + (po + 1) * P],
                                xn2t6[:, kc * TS5:(kc + 1) * TS5],
                                start=(kc == 0), stop=(kc == PO - 1))),
                                waits=([("d_sync", lx), ("dve", max(wasm, p.c("dve"))),
                                        ("act", p.c("act"))]
                                       if po == 0 and kc == 0 else ()))
                        mmv = p.mark("tensor", "pe")
                        aw = p.op("scalar", (lambda e, pi=pi, po=po: e.activation(
                            out=pre[:, po * TS5:(po + 1) * TS5], in_=ps_a[:], func=AF.Identity,
                            bias=bcols[:, pi * PO + po: pi * PO + po + 1], scale=1.0)),
                            waits=[("pe", mmv)], inc="act")
                        p.nop("tensor", waits=[("act", aw)])
                    pre_done = p.c("act")
                    if pi < 2:
                        for half in range(2):
                            for ci2 in range(4):
                                po = 2 * ci2 + half
                                aw = p.op("scalar", (lambda e, po=po: e.activation(
                                    out=sqt[:], in_=pre[:, po * TS5:(po + 1) * TS5],
                                    func=AF.Square)),
                                    waits=([("act", pre_done), ("dve", p.c("dve"))]
                                           if half == 0 and ci2 == 0
                                           else [("dve", p.c("dve"))]), inc="act")
                                if ci2 == 0:
                                    p.op("vector", (lambda e, half=half: e.tensor_copy(
                                        out=nsq[:, half * TS5:(half + 1) * TS5], in_=sqt[:])),
                                        waits=[("act", aw)], inc="dve")
                                else:
                                    p.op("vector", (lambda e, half=half: e.tensor_tensor(
                                        out=nsq[:, half * TS5:(half + 1) * TS5],
                                        in0=nsq[:, half * TS5:(half + 1) * TS5],
                                        in1=sqt[:], op=ALU.add)),
                                        waits=[("act", aw)], inc="dve")
                        dvn = p.c("dve")
                        a2 = p.op("scalar", lambda e: e.activation(
                            out=nsq2[:], in_=nsq[:], func=AF.Sqrt, bias=EPSQ, scale=1.0),
                            waits=[("dve", dvn)], inc="act")
                        dv2 = p.op("vector", lambda e: e.reciprocal(out=nsq[:], in_=nsq2[:]),
                                   waits=[("act", a2)], inc="dve")
                        if pi == 0:
                            dv2 = p.op("vector", lambda e: e.tensor_scalar(
                                out=nsq[:], in0=nsq[:], scalar1=0.125, scalar2=None,
                                op0=ALU.mult), waits=[("dve", dv2)], inc="dve")
                        for po in range(PO):
                            p.op("vector", (lambda e, po=po, t0=t0, dst=dst: e.tensor_tensor(
                                out=dst[:, po * T + t0: po * T + t0 + TS5],
                                in0=pre[:, po * TS5:(po + 1) * TS5],
                                in1=nsq[:, (po % 2) * TS5:((po % 2) + 1) * TS5],
                                op=ALU.mult)),
                                waits=([("dve", dv2)] if po == 0 else ()))
                        p.mark("vector", "dve")
                    else:
                        for po in range(PO):
                            for tc2 in range(TS5 // P):
                                pst = ps_t if (po * 4 + tc2) % 2 == 0 else ps_t2
                                aw = p.op("scalar", (lambda e, po=po, tc2=tc2: e.activation(
                                    out=sqt.ap().bitcast(BF16)[:, 0:P],
                                    in_=pre[:, po * TS5 + tc2 * P: po * TS5 + (tc2 + 1) * P],
                                    func=AF.Copy)),
                                    waits=[("act", pre_done), ("pe", p.c("pe"))], inc="act")
                                mm = p.op("tensor", (lambda e, pst=pst: e.transpose(
                                    pst[:], sqt.ap().bitcast(BF16)[:, 0:P], ident_bf[:])),
                                    waits=[("act", aw)], inc="pe")
                                tcg = t * (TS5 // P) + tc2
                                p.op("vector", (lambda e, po=po, tcg=tcg, pst=pst: e.tensor_copy(
                                    out=vtc.ap().rearrange(
                                        "p (tc h o) -> p tc h o", tc=NTC, h=NH)[
                                        :, tcg, 2 * po: 2 * po + 2, 0:64],
                                    in_=pst.ap().rearrange("p (a b) -> p a b", a=2))),
                                    waits=[("pe", mm)], inc="dve")
            p.emit("ph6" in PHASES_L1)

            p = Prog(nc, sems, cnt)
            dv = p.c("dve")
            p.dma("gpsimd", ck_in.ap(), k_fm[:], waits=[("dve", dv), ("pool", p.c("pool"))])
            p.dma("gpsimd", cv_in.ap(), vtc[:])
            ccw = p.c("d_gpsimd")
            p.op("gpsimd", lambda e: e.collective_compute(
                "AllGather", ALU.bypass,
                replica_groups=[[0, 1, 2, 3], [4, 5, 6, 7]],
                ins=[ck_in.ap().opt()], outs=[ck_out.ap().opt()]),
                waits=[("d_gpsimd", ccw)], inc="cc")
            p.op("gpsimd", lambda e: e.collective_compute(
                "AllGather", ALU.bypass,
                replica_groups=[[0, 1, 2, 3], [4, 5, 6, 7]],
                ins=[cv_in.ap().opt()], outs=[cv_out.ap().opt()]),
                inc="cc")
            p.emit("ph6" in PHASES_L1)
            _s6i.close()

            _s7 = contextlib.ExitStack()
            with _s7:
                k_sb = _s7.enter_context(nc.sbuf_tensor("k_sb_y", [P, PO * T], BF16))
                v_sb = _s7.enter_context(nc.sbuf_tensor("v_sb_y", [P, NTC * NH * 65], BF16))
                oacc = _s7.enter_context(nc.sbuf_tensor("oacc_y", [P, NH * 4 * 4 * 65], F32))
                probs = _s7.enter_context(nc.sbuf_tensor("probs_y", [P, 2 * 512], BF16))
                lcol = _s7.enter_context(nc.sbuf_tensor("lcol_y", [P, 1], F32))
                ps_sc1 = _s7.enter_context(nc.psum_tensor("ps_sc1_y", [P, 512], F32))
                ps_sc2 = _s7.enter_context(nc.psum_tensor("ps_sc2_y", [P, 512], F32))
                ps_o1 = _s7.enter_context(nc.psum_tensor("ps_o1_y", [P, 4 * 65], F32))
                ps_o2 = _s7.enter_context(nc.psum_tensor("ps_o2_y", [P, 4 * 65], F32))
                QB = 512

                p = Prog(nc, sems, cnt)
                p.nop("sync", waits=[("cc", p.c("cc"))])
                exp_log = []
                for r in range(4):
                    lk = p.dma("sync", k_sb[:], ck_out[r * P:(r + 1) * P, :].opt(),
                               waits=([("cc", p.c("cc"))] if r == 0 else
                                      [("pe", p.c("pe")), ("act", p.c("act"))]))
                    lv = p.dma("sync", v_sb[:], cv_out[r * P:(r + 1) * P, :].opt())
                    lkv = p.c("d_sync")
                    for h in range(NH):
                        hb = (h % 2) * HD
                        hp = h // 2
                        for qb in range(4):
                            exps = []
                            for kt in range(NTC):
                                pss = ps_sc1 if kt % 2 == 0 else ps_sc2
                                wts = [("act", p.c("act"))] if len(exp_log) >= 2 else []
                                if kt == 0 and h == 0 and qb == 0:
                                    wts.append(("d_sync", lkv))
                                if kt == 0:
                                    wts.append(("dve", p.c("dve")))
                                mm = p.op("tensor", (lambda e, hb=hb, hp=hp, kt=kt, qb=qb,
                                                     pss=pss: e.matmul(
                                    pss[:], k_sb[hb:hb + HD, hp * T + kt * P: hp * T + (kt + 1) * P],
                                    q_fm[hb:hb + HD, hp * T + qb * QB: hp * T + (qb + 1) * QB],
                                    start=True, stop=True)),
                                    waits=([("act", exp_log[-2])] if len(exp_log) >= 2 else
                                           wts),
                                    inc="pe")
                                pb = probs[:, (kt % 2) * 512:(kt % 2 + 1) * 512]
                                a1 = p.op("scalar", (lambda e, pss=pss, pb=pb: e.activation(
                                    out=pb, in_=pss[:], func=AF.Exp, bias=JBIAS, scale=1.0)),
                                    waits=[("pe", mm)], inc="act")
                                exp_log.append(a1)
                                pso = ps_o1 if (h * 4 + qb) % 2 == 0 else ps_o2
                                for qt in range(4):
                                    p.op("tensor", (lambda e, pb=pb, qt=qt, kt=kt, h=h,
                                                    pso=pso: e.matmul(
                                        pso[:, qt * 65:(qt + 1) * 65],
                                        pb[:, qt * P:(qt + 1) * P],
                                        v_sb[:, (kt * NH + h) * 65:(kt * NH + h + 1) * 65],
                                        start=(kt == 0 and qt == 0),
                                        stop=(kt == NTC - 1 and qt == 3))),
                                        waits=([("act", a1)] if qt == 0 else ()),
                                        inc=("pe" if kt == NTC - 1 and qt == 3 else None))
                            pvd = p.c("pe")
                            if r == 0:
                                p.op("vector", (lambda e, h=h, qb=qb, pso=pso: e.tensor_copy(
                                    out=oacc[:, (h * 4 + qb) * 260:(h * 4 + qb + 1) * 260],
                                    in_=pso[:])), waits=[("pe", pvd)], inc="dve")
                            else:
                                p.op("vector", (lambda e, h=h, qb=qb, pso=pso: e.tensor_tensor(
                                    out=oacc[:, (h * 4 + qb) * 260:(h * 4 + qb + 1) * 260],
                                    in0=oacc[:, (h * 4 + qb) * 260:(h * 4 + qb + 1) * 260],
                                    in1=pso[:], op=ALU.add)), waits=[("pe", pvd)], inc="dve")
                p.emit("ph7" in PHASES_L1)

                p = Prog(nc, sems, cnt)
                if DEBUG_L1:
                    p.dma("scalar", dbg_oacc, oacc[:], waits=[("dve", p.c("dve"))])
                    p.dma("scalar", dbg_xn2, sp_xn2.ap())
                    p.dma("scalar", dbg_ctm, c_tm[:])
                    p.dma("scalar", dbg_k, ck_in.ap())
                    p.dma("scalar", dbg_v, cv_in.ap())
                    p.nop("sync", waits=[("d_scalar", p.c("d_scalar"))])
                for h in range(NH):
                    for tcg in range(NTC):
                        qb, qt = tcg // 4, tcg % 4
                        base = (h * 4 + qb) * 260 + qt * 65
                        rcp = p.op("vector", (lambda e, base=base: e.reciprocal(
                            out=lcol[:], in_=oacc[:, base + 64: base + 65])), inc="dve")
                        p.op("vector", (lambda e, base=base, tcg=tcg, h=h: e.tensor_scalar(
                            out=q_fm.ap().bitcast(BF16)[:, tcg * D + h * HD: tcg * D + (h + 1) * HD]
                            if False else
                            q_fm[:, tcg * D + h * HD: tcg * D + (h + 1) * HD],
                            in0=oacc[:, base: base + 64],
                            scalar1=lcol[:, 0:1], scalar2=None, op0=ALU.mult)),
                            waits=[("dve", rcp)])
                p.mark("vector", "dve")
                p.emit("ph7" in PHASES_L1)
            p.emit if False else None

            _s7b = contextlib.ExitStack()
            with _s7b:
                afm = _s7b.enter_context(nc.sbuf_tensor("afm_y", [P, PO * T], BF16))
                wob = _s7b.enter_context(nc.sbuf_tensor("wob_y", [P, PO * D], BF16))
                qraw2 = _s7b.enter_context(nc.sbuf_tensor("qraw2_y", [P, 8 * 256], BF16))
                sfm = _s7b.enter_context(nc.sbuf_tensor("sfm_y", [P, PO * TS5], BF16))
                ps_t7 = _s7b.enter_context(nc.psum_tensor("ps_t7_y", [P, P], BF16))
                ps_t7b = _s7b.enter_context(nc.psum_tensor("ps_t7b_y", [P, P], BF16))
                ps_a7 = _s7b.enter_context(nc.psum_tensor("ps_a7_y", [P, TS5], F32))

                p = Prog(nc, sems, cnt)
                if DEBUG_L1:
                    p.dma("scalar", dbg_q, q_fm[:], waits=[("dve", p.c("dve"))])
                lw = p.dma("gpsimd", qraw2[:], ow_r)
                p.nop("vector", waits=[("d_gpsimd", p.c("d_gpsimd"))])
                for kc in range(PO):
                    ci, hc = kc // 2, kc % 2
                    for comp in range(4):
                        w, sg = QUAT_TAB[(ci, comp)]
                        p.op("vector", (lambda e, kc=kc, comp=comp, w=w, sg=sg, hc=hc:
                                        e.tensor_scalar(
                            out=wob[:, kc * D + comp * 256: kc * D + (comp + 1) * 256],
                            in0=qraw2[:, (w * 2 + hc) * 256:(w * 2 + hc + 1) * 256],
                            scalar1=sg, scalar2=None, op0=ALU.mult)))
                wdone = p.mark("vector", "dve")
                for tcg in range(NTC):
                    for j in range(PO):
                        pst = ps_t7 if (tcg * PO + j) % 2 == 0 else ps_t7b
                        mm = p.op("tensor", (lambda e, tcg=tcg, j=j, pst=pst: e.transpose(
                            pst[:], q_fm[:, tcg * D + j * P: tcg * D + (j + 1) * P],
                            ident_bf[:])),
                            waits=[("dve", p.c("dve"))], inc="pe")
                        p.op("vector", (lambda e, tcg=tcg, j=j, pst=pst: e.tensor_copy(
                            out=afm[:, j * T + tcg * P: j * T + (tcg + 1) * P], in_=pst[:])),
                            waits=[("pe", mm)], inc="dve")
                tr_done = p.c("dve")
                for t in range(NT5):
                    t0 = t * TS5
                    for po in range(PO):
                        for kc in range(PO):
                            p.op("tensor", (lambda e, po=po, kc=kc, t0=t0: e.matmul(
                                ps_a7[:], wob[:, kc * D + po * P: kc * D + (po + 1) * P],
                                afm[:, kc * T + t0: kc * T + t0 + TS5],
                                start=(kc == 0), stop=(kc == PO - 1))),
                                waits=([("dve", tr_done)] if t == 0 and po == 0 and kc == 0
                                       else ()))
                        mmv = p.mark("tensor", "pe")
                        aw = p.op("scalar", (lambda e, po=po: e.activation(
                            out=sfm[:, po * TS5:(po + 1) * TS5], in_=ps_a7[:],
                            func=AF.Identity, bias=bcols[:, 3 * PO + po: 3 * PO + po + 1],
                            scale=1.0)),
                            waits=[("pe", mmv), ("d_scalar", p.c("d_scalar"))], inc="act")
                        p.nop("tensor", waits=[("act", aw)])
                    p.dma("scalar", po_view(sp_s.ap(), T)[:, :, t0:t0 + TS5],
                          po_view(sfm, TS5), waits=[("act", p.c("act"))])
                p.nop("sync", waits=[("d_scalar", p.c("d_scalar"))])
                p.emit("ph7" in PHASES_L1)

        _s8 = contextlib.ExitStack()
        with _s8:
            zacc = _s8.enter_context(nc.sbuf_tensor("zacc8_y", [P, NTC * D], F32))
            _s8i = contextlib.ExitStack()
            w1e = _s8i.enter_context(nc.sbuf_tensor("w1e_y", [P, PO * MI], BF16))
            w2e = _s8i.enter_context(nc.sbuf_tensor("w2e_y", [P, (MI // P) * D], BF16))
            midb = _s8i.enter_context(nc.sbuf_tensor("midb_y", [P, (MI // P) * 512], BF16))
            xn2t8 = _s8i.enter_context(nc.sbuf_tensor("xn2t8_y", [P, PO * 512], BF16))
            b1c = _s8i.enter_context(nc.sbuf_tensor("b1c_y", [P, MI // P], F32))
            b2row = _s8i.enter_context(nc.sbuf_tensor("b2row_y", [1, D], F32))
            ps_f1 = _s8i.enter_context(nc.psum_tensor("ps_f18_y", [P, 512], F32))
            ps_z = [_s8i.enter_context(nc.psum_tensor(f"ps_z{i}", [P, 512], F32))
                    for i in range(4)]

            p = Prog(nc, sems, cnt)
            if DEBUG_L1:
                p.dma("scalar", dbg_s, sp_s.ap())
                p.nop("sync", waits=[("d_scalar", p.c("d_scalar"))])
            for e_i in range(NEXP):
                lw1 = p.dma("gpsimd", w1e.ap().rearrange("p (k n) -> p k n", k=PO),
                            w1m[e_i].rearrange("(k p) n -> p k n", p=P),
                            waits=[("pe", p.c("pe"))])
                lw2 = p.dma("gpsimd", w2e.ap().rearrange("p (k m) -> p k m", k=MI // P),
                            w2m[e_i].rearrange("(k p) m -> p k m", p=P))
                lb1 = p.dma("sync", b1c[:], b1m[e_i], waits=[("act", p.c("act"))])
                lb2 = p.dma("sync", b2row[:], b2m[0:1, e_i * D:(e_i + 1) * D],
                            waits=[("pe", p.c("pe"))])
                wld8 = p.c("d_gpsimd")
                for ts in range(4):
                    t0 = ts * 512
                    lx = p.dma("sync", po_view(xn2t8, 512),
                               po_view(sp_xn2.ap(), T)[:, :, t0:t0 + 512],
                               waits=[("pe", p.c("pe"))])
                    for nch in range(MI // P):
                        for kc in range(PO):
                            p.op("tensor", (lambda e, nch=nch, kc=kc: e.matmul(
                                ps_f1[:], w1e[:, kc * MI + nch * P: kc * MI + (nch + 1) * P],
                                xn2t8[:, kc * 512:(kc + 1) * 512],
                                start=(kc == 0), stop=(kc == PO - 1))),
                                waits=([("d_gpsimd", wld8), ("d_sync", p.c("d_sync")),
                                        ("act", p.c("act"))]
                                       if nch == 0 and kc == 0 else ()))
                        mmv = p.mark("tensor", "pe")
                        aw = p.op("scalar", (lambda e, nch=nch: e.activation(
                            out=midb[:, nch * 512:(nch + 1) * 512], in_=ps_f1[:], func=AF.Gelu,
                            bias=b1c[:, nch:nch + 1], scale=1.0)),
                            waits=[("pe", mmv), ("d_sync", p.c("d_sync"))], inc="act")
                        p.nop("tensor", waits=[("act", aw)])
                    mid_done = p.c("act")
                    for mh in range(2):
                        for t128 in range(4):
                            tg = (t0 + t128 * P) // P
                            psz = ps_z[t128]
                            p.op("tensor", (lambda e, psz=psz, mh=mh: e.matmul(
                                psz[:], ones_row[:],
                                b2row[0:1, mh * 512: mh * 512 + 512],
                                start=True, stop=False)),
                                waits=([("act", mid_done), ("dve", p.c("dve")),
                                        ("d_sync", p.c("d_sync"))]
                                       if mh == 0 and t128 == 0 else
                                       [("dve", p.c("dve"))]))
                            for nch in range(MI // P):
                                p.op("tensor", (lambda e, nch=nch, t128=t128, mh=mh,
                                                psz=psz: e.matmul(
                                    psz[:],
                                    midb[:, nch * 512 + t128 * P: nch * 512 + (t128 + 1) * P],
                                    w2e[:, nch * D + mh * 512: nch * D + mh * 512 + 512],
                                    start=False, stop=(nch == MI // P - 1))))
                            mmz = p.mark("tensor", "pe")
                            tg2 = ts * 4 + t128
                            if e_i == 0:
                                p.op("vector", (lambda e, tg2=tg2, mh=mh, psz=psz, e_i=e_i:
                                                e.tensor_scalar(
                                    out=zacc[:, tg2 * D + mh * 512: tg2 * D + mh * 512 + 512],
                                    in0=psz[:],
                                    scalar1=c_tm[:, tg2 * NEXP + e_i: tg2 * NEXP + e_i + 1],
                                    scalar2=None, op0=ALU.mult)),
                                    waits=[("pe", mmz)], inc="dve")
                            else:
                                p.op("vector", (lambda e, tg2=tg2, mh=mh, psz=psz, e_i=e_i:
                                                e.scalar_tensor_tensor(
                                    out=zacc[:, tg2 * D + mh * 512: tg2 * D + mh * 512 + 512],
                                    in0=psz[:],
                                    scalar=c_tm[:, tg2 * NEXP + e_i: tg2 * NEXP + e_i + 1],
                                    in1=zacc[:, tg2 * D + mh * 512: tg2 * D + mh * 512 + 512],
                                    op0=ALU.mult, op1=ALU.add)),
                                    waits=[("pe", mmz)], inc="dve")
            p.emit("ph8" in PHASES_L1)
            _s8i.close()

            _s9 = contextlib.ExitStack()
            with _s9:
                gz = _s9.enter_context(nc.sbuf_tensor("gz_y", [P, NTC * D], BF16))
                gfm = _s9.enter_context(nc.sbuf_tensor("gfm_y", [P, PO * T], BF16))
                sfm9 = _s9.enter_context(nc.sbuf_tensor("sfm9_y", [P, PO * TS5], BF16))
                x9 = _s9.enter_context(nc.sbuf_tensor("x9_y", [P, TS5], F32))
                o9 = _s9.enter_context(nc.sbuf_tensor("o9_y", [P, TS5], F32))
                g9 = _s9.enter_context(nc.sbuf_tensor("g9_y", [P, TS5], F32))
                ps_t9 = _s9.enter_context(nc.psum_tensor("ps_t9_y", [P, P], BF16))
                ps_t9b = _s9.enter_context(nc.psum_tensor("ps_t9b_y", [P, P], BF16))

                p = Prog(nc, sems, cnt)
                if DEBUG_L1:
                    p.dma("scalar", dbg_z, zacc[:])
                    p.nop("sync", waits=[("d_scalar", p.c("d_scalar"))])
                for tcg in range(NTC):
                    p.op("scalar", (lambda e, tcg=tcg: e.activation(
                        out=gz[:, tcg * D:(tcg + 1) * D], in_=zacc[:, tcg * D:(tcg + 1) * D],
                        func=AF.Silu)), inc="act")
                silu_done = p.c("act")
                for tcg in range(NTC):
                    for j in range(PO):
                        pst = ps_t9 if (tcg * PO + j) % 2 == 0 else ps_t9b
                        mm = p.op("tensor", (lambda e, tcg=tcg, j=j, pst=pst: e.transpose(
                            pst[:], gz[:, tcg * D + j * P: tcg * D + (j + 1) * P],
                            ident_bf[:])),
                            waits=([("act", silu_done)] if tcg == 0 and j == 0 else
                                   [("dve", p.c("dve"))]), inc="pe")
                        p.op("vector", (lambda e, tcg=tcg, j=j, pst=pst: e.tensor_copy(
                            out=gfm[:, j * T + tcg * P: j * T + (tcg + 1) * P], in_=pst[:])),
                            waits=[("pe", mm)], inc="dve")
                tr9 = p.c("dve")
                for t in range(NT5):
                    t0 = t * TS5
                    ls = p.dma("sync", po_view(sfm9, TS5),
                               po_view(sp_s.ap(), T)[:, :, t0:t0 + TS5],
                               waits=[("dve", p.c("dve"))])
                    for po in range(PO):
                        lx = p.dma("scalar", x9[:], po_view(x1_fm, T)[:, po, t0:t0 + TS5],
                                   waits=[("dve", p.c("dve"))])
                        a1 = p.op("scalar", (lambda e, po=po, t0=t0: e.activation(
                            out=g9[:], in_=gfm[:, po * T + t0: po * T + t0 + TS5],
                            func=AF.Copy)), waits=[("dve", tr9)], inc="act")
                        a2 = p.op("scalar", (lambda e, po=po: e.activation(
                            out=o9[:], in_=sfm9[:, po * TS5:(po + 1) * TS5], func=AF.Copy)),
                            waits=[("d_sync", ls)], inc="act")
                        p.op("vector", lambda e: e.tensor_tensor(
                            out=o9[:], in0=o9[:], in1=g9[:], op=ALU.mult),
                            waits=[("act", a2), ("d_scalar", lx)])
                        dv = p.op("vector", lambda e: e.tensor_tensor(
                            out=o9[:], in0=o9[:], in1=x9[:], op=ALU.add), inc="dve")
                        sw = p.dma("scalar", po_view(out_fm, T)[:, po, t0:t0 + TS5], o9[:],
                                   waits=[("dve", dv)])
                        p.nop("scalar", waits=[("d_scalar", sw)])
                        p.nop("vector", waits=[("d_scalar", sw)])
                p.nop("sync", waits=[("d_scalar", p.c("d_scalar"))])
                p.emit("ph9" in PHASES_L1)



def build_l1(detect_races=True):
    nc = bass.Bass("TRN2", target_bir_lowering=False, detect_race_conditions=detect_races)
    A = _decl_l1(nc)
    sems = {}
    cnt = {}
    stack = contextlib.ExitStack()
    for k in ("d_sync", "d_gpsimd", "d_scalar", "pe", "act", "dve", "pool", "cc"):
        sems[k] = stack.enter_context(nc.semaphore(k))
    with stack:
        _emit_l1(nc, sems, cnt, A)
    return nc


def build_full(detect_races=True):
    nc = bass.Bass("TRN2", target_bir_lowering=False, detect_race_conditions=detect_races)
    A0 = _decl_l0(nc, "Internal")
    A1 = _decl_l1(nc, x1_ap=A0["x1_out"])
    sems = {}
    cnt = {}
    stack = contextlib.ExitStack()
    for k in ("d_sync", "d_gpsimd", "d_scalar", "pe", "act", "dve", "pool", "cc"):
        sems[k] = stack.enter_context(nc.semaphore(k))
    with stack:
        _emit_l0(nc, sems, cnt, {"ph0", "ph1a", "ph1b", "ph2", "ph3a", "ph3b", "ph4"}, A0)
        _emit_l1(nc, sems, cnt, A1)
    return nc


PHASES_L1 = {"ph5", "ph6", "ph7", "ph8", "ph9"}
DEBUG_L1 = False


def prep_l1_inputs(inputs, x1_fm_per_core):
    def quat_w_layout(w):
        return np.ascontiguousarray(
            np.asarray(w).reshape(4, 2, P, 256).transpose(2, 0, 1, 3).reshape(P, 2048))

    base = {
        "qw_r": quat_w_layout(inputs["L1_attn_qW"]),
        "kw_r": quat_w_layout(inputs["L1_attn_kW"]),
        "vw_r": quat_w_layout(inputs["L1_attn_vW"]),
        "ow_r": quat_w_layout(inputs["L1_attn_oW"]),
        "qb_r": np.ascontiguousarray(np.asarray(inputs["L1_attn_qb"]).reshape(1, 1024)),
        "kb_r": np.ascontiguousarray(np.asarray(inputs["L1_attn_kb"]).reshape(1, 1024)),
        "vb_r": np.ascontiguousarray(np.asarray(inputs["L1_attn_vb"]).reshape(1, 1024)),
        "ob_r": np.ascontiguousarray(np.asarray(inputs["L1_attn_ob"]).reshape(1, 1024)),
        "rw": np.ascontiguousarray(np.asarray(inputs["L1_ffn_router_w"])),
        "jcol": np.ascontiguousarray(np.asarray(inputs["L1_attn_J"]).reshape(PO, P).T),
        "rb": np.ascontiguousarray(np.asarray(inputs["L1_ffn_router_b"]).reshape(1, 16)),
        "w1m": np.ascontiguousarray(np.asarray(inputs["L1_ffn_w1"])),
        "b1m": np.ascontiguousarray(
            np.asarray(inputs["L1_ffn_b1"]).reshape(NEXP, MI // P, P).transpose(0, 2, 1)),
        "w2m": np.ascontiguousarray(np.asarray(inputs["L1_ffn_w2"])),
        "b2m": np.ascontiguousarray(np.asarray(inputs["L1_ffn_b2"]).reshape(1, NEXP * D)),
    }
    maps = []
    for core in range(8):
        m = dict(base)
        if x1_fm_per_core[core] is not None:
            m["x1_fm"] = np.ascontiguousarray(x1_fm_per_core[core])
        maps.append(m)
    return maps


def kernel(**inputs):
    nc = build_full()
    maps0 = prep_l0_inputs(inputs)
    maps1 = prep_l1_inputs(inputs, [None] * 8)
    maps = []
    for c in range(8):
        m = dict(maps0[c])
        m.update({k: v for k, v in maps1[c].items() if k != "x1_fm"})
        maps.append(m)
    res = run_bass_kernel_spmd(nc, maps, core_ids=list(range(8)))
    out = np.zeros((2, 8192, 1024), np.float32)
    for core in range(8):
        b, r = core // 4, core % 4
        out[b, r * T:(r + 1) * T, :] = unfm_layout(res.results[core]["out_fm"])
    return out


if __name__ == "__main__":
    pass




# version 05
# speedup vs baseline: 1.1382x; 1.1361x over previous
import numpy as np
import contextlib
import concourse.bass as bass
import concourse.mybir as mybir
from concourse.bass_utils import run_bass_kernel_spmd

F32 = mybir.dt.float32
BF16 = mybir.dt.bfloat16
I32 = mybir.dt.int32
AF = mybir.ActivationFunctionType
ALU = mybir.AluOpType
AX = mybir.AxisListType

P = 128
D = 1024
PO = 8
T = 2048
TT = 256
NT = T // TT
CH = 128
CPT = TT // CH
NCH = T // CH
NH = 16
HD = 64
FF = 4096
NFC = FF // P
NEXP = 16
MI = 2048

ENGINES = ("sync", "scalar", "vector", "tensor", "gpsimd")

PH1A_PARTS = {"stats", "xn", "proj", "gk", "chunk", "spill"}


class Prog:

    def __init__(self, nc, sems, cnt):
        self.nc = nc
        self.sems = sems
        self.cnt = cnt
        self.steps = {e: [] for e in ENGINES}

    def c(self, key):
        return self.cnt.get(key, 0)

    def op(self, engine, fn, inc=None, n=1, waits=()):
        st = self.steps[engine]
        for (k, v) in waits:
            if v > 0:
                st.append(("w", k, v))
        if inc is not None:
            self.cnt[inc] = self.cnt.get(inc, 0) + n
            st.append(("o", fn, inc, n))
            return self.cnt[inc]
        st.append(("o", fn, None, 0))
        return None

    def dma(self, q, out, in_, waits=(), **kw):
        key = "d_" + q
        return self.op(
            q,
            (lambda e, o=out, i=in_, kw=dict(kw): e.dma_start(out=o, in_=i, **kw)),
            inc=key, n=16, waits=waits,
        )

    def nop(self, engine, waits=(), inc=None):
        assert inc is None
        for (k, v) in waits:
            if v > 0:
                self.steps[engine].append(("w", k, v))
        return None

    def mark(self, engine, inc, n=1):
        st = self.steps[engine]
        for i in range(len(st) - 1, -1, -1):
            if st[i][0] == "o":
                assert st[i][2] is None, "last op already carries an inc"
                self.cnt[inc] = self.cnt.get(inc, 0) + n
                st[i] = ("o", st[i][1], inc, n)
                return self.cnt[inc]
        raise AssertionError("no op to mark")

    def emit(self, enabled=True):
        if not enabled:
            return
        with self.nc.Block() as block:
            for e in ENGINES:
                steps = self.steps[e]
                if not steps:
                    continue

                def mk(steps=steps):
                    def body(eng):
                        for s in steps:
                            if s[0] == "w":
                                eng.wait_ge(self.sems[s[1]], s[2])
                            else:
                                ins = s[1](eng)
                                if s[2] is not None:
                                    ins.then_inc(self.sems[s[2]], s[3])
                    return body
                getattr(block, e)(mk())


def po_view(ap_or_handle, width):
    a = ap_or_handle if isinstance(ap_or_handle, bass.AP) else ap_or_handle.ap()
    return a.rearrange("p (po t) -> p po t", po=PO)



def _decl_l0(nc, x1_kind="ExternalOutput"):
    x_fm = nc.dram_tensor("x_fm", [P, PO * T], F32, kind="ExternalInput").ap()
    wq = nc.dram_tensor("wq", [P, PO * D], F32, kind="ExternalInput").ap()
    wk = nc.dram_tensor("wk", [P, PO * D], F32, kind="ExternalInput").ap()
    wv = nc.dram_tensor("wv", [P, PO * D], F32, kind="ExternalInput").ap()
    wg = nc.dram_tensor("wg", [P, PO * D], F32, kind="ExternalInput").ap()
    wout = nc.dram_tensor("wout", [P, PO * D], F32, kind="ExternalInput").ap()
    wgk1 = nc.dram_tensor("wgk1", [P, PO * 16], F32, kind="ExternalInput").ap()
    wgk2 = nc.dram_tensor("wgk2", [16, D], F32, kind="ExternalInput").ap()
    bgk2_col = nc.dram_tensor("bgk2_col", [P, PO], F32, kind="ExternalInput").ap()
    gamma_col = nc.dram_tensor("gamma_col", [P, PO], F32, kind="ExternalInput").ap()
    beta_col = nc.dram_tensor("beta_col", [P, PO], F32, kind="ExternalInput").ap()
    lng_col = nc.dram_tensor("lng_col", [P, PO], F32, kind="ExternalInput").ap()
    lnb_col = nc.dram_tensor("lnb_col", [P, PO], F32, kind="ExternalInput").ap()
    fc1b_col = nc.dram_tensor("fc1b_col", [P, NFC], F32, kind="ExternalInput").ap()
    fc2b_col = nc.dram_tensor("fc2b_col", [P, PO], F32, kind="ExternalInput").ap()
    fc1_q = nc.dram_tensor("fc1_q", [D, FF], I32, kind="ExternalInput").ap()
    fc1_sc = nc.dram_tensor("fc1_sc", [8, FF], F32, kind="ExternalInput").ap()
    fc2_q = nc.dram_tensor("fc2_q", [FF, D], I32, kind="ExternalInput").ap()
    fc2_sc = nc.dram_tensor("fc2_sc", [32, D], F32, kind="ExternalInput").ap()
    sbeta = nc.dram_tensor("sbeta", [P, 4], F32, kind="ExternalInput").ap()

    x1_out = nc.dram_tensor("x1_fm", [P, PO * T], F32, kind=x1_kind).ap()

    sp_kfm = nc.dram_tensor("sp_kfm", [NT, P, PO * TT], BF16)
    sp_qfm = nc.dram_tensor("sp_qfm", [NT, P, PO * TT], BF16)
    sp_gate = nc.dram_tensor("sp_gate", [NT, P, PO * TT], BF16)
    sp_An = nc.dram_tensor("sp_An", [NT, P, PO * TT], F32)
    sp_ktm = nc.dram_tensor("sp_ktm", [NT, P, CPT * D], BF16)
    sp_vt = nc.dram_tensor("sp_vt", [NT, P, CPT * D], BF16)
    w1bf = nc.dram_tensor("w1bf", [NFC, 8, P, P], BF16)
    w2bf = nc.dram_tensor("w2bf", [32, P, D], BF16)
    cc_in = nc.dram_tensor("cc_in", [P, PO * (HD + 1)], F32)
    cc_out = nc.dram_tensor("cc_out", [4 * P, PO * (HD + 1)], F32)

    return dict(x_fm=x_fm, wq=wq, wk=wk, wv=wv, wg=wg, wout=wout, wgk1=wgk1, wgk2=wgk2, bgk2_col=bgk2_col, gamma_col=gamma_col, beta_col=beta_col, lng_col=lng_col, lnb_col=lnb_col, fc1b_col=fc1b_col, fc2b_col=fc2b_col, fc1_q=fc1_q, fc1_sc=fc1_sc, fc2_q=fc2_q, fc2_sc=fc2_sc, sbeta=sbeta, x1_out=x1_out, sp_kfm=sp_kfm, sp_qfm=sp_qfm, sp_gate=sp_gate, sp_An=sp_An, sp_ktm=sp_ktm, sp_vt=sp_vt, w1bf=w1bf, w2bf=w2bf, cc_in=cc_in, cc_out=cc_out)


def _emit_l0(nc, sems, cnt, phases, A):
    x_fm = A["x_fm"]
    wq = A["wq"]
    wk = A["wk"]
    wv = A["wv"]
    wg = A["wg"]
    wout = A["wout"]
    wgk1 = A["wgk1"]
    wgk2 = A["wgk2"]
    bgk2_col = A["bgk2_col"]
    gamma_col = A["gamma_col"]
    beta_col = A["beta_col"]
    lng_col = A["lng_col"]
    lnb_col = A["lnb_col"]
    fc1b_col = A["fc1b_col"]
    fc2b_col = A["fc2b_col"]
    fc1_q = A["fc1_q"]
    fc1_sc = A["fc1_sc"]
    fc2_q = A["fc2_q"]
    fc2_sc = A["fc2_sc"]
    sbeta = A["sbeta"]
    x1_out = A["x1_out"]
    sp_kfm = A["sp_kfm"]
    sp_qfm = A["sp_qfm"]
    sp_gate = A["sp_gate"]
    sp_An = A["sp_An"]
    sp_ktm = A["sp_ktm"]
    sp_vt = A["sp_vt"]
    w1bf = A["w1bf"]
    w2bf = A["w2bf"]
    cc_in = A["cc_in"]
    cc_out = A["cc_out"]
    _st1 = contextlib.ExitStack()
    with _st1:
        xn_bf = _st1.enter_context(nc.sbuf_tensor("xn_bf", [P, PO * T], BF16))
        Call = _st1.enter_context(nc.sbuf_tensor("Call", [P, PO * NCH * HD], F32))
        Eall = _st1.enter_context(nc.sbuf_tensor("Eall", [P, PO * NCH], F32))
        cumE = _st1.enter_context(nc.sbuf_tensor("cumE", [P, PO * NCH], F32))
        Sinit = _st1.enter_context(nc.sbuf_tensor("Sinit", [P, PO * HD], F32))
        murow = _st1.enter_context(nc.sbuf_tensor("murow", [1, T], F32))
        rstdrow = _st1.enter_context(nc.sbuf_tensor("rstdrow", [1, T], F32))
        mrsrow = _st1.enter_context(nc.sbuf_tensor("mrsrow", [1, T], F32))
        ident_bf = _st1.enter_context(nc.sbuf_tensor("ident_bf", [P, P], BF16))
        ident_f = _st1.enter_context(nc.sbuf_tensor("ident_f", [P, P], F32))
        trimask = _st1.enter_context(nc.sbuf_tensor("trimask", [P, P], F32))
        zeros_f = _st1.enter_context(nc.sbuf_tensor("zeros_f", [P, P], F32))
        ccol = _st1.enter_context(nc.sbuf_tensor("ccol", [P, 8], F32))
        ones_row = _st1.enter_context(nc.sbuf_tensor("ones_row", [1, P], F32))
        gammas = _st1.enter_context(nc.sbuf_tensor("gammas", [P, PO * 2], F32))
        lns = _st1.enter_context(nc.sbuf_tensor("lns", [P, PO * 2], F32))
        fcb = _st1.enter_context(nc.sbuf_tensor("fcb", [P, NFC + PO], F32))
        bgk2s = _st1.enter_context(nc.sbuf_tensor("bgk2s", [P, PO], F32))
        sbetas = _st1.enter_context(nc.sbuf_tensor("sbetas", [P, 4], F32))
        rowtmp = _st1.enter_context(nc.sbuf_tensor("rowtmp", [1, TT], F32))
        rowtmp2 = _st1.enter_context(nc.sbuf_tensor("rowtmp2", [1, TT], F32))
        NEG1 = ccol[:, 0:1]
        NEG128 = ccol[:, 1:2]
        EPS = ccol[:, 2:3]
        ONEC = ccol[:, 4:5]

        p = Prog(nc, sems, cnt)
        for (buf, cmp) in ((ident_bf, ALU.not_equal), (ident_f, ALU.not_equal),
                           (trimask, ALU.is_gt)):
            p.op("gpsimd", (lambda g, b=buf: g.memset(b[:], 0.0)))
            p.op("gpsimd", (lambda g, b=buf, cm=cmp: g.affine_select(
                out=b[:], in_=b[:], compare_op=cm,
                fill=1.0, base=0, pattern=[[-1, P]], channel_multiplier=1)))
        p.op("gpsimd", lambda g: g.memset(zeros_f[:], 0.0))
        for (i, v) in ((0, -1.0), (1, -128.0), (2, 1e-5), (3, 1.0 / 1024.0), (4, 1.0)):
            p.op("gpsimd", (lambda g, i=i, v=v: g.memset(ccol[:, i:i + 1], v)))
        p.op("gpsimd", lambda g: g.memset(ones_row[:], 1.0))
        p.dma("sync", gammas[:, 0:PO], gamma_col)
        p.dma("sync", gammas[:, PO:2 * PO], beta_col)
        p.dma("sync", lns[:, 0:PO], lng_col)
        p.dma("sync", lns[:, PO:2 * PO], lnb_col)
        p.dma("sync", fcb[:, 0:NFC], fc1b_col)
        p.dma("sync", fcb[:, NFC:], fc2b_col)
        p.dma("sync", bgk2s[:], bgk2_col)
        p.dma("sync", sbetas[:], sbeta)
        p.emit("ph0" in phases)

        _st2 = contextlib.ExitStack()
        with _st2:
            wk_s = _st2.enter_context(nc.sbuf_tensor("wk_s", [P, PO * D], BF16))
            wv_s = _st2.enter_context(nc.sbuf_tensor("wv_s", [P, PO * D], BF16))
            wgk1_s = _st2.enter_context(nc.sbuf_tensor("wgk1_s", [P, PO * 16], BF16))
            wgk2_s = _st2.enter_context(nc.sbuf_tensor("wgk2_s", [16, D], BF16))
            xbuf = _st2.enter_context(nc.sbuf_tensor("xbuf", [P, PO * TT], F32))
            xsq = _st2.enter_context(nc.sbuf_tensor("xsq", [P, TT], F32))
            B1 = _st2.enter_context(nc.sbuf_tensor("B1", [P, TT], F32))
            B2 = _st2.enter_context(nc.sbuf_tensor("B2", [P, TT], F32))
            t1f = _st2.enter_context(nc.sbuf_tensor("t1f", [P, TT], F32))
            kfm_t = _st2.enter_context(nc.sbuf_tensor("kfm_t", [P, PO * TT], BF16))
            v32 = _st2.enter_context(nc.sbuf_tensor("v32", [P, PO * TT], F32))
            g32 = _st2.enter_context(nc.sbuf_tensor("g32", [P, PO * TT], F32))
            An_t = _st2.enter_context(nc.sbuf_tensor("An_t", [P, PO * TT], F32))
            mid_bf = _st2.enter_context(nc.sbuf_tensor("mid_bf", [16, TT], BF16))
            zf = _st2.enter_context(nc.sbuf_tensor("zf", [P, TT], F32))
            uf = _st2.enter_context(nc.sbuf_tensor("uf", [P, TT], F32))
            uf2 = _st2.enter_context(nc.sbuf_tensor("uf2", [P, TT], F32))
            expnA = _st2.enter_context(nc.sbuf_tensor("expnA", [P, PO * CH], F32))
            vt_fm = _st2.enter_context(nc.sbuf_tensor("vt_fm", [P, PO * CH], BF16))
            ktm_t = _st2.enter_context(nc.sbuf_tensor("ktm_t", [P, CPT * D], BF16))
            vt_t = _st2.enter_context(nc.sbuf_tensor("vt_t", [P, CPT * D], BF16))
            ps_a = _st2.enter_context(nc.psum_tensor("ps_a", [P, TT], F32))
            ps_b = _st2.enter_context(nc.psum_tensor("ps_b", [P, TT], F32))
            ps_s1 = _st2.enter_context(nc.psum_tensor("ps_s1", [1, TT], F32))
            ps_s2 = _st2.enter_context(nc.psum_tensor("ps_s2", [1, TT], F32))
            ps_mid = _st2.enter_context(nc.psum_tensor("ps_mid", [16, TT], F32))
            ps_t1 = _st2.enter_context(nc.psum_tensor("ps_t1", [P, CH], BF16))
            ps_t2 = _st2.enter_context(nc.psum_tensor("ps_t2", [P, CH], BF16))
            ps_c = _st2.enter_context(nc.psum_tensor("ps_c", [HD, HD], F32))
            p = Prog(nc, sems, cnt)
            p.dma("gpsimd", wk_s[:], wk)
            p.dma("gpsimd", wv_s[:], wv)
            p.dma("gpsimd", wgk1_s[:], wgk1)
            p.dma("gpsimd", wgk2_s[:], wgk2)
            wld = p.c("d_gpsimd")

            for t in range(NT):
                t0 = t * TT
                g_pe, g_act, g_dve = p.c("pe"), p.c("act"), p.c("dve")
                xw = p.dma("sync", po_view(xbuf, TT), po_view(x_fm, T)[:, :, t0:t0 + TT],
                           waits=[("pe", g_pe), ("act", g_act), ("dve", g_dve)])
                if "stats" not in PH1A_PARTS:
                    continue
                for po in range(PO):
                    p.op("tensor", (lambda e, po=po: e.matmul(
                        ps_s1[:], ONEC, xbuf[:, po * TT:(po + 1) * TT],
                        start=(po == 0), stop=(po == PO - 1))),
                        waits=([("d_sync", xw)] if po == 0 else ()),
                        inc=("pe" if po == PO - 1 else None))
                for po in range(PO):
                    aw = p.op("scalar", (lambda e, po=po: e.activation(
                        out=xsq[:], in_=xbuf[:, po * TT:(po + 1) * TT], func=AF.Square)),
                        waits=([("d_sync", xw), ("pe", g_pe)] if po == 0 else [("pe", p.c("pe"))]),
                        inc="act")
                    p.op("tensor", (lambda e, po=po: e.matmul(
                        ps_s2[:], ONEC, xsq[:],
                        start=(po == 0), stop=(po == PO - 1))),
                        waits=[("act", aw)], inc="pe")
                sq_done = p.c("pe")
                p.op("vector", (lambda e, t0=t0: e.tensor_scalar(
                    out=murow[0:1, t0:t0 + TT], in0=ps_s1[:], scalar1=1.0 / 1024.0,
                    scalar2=None, op0=ALU.mult)), waits=[("pe", sq_done)])
                p.op("vector", lambda e: e.tensor_scalar(
                    out=rowtmp[:], in0=ps_s2[:], scalar1=1.0 / 1024.0,
                    scalar2=None, op0=ALU.mult))
                p.op("vector", (lambda e, t0=t0: e.tensor_tensor(
                    out=rowtmp2[:], in0=murow[0:1, t0:t0 + TT],
                    in1=murow[0:1, t0:t0 + TT], op=ALU.mult)))
                dv = p.op("vector", lambda e: e.tensor_tensor(
                    out=rowtmp[:], in0=rowtmp[:], in1=rowtmp2[:], op=ALU.subtract),
                    inc="dve")
                av = p.op("scalar", lambda e: e.activation(
                    out=rowtmp2[:], in_=rowtmp[:], func=AF.Sqrt, bias=EPS[0:1, :], scale=1.0),
                    waits=[("dve", dv)], inc="act")
                rcp = p.op("vector", (lambda e, t0=t0: e.reciprocal(
                    out=rstdrow[0:1, t0:t0 + TT], in_=rowtmp2[:])),
                    waits=[("act", av)], inc="dve")
                dv = p.op("vector", (lambda e, t0=t0: e.tensor_tensor(
                    out=mrsrow[0:1, t0:t0 + TT], in0=murow[0:1, t0:t0 + TT],
                    in1=rstdrow[0:1, t0:t0 + TT], op=ALU.mult)),
                    waits=[("dve", rcp)], inc="dve")
                bm = p.op("tensor", (lambda e, t0=t0: e.matmul(
                    ps_b[:], ones_row[:], rstdrow[0:1, t0:t0 + TT], start=True, stop=True)),
                    waits=[("dve", dv)], inc="pe")
                ev = p.op("vector", lambda e: e.tensor_copy(out=B1[:], in_=ps_b[:]),
                          waits=[("pe", bm)], inc="dve")
                bm = p.op("tensor", (lambda e, t0=t0: e.matmul(
                    ps_b[:], ones_row[:], mrsrow[0:1, t0:t0 + TT], start=True, stop=True)),
                    waits=[("dve", ev)], inc="pe")
                ev = p.op("vector", lambda e: e.tensor_copy(out=B2[:], in_=ps_b[:]),
                          waits=[("pe", bm)], inc="dve")
                for po in range(PO):
                    p.op("vector", (lambda e, po=po: e.tensor_tensor(
                        out=t1f[:], in0=xbuf[:, po * TT:(po + 1) * TT], in1=B1[:], op=ALU.mult)))
                    p.op("vector", (lambda e, po=po, t0=t0: e.tensor_tensor(
                        out=xn_bf[:, po * T + t0: po * T + t0 + TT], in0=t1f[:], in1=B2[:],
                        op=ALU.subtract)))
                xn_done = p.mark("vector", "dve")
                if "proj" not in PH1A_PARTS:
                    continue

                for (wbuf, isk) in ((wk_s, True), (wv_s, False)):
                    for po in range(PO):
                        for kc in range(PO):
                            p.op("tensor", (lambda e, wbuf=wbuf, po=po, kc=kc, t0=t0: e.matmul(
                                ps_a[:], wbuf[:, kc * D + po * P: kc * D + (po + 1) * P],
                                xn_bf[:, kc * T + t0: kc * T + t0 + TT],
                                start=(kc == 0), stop=(kc == PO - 1))),
                                waits=([("dve", xn_done), ("d_gpsimd", wld)]
                                       if po == 0 and kc == 0 else ()))
                        mmv = p.mark("tensor", "pe")
                        if isk:
                            aw = p.op("scalar", (lambda e, po=po: e.activation(
                                out=kfm_t[:, po * TT:(po + 1) * TT], in_=ps_a[:], func=AF.Copy)),
                                waits=[("pe", mmv)], inc="act")
                            p.nop("tensor", waits=[("act", aw)])
                        else:
                            dw = p.op("vector", (lambda e, po=po: e.tensor_copy(
                                out=v32[:, po * TT:(po + 1) * TT], in_=ps_a[:])),
                                waits=[("pe", mmv)], inc="dve")
                            p.nop("tensor", waits=[("dve", dw)])
                kv_done_act = p.c("act")
                if "gk" not in PH1A_PARTS:
                    continue

                for kc in range(PO):
                    p.op("tensor", (lambda e, kc=kc, t0=t0: e.matmul(
                        ps_mid[:], wgk1_s[:, kc * 16:(kc + 1) * 16],
                        xn_bf[:, kc * T + t0: kc * T + t0 + TT],
                        start=(kc == 0), stop=(kc == PO - 1))))
                mmv = p.mark("tensor", "pe")
                am = p.op("scalar", lambda e: e.activation(
                    out=mid_bf[:], in_=ps_mid[:], func=AF.Copy),
                    waits=[("pe", mmv)], inc="act")
                for po in range(PO):
                    mm = p.op("tensor", (lambda e, po=po: e.matmul(
                        ps_b[:], wgk2_s[:, po * P:(po + 1) * P], mid_bf[:],
                        start=True, stop=True)),
                        waits=([("act", am)] if po == 0 else [("dve", p.c("dve"))]),
                        inc="pe")
                    p.op("scalar", (lambda e, po=po: e.activation(
                        out=zf[:], in_=ps_b[:], func=AF.Identity,
                        bias=bgk2s[:, po:po + 1], scale=1.0)), waits=[("pe", mm)])
                    p.op("scalar", lambda e: e.activation(out=uf[:], in_=zf[:], func=AF.Abs))
                    p.op("scalar", lambda e: e.activation(out=uf2[:], in_=uf[:], func=AF.Exp, scale=NEG1))
                    a2 = p.op("scalar", lambda e: e.activation(
                        out=uf[:], in_=uf2[:], func=AF.Ln, bias=1.0, scale=1.0), inc="act")
                    p.op("vector", lambda e: e.tensor_scalar(
                        out=zf[:], in0=zf[:], scalar1=0.0, scalar2=None, op0=ALU.min),
                        waits=[("act", a2)])
                    p.op("vector", lambda e: e.tensor_tensor(
                        out=uf[:], in0=uf[:], in1=zf[:], op=ALU.subtract))
                    p.op("vector", (lambda e, po=po: e.tensor_scalar(
                        out=g32[:, po * TT:(po + 1) * TT], in0=uf[:], scalar1=0.0625,
                        scalar2=None, op0=ALU.mult)), inc="dve")
                g_done = p.c("dve")
                if "chunk" not in PH1A_PARTS:
                    continue

                for c in range(CPT):
                    ci = t * CPT + c
                    for po in range(PO):
                        p.op("vector", (lambda e, po=po, c=c: e.tensor_tensor_scan(
                            out=An_t[:, po * TT + c * CH: po * TT + (c + 1) * CH],
                            data0=g32[:, po * TT + c * CH: po * TT + (c + 1) * CH],
                            data1=zeros_f[:, 0:CH], initial=0.0,
                            op0=ALU.add, op1=ALU.add)),
                            waits=([("dve", g_done)] if po == 0 and c == 0 else ()))
                    sc_done = p.mark("vector", "dve")
                    p.op("scalar", (lambda e, c=c, ci=ci: e.activation(
                        out=po_view(Eall, NCH)[:, :, ci:ci + 1],
                        in_=po_view(An_t, TT)[:, :, c * CH + CH - 1: c * CH + CH],
                        func=AF.Exp, scale=NEG1)), waits=[("dve", sc_done)], inc="act")
                    a4 = p.op("scalar", (lambda e, c=c: e.activation(
                        out=po_view(expnA, CH),
                        in_=po_view(An_t, TT)[:, :, c * CH:(c + 1) * CH],
                        func=AF.Exp)), inc="act")
                    for po in range(PO):
                        p.op("vector", (lambda e, po=po, c=c: e.tensor_tensor(
                            out=vt_fm[:, po * CH:(po + 1) * CH],
                            in0=expnA[:, po * CH:(po + 1) * CH],
                            in1=v32[:, po * TT + c * CH: po * TT + (c + 1) * CH],
                            op=ALU.mult)), waits=([("act", a4)] if po == 0 else ()))
                    vt_done = p.mark("vector", "dve")
                    for po in range(PO):
                        pst = ps_t1 if po % 2 == 0 else ps_t2
                        mm = p.op("tensor", (lambda e, po=po, pst=pst: e.transpose(
                            pst[:], vt_fm[:, po * CH:(po + 1) * CH], ident_bf[:])),
                            waits=[("dve", (vt_done if po == 0 else p.c("dve")))], inc="pe")
                        p.op("vector", (lambda e, po=po, c=c, pst=pst: e.tensor_copy(
                            out=vt_t[:, c * D + po * CH: c * D + (po + 1) * CH], in_=pst[:])),
                            waits=[("pe", mm)], inc="dve")
                    for po in range(PO):
                        pst = ps_t1 if po % 2 == 0 else ps_t2
                        mm = p.op("tensor", (lambda e, po=po, c=c, pst=pst: e.transpose(
                            pst[:], kfm_t[:, po * TT + c * CH: po * TT + (c + 1) * CH],
                            ident_bf[:])),
                            waits=[("dve", p.c("dve")), ("act", kv_done_act)], inc="pe")
                        p.op("vector", (lambda e, po=po, c=c, pst=pst: e.tensor_copy(
                            out=ktm_t[:, c * D + po * CH: c * D + (po + 1) * CH], in_=pst[:])),
                            waits=[("pe", mm)], inc="dve")
                    tr_done = p.c("dve")
                    for h in range(NH):
                        hb = (h % 2) * HD
                        po = h // 2
                        mm = p.op("tensor", (lambda e, h=h, c=c: e.matmul(
                            ps_c[:], vt_t[:, c * D + h * HD: c * D + (h + 1) * HD],
                            ktm_t[:, c * D + h * HD: c * D + (h + 1) * HD],
                            start=True, stop=True)),
                            waits=[("dve", (tr_done if h == 0 else p.c("dve")))], inc="pe")
                        p.op("vector", (lambda e, hb=hb, po=po, ci=ci: e.tensor_scalar(
                            out=Call[hb:hb + HD, (po * NCH + ci) * HD:(po * NCH + ci + 1) * HD],
                            in0=ps_c[:],
                            scalar1=Eall[hb:hb + HD, po * NCH + ci: po * NCH + ci + 1],
                            scalar2=None, op0=ALU.mult)),
                            waits=[("pe", mm), ("act", p.c("act"))], inc="dve")
                if "spill" not in PH1A_PARTS:
                    continue
                p.dma("scalar", sp_An[t], An_t[:], waits=[("dve", p.c("dve")), ("act", p.c("act"))])
                p.dma("scalar", sp_kfm[t], kfm_t[:])
                p.dma("scalar", sp_ktm[t], ktm_t[:])
                p.dma("scalar", sp_vt[t], vt_t[:])
                spw = p.c("d_scalar")
                p.nop("tensor", waits=[("d_scalar", spw)])
                p.nop("vector", waits=[("d_scalar", spw)])
                p.nop("scalar", waits=[("d_scalar", spw)])
            p.emit("ph1a" in phases)

        _st3 = contextlib.ExitStack()
        with _st3:
            wq_s = _st3.enter_context(nc.sbuf_tensor("wq_s", [P, PO * D], BF16))
            wg_s = _st3.enter_context(nc.sbuf_tensor("wg_s", [P, PO * D], BF16))
            qfm_t = _st3.enter_context(nc.sbuf_tensor("qfm_t", [P, PO * TT], BF16))
            gate_t = _st3.enter_context(nc.sbuf_tensor("gate_t", [P, PO * TT], BF16))
            ps_a1 = _st3.enter_context(nc.psum_tensor("ps_a1", [P, TT], F32))
            ps_b1 = _st3.enter_context(nc.psum_tensor("ps_b1", [P, TT], F32))
            p = Prog(nc, sems, cnt)
            p.dma("gpsimd", wq_s[:], wq)
            p.dma("gpsimd", wg_s[:], wg)
            wld = p.c("d_gpsimd")
            for t in range(NT):
                t0 = t * TT
                for po in range(PO):
                    for kc in range(PO):
                        p.op("tensor", (lambda e, po=po, kc=kc, t0=t0: e.matmul(
                            ps_a1[:], wq_s[:, kc * D + po * P: kc * D + (po + 1) * P],
                            xn_bf[:, kc * T + t0: kc * T + t0 + TT],
                            start=(kc == 0), stop=(kc == PO - 1))),
                            waits=([("d_gpsimd", wld)] if po == 0 and kc == 0 and t == 0 else ()))
                    m1 = p.mark("tensor", "pe")
                    a1 = p.op("scalar", (lambda e, po=po: e.activation(
                        out=qfm_t[:, po * TT:(po + 1) * TT], in_=ps_a1[:], func=AF.Copy)),
                        waits=[("pe", m1), ("d_scalar", p.c("d_scalar"))], inc="act")
                    for kc in range(PO):
                        p.op("tensor", (lambda e, po=po, kc=kc, t0=t0: e.matmul(
                            ps_b1[:], wg_s[:, kc * D + po * P: kc * D + (po + 1) * P],
                            xn_bf[:, kc * T + t0: kc * T + t0 + TT],
                            start=(kc == 0), stop=(kc == PO - 1))),
                            waits=([("act", a1)] if kc == 0 else ()))
                    m2 = p.mark("tensor", "pe")
                    a2 = p.op("scalar", (lambda e, po=po: e.activation(
                        out=gate_t[:, po * TT:(po + 1) * TT], in_=ps_b1[:], func=AF.Silu)),
                        waits=[("pe", m2)], inc="act")
                    p.nop("tensor", waits=[("act", a2)])
                p.dma("scalar", sp_qfm[t], qfm_t[:], waits=[("act", p.c("act"))])
                p.dma("scalar", sp_gate[t], gate_t[:])
            p.emit("ph1b" in phases)

        _st4 = contextlib.ExitStack()
        with _st4:
            tmp_s = _st4.enter_context(nc.sbuf_tensor("tmp_s", [P, HD], F32))
            gath = _st4.enter_context(nc.sbuf_tensor("gath", [P, PO * (HD + 1)], F32))
            p = Prog(nc, sems, cnt)
            for po in range(PO):
                p.op("vector", (lambda e, po=po: e.tensor_copy(
                    out=cumE[:, po * NCH: po * NCH + 1], in_=Eall[:, po * NCH: po * NCH + 1])))
            for j in range(1, NCH):
                for po in range(PO):
                    p.op("vector", (lambda e, po=po, j=j: e.tensor_scalar(
                        out=tmp_s[:],
                        in0=Call[:, (po * NCH + j - 1) * HD:(po * NCH + j) * HD],
                        scalar1=Eall[:, po * NCH + j: po * NCH + j + 1],
                        scalar2=None, op0=ALU.mult)))
                    p.op("vector", (lambda e, po=po, j=j: e.tensor_tensor(
                        out=Call[:, (po * NCH + j) * HD:(po * NCH + j + 1) * HD],
                        in0=Call[:, (po * NCH + j) * HD:(po * NCH + j + 1) * HD],
                        in1=tmp_s[:], op=ALU.add)))
                    p.op("vector", (lambda e, po=po, j=j: e.tensor_tensor(
                        out=cumE[:, po * NCH + j: po * NCH + j + 1],
                        in0=cumE[:, po * NCH + j - 1: po * NCH + j],
                        in1=Eall[:, po * NCH + j: po * NCH + j + 1], op=ALU.mult)))
            for po in range(PO):
                p.op("vector", (lambda e, po=po: e.tensor_copy(
                    out=gath[:, po * (HD + 1): po * (HD + 1) + HD],
                    in_=Call[:, (po * NCH + NCH - 1) * HD:(po * NCH + NCH) * HD])))
                p.op("vector", (lambda e, po=po: e.tensor_copy(
                    out=gath[:, po * (HD + 1) + HD: (po + 1) * (HD + 1)],
                    in_=cumE[:, po * NCH + NCH - 1: po * NCH + NCH])))
            dv = p.mark("vector", "dve")
            p.dma("gpsimd", cc_in.ap(), gath[:], waits=[("dve", dv)])
            ccw = p.c("d_gpsimd")
            p.op("gpsimd", lambda e: e.collective_compute(
                "AllGather", ALU.bypass,
                replica_groups=[[0, 1, 2, 3], [4, 5, 6, 7]],
                ins=[cc_in.ap().opt()], outs=[cc_out.ap().opt()]),
                waits=[("d_gpsimd", ccw)], inc="cc")
            p.emit("ph2" in phases)

            p = Prog(nc, sems, cnt)
            mz = p.op("gpsimd", lambda e: e.memset(Sinit[:], 0.0), inc="pool")
            for r in range(4):
                ld = p.dma("sync", gath[:], cc_out[r * P:(r + 1) * P, :].opt(),
                           waits=([("cc", p.c("cc"))] if r == 0 else [("dve", p.c("dve"))]))
                p.nop("vector", waits=[("d_sync", ld), ("pool", mz)])
                for po in range(PO):
                    p.op("vector", (lambda e, po=po: e.tensor_scalar(
                        out=tmp_s[:],
                        in0=Sinit[:, po * HD:(po + 1) * HD],
                        scalar1=gath[:, po * (HD + 1) + HD: (po + 1) * (HD + 1)],
                        scalar2=None, op0=ALU.mult)))
                    p.op("vector", (lambda e, po=po: e.tensor_tensor(
                        out=tmp_s[:], in0=tmp_s[:],
                        in1=gath[:, po * (HD + 1): po * (HD + 1) + HD], op=ALU.add)))
                    p.op("vector", (lambda e, po=po: e.tensor_tensor(
                        out=tmp_s[:], in0=tmp_s[:],
                        in1=Sinit[:, po * HD:(po + 1) * HD], op=ALU.subtract)))
                    p.op("vector", (lambda e, po=po, r=r: e.scalar_tensor_tensor(
                        out=Sinit[:, po * HD:(po + 1) * HD],
                        in0=tmp_s[:], scalar=sbetas[:, r:r + 1],
                        in1=Sinit[:, po * HD:(po + 1) * HD],
                        op0=ALU.mult, op1=ALU.add)), inc="dve")
            for cch in range(NCH - 1, 0, -1):
                for po in range(PO):
                    p.op("vector", (lambda e, po=po, cch=cch: e.tensor_scalar(
                        out=tmp_s[:], in0=Sinit[:, po * HD:(po + 1) * HD],
                        scalar1=cumE[:, po * NCH + cch - 1: po * NCH + cch],
                        scalar2=None, op0=ALU.mult)))
                    p.op("vector", (lambda e, po=po, cch=cch: e.tensor_tensor(
                        out=Call[:, (po * NCH + cch) * HD:(po * NCH + cch + 1) * HD],
                        in0=Call[:, (po * NCH + cch - 1) * HD:(po * NCH + cch) * HD],
                        in1=tmp_s[:], op=ALU.add)))
            for po in range(PO):
                p.op("vector", (lambda e, po=po: e.tensor_copy(
                    out=Call[:, (po * NCH) * HD:(po * NCH + 1) * HD],
                    in_=Sinit[:, po * HD:(po + 1) * HD])))
            p.emit("ph2" in phases)

        _gffn_cm = nc.sbuf_tensor("gffn", [P, PO * T], BF16)
        gffn = _gffn_cm.__enter__()

        _st5 = contextlib.ExitStack()
        with _st5:
            qi_b = _st5.enter_context(nc.sbuf_tensor("qi_b", [P, FF], I32))
            scb = _st5.enter_context(nc.sbuf_tensor("scb", [P, FF], F32))
            qf_b = _st5.enter_context(nc.sbuf_tensor("qf_b", [P, FF], F32))
            wrow = _st5.enter_context(nc.sbuf_tensor("wrow", [P, FF], BF16))
            p = Prog(nc, sems, cnt)
            for kc in range(8):
                lq = p.dma("sync", qi_b[:], fc1_q[kc * P:(kc + 1) * P, :],
                           waits=[("d_scalar", p.c("d_scalar"))])
                ls = p.dma("gpsimd", scb[:], fc1_sc.partition_broadcast(P)[:, kc, :],
                           waits=[("dve", p.c("dve"))])
                aw = p.op("scalar", lambda e: e.activation(
                    out=qf_b[:], in_=qi_b[:], func=AF.Identity, bias=NEG128, scale=1.0),
                    waits=[("d_sync", lq)], inc="act")
                dw = p.op("vector", lambda e: e.tensor_tensor(
                    out=wrow[:], in0=qf_b[:], in1=scb[:], op=ALU.mult),
                    waits=[("act", aw), ("d_gpsimd", ls)], inc="dve")
                sw = p.dma("scalar",
                           w1bf.ap().rearrange("n k p q -> p n k q")[:, :, kc, :],
                           wrow.ap().rearrange("p (n q) -> p n q", n=NFC),
                           waits=[("dve", dw)])
                p.nop("sync", waits=[("d_scalar", sw)])
            for kc in range(32):
                lq = p.dma("sync", qi_b[:, 0:D], fc2_q[kc * P:(kc + 1) * P, :],
                           waits=[("d_scalar", p.c("d_scalar"))])
                ls = p.dma("gpsimd", scb[:, 0:D], fc2_sc.partition_broadcast(P)[:, kc, :],
                           waits=[("dve", p.c("dve"))])
                aw = p.op("scalar", lambda e: e.activation(
                    out=qf_b[:, 0:D], in_=qi_b[:, 0:D], func=AF.Identity, bias=NEG128, scale=1.0),
                    waits=[("d_sync", lq)], inc="act")
                dw = p.op("vector", lambda e: e.tensor_tensor(
                    out=wrow[:, 0:D], in0=qf_b[:, 0:D], in1=scb[:, 0:D], op=ALU.mult),
                    waits=[("act", aw), ("d_gpsimd", ls)], inc="dve")
                sw = p.dma("scalar", w2bf[kc], wrow[:, 0:D], waits=[("dve", dw)])
                p.nop("sync", waits=[("d_scalar", sw)])
            p.emit("ph3a" in phases)

        _st6 = contextlib.ExitStack()
        with _st6:
            w1sl = _st6.enter_context(nc.sbuf_tensor("w1sl", [P, 8 * 8 * P], BF16))
            w2sl = _st6.enter_context(nc.sbuf_tensor("w2sl", [P, 8 * D], BF16))
            ymid = _st6.enter_context(nc.sbuf_tensor("ymid", [P, NFC * TT], BF16))
            lnin = _st6.enter_context(nc.sbuf_tensor("lnin", [P, PO * TT], BF16))
            zacc = _st6.enter_context(nc.sbuf_tensor("zacc", [P, PO * TT], F32))
            xchk = _st6.enter_context(nc.sbuf_tensor("xchk", [P, TT], F32))
            B13 = _st6.enter_context(nc.sbuf_tensor("B13", [P, TT], F32))
            B23 = _st6.enter_context(nc.sbuf_tensor("B23", [P, TT], F32))
            t3a = _st6.enter_context(nc.sbuf_tensor("t3a", [P, TT], F32))
            t3b = _st6.enter_context(nc.sbuf_tensor("t3b", [P, TT], F32))
            ps_f1 = _st6.enter_context(nc.psum_tensor("ps_f1", [P, TT], F32))
            ps_f2 = _st6.enter_context(nc.psum_tensor("ps_f2", [P, TT], F32))
            ps_bc3 = _st6.enter_context(nc.psum_tensor("ps_bc3", [P, TT], F32))
            p = Prog(nc, sems, cnt)
            for t in range(NT):
                t0 = t * TT
                g_dve = p.c("dve")
                bm = p.op("tensor", (lambda e, t0=t0: e.matmul(
                    ps_bc3[:], ones_row[:], rstdrow[0:1, t0:t0 + TT], start=True, stop=True)),
                    waits=[("dve", g_dve), ("act", p.c("act"))], inc="pe")
                ev1 = p.op("vector", lambda e: e.tensor_copy(out=B13[:], in_=ps_bc3[:]),
                           waits=[("pe", bm)], inc="dve")
                bm = p.op("tensor", (lambda e, t0=t0: e.matmul(
                    ps_bc3[:], ones_row[:], mrsrow[0:1, t0:t0 + TT], start=True, stop=True)),
                    waits=[("dve", ev1)], inc="pe")
                ev2 = p.op("vector", lambda e: e.tensor_copy(out=B23[:], in_=ps_bc3[:]),
                           waits=[("pe", bm)], inc="dve")
                for kc in range(PO):
                    p.op("vector", (lambda e, kc=kc, t0=t0: e.tensor_scalar(
                        out=lnin[:, kc * TT:(kc + 1) * TT],
                        in0=xn_bf[:, kc * T + t0: kc * T + t0 + TT],
                        scalar1=lns[:, kc:kc + 1], scalar2=lns[:, PO + kc:PO + kc + 1],
                        op0=ALU.mult, op1=ALU.add)))
                lnin_done = p.mark("vector", "dve")

                for slab in range(4):
                    lw = p.dma("gpsimd",
                               w1sl.ap().rearrange("p (n k q) -> p n k q", n=8, k=8),
                               w1bf[slab * 8:(slab + 1) * 8].rearrange("n k p q -> p n k q"),
                               waits=[("pe", p.c("pe"))])
                    for nch in range(8):
                        n_g = slab * 8 + nch
                        for kc in range(PO):
                            p.op("tensor", (lambda e, nch=nch, kc=kc: e.matmul(
                                ps_f1[:], w1sl[:, (nch * 8 + kc) * P:(nch * 8 + kc + 1) * P],
                                lnin[:, kc * TT:(kc + 1) * TT],
                                start=(kc == 0), stop=(kc == PO - 1))),
                                waits=([("d_gpsimd", lw), ("dve", lnin_done)]
                                       if nch == 0 and kc == 0 else ()))
                        mmv = p.mark("tensor", "pe")
                        aw = p.op("scalar", (lambda e, n_g=n_g: e.activation(
                            out=ymid[:, n_g * TT:(n_g + 1) * TT], in_=ps_f1[:], func=AF.Silu,
                            bias=fcb[:, n_g:n_g + 1], scale=1.0)),
                            waits=[("pe", mmv)], inc="act")
                        p.nop("tensor", waits=[("act", aw)])
                for slab2 in range(4):
                    lw2 = p.dma("gpsimd",
                                w2sl.ap().rearrange("p (k m) -> p k m", k=8),
                                w2bf[slab2 * 8:(slab2 + 1) * 8].rearrange("k p m -> p k m"),
                                waits=[("pe", p.c("pe"))])
                    for mpo in range(PO):
                        for kk in range(8):
                            p.op("tensor", (lambda e, kk=kk, mpo=mpo, slab2=slab2: e.matmul(
                                ps_f2[:], w2sl[:, kk * D + mpo * P: kk * D + (mpo + 1) * P],
                                ymid[:, (slab2 * 8 + kk) * TT:(slab2 * 8 + kk + 1) * TT],
                                start=(kk == 0), stop=(kk == 7))),
                                waits=([("d_gpsimd", lw2), ("act", p.c("act"))]
                                       if kk == 0 and mpo == 0 else ()))
                        mmv = p.mark("tensor", "pe")
                        if slab2 == 0:
                            dw = p.op("vector", (lambda e, mpo=mpo: e.tensor_copy(
                                out=zacc[:, mpo * TT:(mpo + 1) * TT], in_=ps_f2[:])),
                                waits=[("pe", mmv)], inc="dve")
                        else:
                            dw = p.op("vector", (lambda e, mpo=mpo: e.tensor_tensor(
                                out=zacc[:, mpo * TT:(mpo + 1) * TT],
                                in0=zacc[:, mpo * TT:(mpo + 1) * TT],
                                in1=ps_f2[:], op=ALU.add)),
                                waits=[("pe", mmv)], inc="dve")
                        p.nop("tensor", waits=[("dve", dw)])
                for mpo in range(PO):
                    lx = p.dma("sync", xchk[:],
                               po_view(x_fm, T)[:, mpo, t0:t0 + TT],
                               waits=[("dve", p.c("dve")), ("act", p.c("act"))])
                    p.op("vector", (lambda e, mpo=mpo: e.tensor_scalar(
                        out=t3a[:], in0=zacc[:, mpo * TT:(mpo + 1) * TT],
                        scalar1=fcb[:, NFC + mpo: NFC + mpo + 1], scalar2=None,
                        op0=ALU.add)))
                    p.op("vector", (lambda e, mpo=mpo: e.tensor_tensor(
                        out=t3b[:], in0=xchk[:], in1=B13[:], op=ALU.mult)),
                        waits=[("d_sync", lx)])
                    p.op("vector", lambda e: e.tensor_tensor(
                        out=t3b[:], in0=t3b[:], in1=B23[:], op=ALU.subtract))
                    d1 = p.op("vector", lambda e: e.scalar_tensor_tensor(
                        out=t3b[:], in0=t3a[:], scalar=0.5, in1=t3b[:],
                        op0=ALU.mult, op1=ALU.add), inc="dve")
                    a2 = p.op("scalar", (lambda e, mpo=mpo, t0=t0: e.activation(
                        out=gffn[:, mpo * T + t0: mpo * T + t0 + TT], in_=t3b[:], func=AF.Silu)),
                        waits=[("dve", d1)], inc="act")
                    p.nop("vector", waits=[("act", a2)])
            p.emit("ph3b" in phases)

        _st7 = contextlib.ExitStack()
        with _st7:
            wout_s = _st7.enter_context(nc.sbuf_tensor("wout_s", [P, PO * D], BF16))
            qg = _st7.enter_context(nc.sbuf_tensor("qg", [P, PO * TT], BF16))
            kg = _st7.enter_context(nc.sbuf_tensor("kg", [P, PO * TT], BF16))
            gateg = _st7.enter_context(nc.sbuf_tensor("gateg", [P, PO * TT], BF16))
            Ang = _st7.enter_context(nc.sbuf_tensor("Ang", [P, PO * TT], F32))
            ktg = _st7.enter_context(nc.sbuf_tensor("ktg", [P, CPT * D], BF16))
            vtg = _st7.enter_context(nc.sbuf_tensor("vtg", [P, CPT * D], BF16))
            Skv = _st7.enter_context(nc.sbuf_tensor("Skv", [P, (NH // 2) * HD], BF16))
            expA = _st7.enter_context(nc.sbuf_tensor("expA", [P, PO * CH], F32))
            mask_s = _st7.enter_context(nc.sbuf_tensor("mask_s", [P, CH], BF16))
            outb = _st7.enter_context(nc.sbuf_tensor("outb", [P, PO * CH], F32))
            sqb = _st7.enter_context(nc.sbuf_tensor("sqb", [P, CH], F32))
            B14 = _st7.enter_context(nc.sbuf_tensor("B14", [P, CH], F32))
            B24 = _st7.enter_context(nc.sbuf_tensor("B24", [P, CH], F32))
            lng4 = _st7.enter_context(nc.sbuf_tensor("lng4", [P, PO * TT], BF16))
            g32b = _st7.enter_context(nc.sbuf_tensor("g32b", [P, TT], F32))
            xob = _st7.enter_context(nc.sbuf_tensor("xob", [P, TT], F32))
            xg4 = _st7.enter_context(nc.sbuf_tensor("xg4", [P, PO * TT], F32))
            row4 = _st7.enter_context(nc.sbuf_tensor("row4", [1, CH], F32))
            row4b = _st7.enter_context(nc.sbuf_tensor("row4b", [1, CH], F32))
            ps_sc4 = _st7.enter_context(nc.psum_tensor("ps_sc4", [P, CH], F32))
            ps_mi4 = _st7.enter_context(nc.psum_tensor("ps_mi4", [HD, CH], F32))
            ps_st4 = _st7.enter_context(nc.psum_tensor("ps_st4", [HD, HD], F32))
            ps_s14 = _st7.enter_context(nc.psum_tensor("ps_s14", [1, CH], F32))
            ps_s24 = _st7.enter_context(nc.psum_tensor("ps_s24", [1, CH], F32))
            ps_bc4 = _st7.enter_context(nc.psum_tensor("ps_bc4", [P, CH], F32))
            ps_w4 = _st7.enter_context(nc.psum_tensor("ps_w4", [P, TT], F32))
            p = Prog(nc, sems, cnt)
            p.dma("gpsimd", wout_s[:], wout)
            wld = p.c("d_gpsimd")
            for t in range(NT):
                t0 = t * TT
                guard = [("pe", p.c("pe")), ("act", p.c("act")), ("dve", p.c("dve"))]
                p.dma("sync", qg[:], sp_qfm[t], waits=guard)
                p.dma("sync", kg[:], sp_kfm[t])
                p.dma("sync", gateg[:], sp_gate[t])
                lall_sync = p.c("d_sync")
                p.dma("scalar", Ang[:], sp_An[t], waits=guard)
                p.dma("scalar", ktg[:], sp_ktm[t])
                p.dma("scalar", vtg[:], sp_vt[t])
                lall_sc = p.c("d_scalar")
                lxo = p.dma("gpsimd", xg4[:].rearrange("p (po t) -> p po t", po=PO),
                            po_view(x_fm, T)[:, :, t0:t0 + TT], waits=guard)
                for c in range(CPT):
                    ci = t * CPT + c
                    ea = p.op("scalar", (lambda e, c=c: e.activation(
                        out=po_view(expA, CH),
                        in_=po_view(Ang, TT)[:, :, c * CH:(c + 1) * CH],
                        func=AF.Exp, scale=NEG1)),
                        waits=[("d_scalar", lall_sc), ("dve", p.c("dve"))], inc="act")
                    for h in range(NH):
                        hb = (h % 2) * HD
                        po = h // 2
                        mm = p.op("tensor", (lambda e, hb=hb, po=po, ci=ci: e.transpose(
                            ps_st4[:], Call[hb:hb + HD, (po * NCH + ci) * HD:(po * NCH + ci + 1) * HD],
                            ident_f[hb:hb + HD, hb:hb + HD])),
                            waits=[("act", p.c("act"))], inc="pe")
                        p.op("scalar", (lambda e, hb=hb, po=po: e.activation(
                            out=Skv[hb:hb + HD, po * HD:(po + 1) * HD], in_=ps_st4[:], func=AF.Copy)),
                            waits=[("pe", mm)], inc="act")
                    st_done = p.c("act")
                    for h in range(NH):
                        hb = (h % 2) * HD
                        po = h // 2
                        mm = p.op("tensor", (lambda e, hb=hb, po=po, c=c: e.matmul(
                            ps_sc4[:], kg[hb:hb + HD, po * TT + c * CH: po * TT + (c + 1) * CH],
                            qg[hb:hb + HD, po * TT + c * CH: po * TT + (c + 1) * CH],
                            start=True, stop=True)),
                            waits=([("d_sync", lall_sync), ("act", st_done)] if h == 0 else
                                   [("dve", p.c("dve"))]),
                            inc="pe")
                        msk = p.op("vector", (lambda e: e.tensor_tensor(
                            out=mask_s[:], in0=ps_sc4[:], in1=trimask[:], op=ALU.mult)),
                            waits=[("pe", mm)], inc="dve")
                        p.op("tensor", (lambda e, h=h, c=c: e.matmul(
                            ps_mi4[:], vtg[:, c * D + h * HD: c * D + (h + 1) * HD],
                            mask_s[:], start=True, stop=False)),
                            waits=[("dve", msk)])
                        mm2 = p.op("tensor", (lambda e, hb=hb, po=po, c=c: e.matmul(
                            ps_mi4[:], Skv[hb:hb + HD, po * HD:(po + 1) * HD],
                            qg[hb:hb + HD, po * TT + c * CH: po * TT + (c + 1) * CH],
                            start=False, stop=True)), inc="pe")
                        p.op("vector", (lambda e, hb=hb, po=po: e.tensor_tensor(
                            out=outb[hb:hb + HD, po * CH:(po + 1) * CH],
                            in0=ps_mi4[:], in1=expA[hb:hb + HD, po * CH:(po + 1) * CH],
                            op=ALU.mult)),
                            waits=[("pe", mm2), ("act", ea)], inc="dve")
                    out_done = p.c("dve")
                    for po in range(PO):
                        p.op("tensor", (lambda e, po=po: e.matmul(
                            ps_s14[:], ONEC, outb[:, po * CH:(po + 1) * CH],
                            start=(po == 0), stop=(po == PO - 1))),
                            waits=([("dve", out_done)] if po == 0 else ()))
                    p.mark("tensor", "pe")
                    for po in range(PO):
                        aw = p.op("scalar", (lambda e, po=po: e.activation(
                            out=sqb[:], in_=outb[:, po * CH:(po + 1) * CH], func=AF.Square)),
                            waits=([("dve", out_done)] if po == 0 else [("pe", p.c("pe"))]),
                            inc="act")
                        p.op("tensor", (lambda e, po=po: e.matmul(
                            ps_s24[:], ONEC, sqb[:],
                            start=(po == 0), stop=(po == PO - 1))),
                            waits=[("act", aw)], inc="pe")
                    stat_done = p.c("pe")
                    p.op("vector", lambda e: e.tensor_scalar(
                        out=row4[:], in0=ps_s14[:], scalar1=1.0 / 1024.0, scalar2=None,
                        op0=ALU.mult), waits=[("pe", stat_done)])
                    p.op("vector", lambda e: e.tensor_scalar(
                        out=row4b[:], in0=ps_s24[:], scalar1=1.0 / 1024.0, scalar2=None,
                        op0=ALU.mult))
                    p.op("vector", lambda e: e.tensor_tensor(
                        out=g32b[0:1, 0:CH], in0=row4[:], in1=row4[:], op=ALU.mult))
                    dv = p.op("vector", lambda e: e.tensor_tensor(
                        out=row4b[:], in0=row4b[:], in1=g32b[0:1, 0:CH], op=ALU.subtract),
                        inc="dve")
                    av = p.op("scalar", lambda e: e.activation(
                        out=g32b[0:1, 0:CH], in_=row4b[:], func=AF.Sqrt, bias=EPS[0:1, :], scale=1.0),
                        waits=[("dve", dv)], inc="act")
                    dv = p.op("vector", lambda e: e.reciprocal(out=row4b[:], in_=g32b[0:1, 0:CH]),
                              waits=[("act", av)], inc="dve")
                    p.op("vector", lambda e: e.tensor_tensor(
                        out=row4[:], in0=row4[:], in1=row4b[:], op=ALU.mult),
                        waits=[("dve", dv)], inc="dve")
                    bm = p.op("tensor", lambda e: e.matmul(
                        ps_bc4[:], ones_row[:], row4b[:], start=True, stop=True),
                        waits=[("dve", p.c("dve"))], inc="pe")
                    e1 = p.op("vector", lambda e: e.tensor_copy(out=B14[:], in_=ps_bc4[:]),
                              waits=[("pe", bm)], inc="dve")
                    bm = p.op("tensor", lambda e: e.matmul(
                        ps_bc4[:], ones_row[:], row4[:], start=True, stop=True),
                        waits=[("dve", e1)], inc="pe")
                    e2 = p.op("vector", lambda e: e.tensor_copy(out=B24[:], in_=ps_bc4[:]),
                              waits=[("pe", bm)], inc="dve")
                    for po in range(PO):
                        p.op("vector", (lambda e, po=po: e.tensor_tensor(
                            out=sqb[:], in0=outb[:, po * CH:(po + 1) * CH], in1=B14[:],
                            op=ALU.mult)),
                            waits=([("dve", e2), ("pe", stat_done)] if po == 0 else ()))
                        p.op("vector", lambda e: e.tensor_tensor(
                            out=sqb[:], in0=sqb[:], in1=B24[:], op=ALU.subtract))
                        p.op("vector", (lambda e, po=po: e.tensor_scalar(
                            out=sqb[:], in0=sqb[:],
                            scalar1=gammas[:, po:po + 1], scalar2=gammas[:, PO + po:PO + po + 1],
                            op0=ALU.mult, op1=ALU.add)))
                        aw = p.op("scalar", (lambda e, po=po, c=c: e.activation(
                            out=g32b[:, 0:CH],
                            in_=gateg[:, po * TT + c * CH: po * TT + (c + 1) * CH],
                            func=AF.Copy)),
                            waits=[("d_sync", lall_sync), ("dve", p.c("dve"))], inc="act")
                        p.op("vector", (lambda e, po=po, c=c: e.tensor_tensor(
                            out=lng4[:, po * TT + c * CH: po * TT + (c + 1) * CH],
                            in0=sqb[:], in1=g32b[:, 0:CH], op=ALU.mult)),
                            waits=[("act", aw)], inc="dve")
                ln_done = p.c("dve")
                for po in range(PO):
                    for kc in range(PO):
                        p.op("tensor", (lambda e, po=po, kc=kc: e.matmul(
                            ps_w4[:], wout_s[:, kc * D + po * P: kc * D + (po + 1) * P],
                            lng4[:, kc * TT:(kc + 1) * TT],
                            start=(kc == 0), stop=(kc == PO - 1))),
                            waits=([("dve", ln_done), ("d_gpsimd", wld)]
                                   if po == 0 and kc == 0 else ()))
                    mmv = p.mark("tensor", "pe")
                    aw = p.op("scalar", (lambda e, po=po, t0=t0: e.activation(
                        out=g32b[:], in_=gffn[:, po * T + t0: po * T + t0 + TT], func=AF.Copy)),
                        waits=[("dve", p.c("dve"))], inc="act")
                    p.op("vector", (lambda e: e.tensor_tensor(
                        out=xob[:], in0=ps_w4[:], in1=g32b[:], op=ALU.mult)),
                        waits=[("pe", mmv), ("act", aw), ("d_gpsimd", lxo)])
                    dv = p.op("vector", (lambda e, po=po: e.tensor_tensor(
                        out=xob[:], in0=xob[:], in1=xg4[:, po * TT:(po + 1) * TT], op=ALU.add)),
                        inc="dve")
                    sw = p.dma("scalar", po_view(x1_out, T)[:, po, t0:t0 + TT],
                               xob[:], waits=[("dve", dv)])
                    p.nop("vector", waits=[("d_scalar", sw)])
                    p.nop("tensor", waits=[("d_scalar", sw)])
            p.emit("ph4" in phases)

        _gffn_cm.__exit__(None, None, None)

        p = Prog(nc, sems, cnt)
        p.nop("sync", waits=[("d_scalar", cnt.get("d_scalar", 0))])
        p.emit("ph4" in phases)



def build_l0(detect_races=True, phases=("ph0", "ph1a", "ph1b", "ph2", "ph3a", "ph3b", "ph4")):
    phases = set(phases)
    nc = bass.Bass("TRN2", target_bir_lowering=False, detect_race_conditions=detect_races)
    A = _decl_l0(nc, "ExternalOutput")
    sems = {}
    cnt = {}
    stack = contextlib.ExitStack()
    for k in ("d_sync", "d_gpsimd", "d_scalar", "pe", "act", "dve", "pool", "cc"):
        sems[k] = stack.enter_context(nc.semaphore(k))
    with stack:
        _emit_l0(nc, sems, cnt, phases, A)
    return nc




def _fm_layout(a2d):
    Tc = a2d.shape[0]
    return np.ascontiguousarray(
        a2d.T.reshape(PO, P, Tc).transpose(1, 0, 2).reshape(P, PO * Tc))


def _w_layout(w):
    return np.ascontiguousarray(
        w.reshape(PO, P, w.shape[1]).transpose(1, 0, 2).reshape(P, PO * w.shape[1]))


def _col_layout(v):
    n = v.shape[0] // P
    return np.ascontiguousarray(v.reshape(n, P).T)


def unfm_layout(a):
    Tc = a.shape[1] // PO
    return a.reshape(P, PO, Tc).transpose(1, 0, 2).reshape(D, Tc).T


def prep_l0_inputs(inputs):
    x = np.asarray(inputs["L0_attn_x"])
    base = {
        "wq": _w_layout(np.asarray(inputs["L0_attn_Wq"])),
        "wk": _w_layout(np.asarray(inputs["L0_attn_Wk"])),
        "wv": _w_layout(np.asarray(inputs["L0_attn_Wv"])),
        "wg": _w_layout(np.asarray(inputs["L0_attn_Wg"])),
        "wout": _w_layout(np.asarray(inputs["L0_attn_Wout"])),
        "wgk1": _w_layout(np.asarray(inputs["L0_attn_Wgk1"])),
        "wgk2": np.ascontiguousarray(np.asarray(inputs["L0_attn_Wgk2"])),
        "bgk2_col": _col_layout(np.asarray(inputs["L0_attn_bgk2"])),
        "gamma_col": _col_layout(np.asarray(inputs["L0_attn_gamma"])),
        "beta_col": _col_layout(np.asarray(inputs["L0_attn_beta"])),
        "lng_col": _col_layout(np.asarray(inputs["L0_ffn_ln_gamma"])),
        "lnb_col": _col_layout(np.asarray(inputs["L0_ffn_ln_beta"])),
        "fc1b_col": _col_layout(np.asarray(inputs["L0_ffn_fc1_bias"])),
        "fc2b_col": _col_layout(np.asarray(inputs["L0_ffn_fc2_bias"])),
        "fc1_q": np.ascontiguousarray(np.asarray(inputs["L0_ffn_fc1_q"])),
        "fc1_sc": np.ascontiguousarray(np.asarray(inputs["L0_ffn_fc1_scales"])),
        "fc2_q": np.ascontiguousarray(np.asarray(inputs["L0_ffn_fc2_q"])),
        "fc2_sc": np.ascontiguousarray(np.asarray(inputs["L0_ffn_fc2_scales"])),
    }
    maps = []
    for core in range(8):
        b, r = core // 4, core % 4
        m = dict(base)
        m["x_fm"] = _fm_layout(x[b, r * T:(r + 1) * T, :])
        sb = np.zeros((P, 4), np.float32)
        sb[:, :r] = 1.0
        m["sbeta"] = sb
        maps.append(m)
    return maps


QUAT_TAB = {
    (0, 0): (0, 1.0), (1, 0): (1, -1.0), (2, 0): (2, -1.0), (3, 0): (3, -1.0),
    (0, 1): (1, 1.0), (1, 1): (0, 1.0), (2, 1): (3, -1.0), (3, 1): (2, 1.0),
    (0, 2): (2, 1.0), (1, 2): (3, 1.0), (2, 2): (0, 1.0), (3, 2): (1, -1.0),
    (0, 3): (3, 1.0), (1, 3): (2, -1.0), (2, 3): (1, 1.0), (3, 3): (0, 1.0),
}

TS5 = 512
NT5 = T // TS5
NTC = T // P


def _decl_l1(nc, x1_ap=None):
    x1_fm = x1_ap if x1_ap is not None else nc.dram_tensor("x1_fm", [P, PO * T], F32, kind="ExternalInput").ap()
    qw_r = nc.dram_tensor("qw_r", [P, 8 * 256], F32, kind="ExternalInput").ap()
    kw_r = nc.dram_tensor("kw_r", [P, 8 * 256], F32, kind="ExternalInput").ap()
    vw_r = nc.dram_tensor("vw_r", [P, 8 * 256], F32, kind="ExternalInput").ap()
    ow_r = nc.dram_tensor("ow_r", [P, 8 * 256], F32, kind="ExternalInput").ap()
    qb_r = nc.dram_tensor("qb_r", [1, 4 * 256], F32, kind="ExternalInput").ap()
    kb_r = nc.dram_tensor("kb_r", [1, 4 * 256], F32, kind="ExternalInput").ap()
    vb_r = nc.dram_tensor("vb_r", [1, 4 * 256], F32, kind="ExternalInput").ap()
    ob_r = nc.dram_tensor("ob_r", [1, 4 * 256], F32, kind="ExternalInput").ap()
    rw = nc.dram_tensor("rw", [D, 16], F32, kind="ExternalInput").ap()
    rb = nc.dram_tensor("rb", [1, 16], F32, kind="ExternalInput").ap()
    w1m = nc.dram_tensor("w1m", [NEXP, D, MI], F32, kind="ExternalInput").ap()
    b1m = nc.dram_tensor("b1m", [NEXP, P, MI // P], F32, kind="ExternalInput").ap()
    w2m = nc.dram_tensor("w2m", [NEXP, MI, D], F32, kind="ExternalInput").ap()
    b2m = nc.dram_tensor("b2m", [1, NEXP * D], F32, kind="ExternalInput").ap()
    jcol = nc.dram_tensor("jcol", [P, PO], F32, kind="ExternalInput").ap()

    out_fm = nc.dram_tensor("out_fm", [P, PO * T], F32, kind="ExternalOutput").ap()
    if DEBUG_L1:
        dbg_xn2 = nc.dram_tensor("dbg_xn2", [P, PO * T], BF16, kind="ExternalOutput").ap()
        dbg_ctm = nc.dram_tensor("dbg_ctm", [P, NTC * NEXP], F32, kind="ExternalOutput").ap()
        dbg_k = nc.dram_tensor("dbg_k", [P, PO * T], BF16, kind="ExternalOutput").ap()
        dbg_v = nc.dram_tensor("dbg_v", [P, NTC * NH * 65], BF16, kind="ExternalOutput").ap()
        dbg_q = nc.dram_tensor("dbg_q", [P, PO * T], BF16, kind="ExternalOutput").ap()
        dbg_oacc = nc.dram_tensor("dbg_oacc", [P, NH * 4 * 4 * 65], F32, kind="ExternalOutput").ap()
        dbg_s = nc.dram_tensor("dbg_s", [P, PO * T], BF16, kind="ExternalOutput").ap()
        dbg_z = nc.dram_tensor("dbg_z", [P, NTC * D], F32, kind="ExternalOutput").ap()

    sp_xn2 = nc.dram_tensor("sp_xn2", [P, PO * T], BF16)
    sp_s = nc.dram_tensor("sp_s", [P, PO * T], BF16)
    bias_sc = nc.dram_tensor("bias_sc", [4, PO, P], F32)
    ck_in = nc.dram_tensor("ck_in", [P, PO * T], BF16)
    ck_out = nc.dram_tensor("ck_out", [4 * P, PO * T], BF16)
    cv_in = nc.dram_tensor("cv_in", [P, NTC * NH * 65], BF16)
    cv_out = nc.dram_tensor("cv_out", [4 * P, NTC * NH * 65], BF16)

    d = dict(x1_fm=x1_fm, qw_r=qw_r, kw_r=kw_r, vw_r=vw_r, ow_r=ow_r, qb_r=qb_r, kb_r=kb_r, vb_r=vb_r, ob_r=ob_r, rw=rw, rb=rb, w1m=w1m, b1m=b1m, w2m=w2m, b2m=b2m, jcol=jcol, out_fm=out_fm, sp_xn2=sp_xn2, sp_s=sp_s, bias_sc=bias_sc, ck_in=ck_in, ck_out=ck_out, cv_in=cv_in, cv_out=cv_out)
    if DEBUG_L1:
        d.update(dbg_xn2=dbg_xn2, dbg_ctm=dbg_ctm, dbg_k=dbg_k, dbg_v=dbg_v, dbg_q=dbg_q, dbg_oacc=dbg_oacc, dbg_s=dbg_s, dbg_z=dbg_z)
    return d


def _emit_l1(nc, sems, cnt, A):
    x1_fm = A["x1_fm"]
    qw_r = A["qw_r"]
    kw_r = A["kw_r"]
    vw_r = A["vw_r"]
    ow_r = A["ow_r"]
    qb_r = A["qb_r"]
    kb_r = A["kb_r"]
    vb_r = A["vb_r"]
    ob_r = A["ob_r"]
    rw = A["rw"]
    rb = A["rb"]
    w1m = A["w1m"]
    b1m = A["b1m"]
    w2m = A["w2m"]
    b2m = A["b2m"]
    jcol = A["jcol"]
    out_fm = A["out_fm"]
    dbg_xn2 = A.get("dbg_xn2")
    dbg_ctm = A.get("dbg_ctm")
    dbg_k = A.get("dbg_k")
    dbg_v = A.get("dbg_v")
    dbg_q = A.get("dbg_q")
    dbg_oacc = A.get("dbg_oacc")
    dbg_s = A.get("dbg_s")
    dbg_z = A.get("dbg_z")
    sp_xn2 = A["sp_xn2"]
    sp_s = A["sp_s"]
    bias_sc = A["bias_sc"]
    ck_in = A["ck_in"]
    ck_out = A["ck_out"]
    cv_in = A["cv_in"]
    cv_out = A["cv_out"]
    _sl1 = contextlib.ExitStack()
    with _sl1:
        ident_bf = _sl1.enter_context(nc.sbuf_tensor("ident_bf_y", [P, P], BF16))
        ccol = _sl1.enter_context(nc.sbuf_tensor("ccol_y", [P, 8], F32))
        ones_row = _sl1.enter_context(nc.sbuf_tensor("ones_row_y", [1, P], F32))
        c_tm = _sl1.enter_context(nc.sbuf_tensor("c_tm_y", [P, NTC * NEXP], F32))
        bcols = _sl1.enter_context(nc.sbuf_tensor("bcols_y", [P, 4 * PO], F32))
        rbs = _sl1.enter_context(nc.sbuf_tensor("rbs_y", [P, 16], F32))
        ps_j = _sl1.enter_context(nc.psum_tensor("ps_j_y", [1, PO], F32))
        ps_j2 = _sl1.enter_context(nc.psum_tensor("ps_j2_y", [P, 1], F32))

        NEG1 = ccol[:, 0:1]
        EPSL = ccol[:, 2:3]
        EPSQ = ccol[:, 3:4]
        ONEC = ccol[:, 4:5]
        NEG8 = ccol[:, 5:6]
        JBIAS = ccol[:, 6:7]

        p = Prog(nc, sems, cnt)
        p.op("gpsimd", lambda g: g.memset(ident_bf[:], 0.0), inc="pool")
        p.op("gpsimd", lambda g: g.affine_select(
            out=ident_bf[:], in_=ident_bf[:], compare_op=ALU.not_equal,
            fill=1.0, base=0, pattern=[[-1, P]], channel_multiplier=1),
            waits=[("pool", p.c("pool"))], inc="pool")
        for (i, v) in ((0, -1.0), (2, 1e-5), (3, 1e-6), (4, 1.0), (5, -8.0)):
            p.op("gpsimd", (lambda g, i=i, v=v: g.memset(ccol[:, i:i + 1], v)))
        p.op("gpsimd", lambda g: g.memset(ones_row[:], 1.0))
        p.dma("sync", rbs[:], rb.partition_broadcast(P)[:, 0, :])
        jw = p.dma("sync", bcols[:, 0:PO], jcol)
        aj = p.op("scalar", lambda e: e.activation(
            out=bcols[:, PO:2 * PO], in_=bcols[:, 0:PO], func=AF.Sin),
            waits=[("d_sync", jw), ("pool", p.c("pool"))], inc="act")
        mj = p.op("tensor", lambda e: e.matmul(
            ps_j[:], ccol[:, 4:5], bcols[:, PO:2 * PO], start=True, stop=True),
            waits=[("act", aj)], inc="pe")
        dj = p.op("vector", lambda e: e.reduce_sum(
            out=ccol[0:1, 6:7], in_=ps_j[:], axis=AX.X), waits=[("pe", mj)], inc="dve")
        dj = p.op("vector", lambda e: e.tensor_scalar(
            out=ccol[0:1, 6:7], in0=ccol[0:1, 6:7], scalar1=0.1 / 1024.0, scalar2=-8.0,
            op0=ALU.mult, op1=ALU.add), waits=[("dve", dj)], inc="dve")
        mj = p.op("tensor", lambda e: e.matmul(
            ps_j2[:], ones_row[:], ccol[0:1, 6:7], start=True, stop=True),
            waits=[("dve", dj)], inc="pe")
        p.op("vector", (lambda e: e.tensor_copy(out=ccol[:, 6:7], in_=ps_j2[:])),
             waits=[("pe", mj)], inc="dve")
        p.emit("ph5" in PHASES_L1)

        _s5 = contextlib.ExitStack()
        with _s5:
            brow = _s5.enter_context(nc.sbuf_tensor("brow_y", [1, 4 * 256], F32))
            bcomb = _s5.enter_context(nc.sbuf_tensor("bcomb_y", [1, D], F32))
            p = Prog(nc, sems, cnt)
            for (pi, bsrc) in enumerate((qb_r, kb_r, vb_r, ob_r)):
                ld = p.dma("sync", brow[:], bsrc, waits=[("d_scalar", p.c("d_scalar"))])
                p.nop("vector", waits=[("d_sync", ld)])
                for comp in range(4):
                    for ci in range(4):
                        w, sg = QUAT_TAB[(ci, comp)]
                        if ci == 0:
                            p.op("vector", (lambda e, comp=comp, w=w, sg=sg: e.tensor_scalar(
                                out=bcomb[0:1, comp * 256:(comp + 1) * 256],
                                in0=brow[0:1, w * 256:(w + 1) * 256],
                                scalar1=sg, scalar2=None, op0=ALU.mult)))
                        elif sg > 0:
                            p.op("vector", (lambda e, comp=comp, w=w: e.tensor_tensor(
                                out=bcomb[0:1, comp * 256:(comp + 1) * 256],
                                in0=bcomb[0:1, comp * 256:(comp + 1) * 256],
                                in1=brow[0:1, w * 256:(w + 1) * 256], op=ALU.add)))
                        else:
                            p.op("vector", (lambda e, comp=comp, w=w: e.tensor_tensor(
                                out=bcomb[0:1, comp * 256:(comp + 1) * 256],
                                in0=bcomb[0:1, comp * 256:(comp + 1) * 256],
                                in1=brow[0:1, w * 256:(w + 1) * 256], op=ALU.subtract)))
                dv = p.mark("vector", "dve")
                p.dma("scalar", bias_sc[pi].rearrange("po q -> (po q)").unsqueeze(0), bcomb[:],
                      waits=[("dve", dv)])
            sw = p.c("d_scalar")
            ld = p.dma("sync", bcols[:],
                       bias_sc.rearrange("w po q -> q (w po)"),
                       waits=[("d_scalar", sw)], allow_slow_non_contiguous=True)
            p.emit("ph5" in PHASES_L1)

        _s5b = contextlib.ExitStack()
        with _s5b:
            xbuf = _s5b.enter_context(nc.sbuf_tensor("xbuf5_y", [P, PO * TS5], F32))
            murow2 = _s5b.enter_context(nc.sbuf_tensor("murow2_y", [1, T], F32))
            rstdrow2 = _s5b.enter_context(nc.sbuf_tensor("rstdrow2_y", [1, T], F32))
            xsq = _s5b.enter_context(nc.sbuf_tensor("xsq5_y", [P, TS5], F32))
            B1 = _s5b.enter_context(nc.sbuf_tensor("B15_y", [P, TS5], F32))
            B2 = _s5b.enter_context(nc.sbuf_tensor("B25_y", [P, TS5], F32))
            t1f = _s5b.enter_context(nc.sbuf_tensor("t1f5_y", [P, TS5], F32))
            xn2t = _s5b.enter_context(nc.sbuf_tensor("xn2t_y", [P, PO * TS5], BF16))
            rwsb = _s5b.enter_context(nc.sbuf_tensor("rwsb_y", [P, PO * 16], BF16))
            logit = _s5b.enter_context(nc.sbuf_tensor("logit_y", [P, 16], F32))
            pwork = _s5b.enter_context(nc.sbuf_tensor("pwork_y", [P, 16], F32))
            macc = _s5b.enter_context(nc.sbuf_tensor("macc_y", [P, 16], F32))
            eqt = _s5b.enter_context(nc.sbuf_tensor("eqt_y", [P, 16], F32))
            eqt2 = _s5b.enter_context(nc.sbuf_tensor("eqt2_y", [P, 16], F32))
            ones16 = _s5b.enter_context(nc.sbuf_tensor("ones16_y", [P, 16], F32))
            col1 = _s5b.enter_context(nc.sbuf_tensor("col1_y", [P, 1], F32))
            col2 = _s5b.enter_context(nc.sbuf_tensor("col2_y", [P, 1], F32))
            rowt = _s5b.enter_context(nc.sbuf_tensor("rowt5_y", [1, TS5], F32))
            rowt2 = _s5b.enter_context(nc.sbuf_tensor("rowt5b_y", [1, TS5], F32))
            ps_a = _s5b.enter_context(nc.psum_tensor("ps_a5_y", [P, TS5], F32))
            ps_s1 = _s5b.enter_context(nc.psum_tensor("ps_s15_y", [1, TS5], F32))
            ps_s2 = _s5b.enter_context(nc.psum_tensor("ps_s25_y", [1, TS5], F32))
            ps_r = _s5b.enter_context(nc.psum_tensor("ps_r5_y", [P, 16], F32))

            p = Prog(nc, sems, cnt)
            p.dma("gpsimd", rwsb.ap().rearrange("p (po e) -> p po e", po=PO),
                   rw.rearrange("(po p) e -> p po e", p=P))
            p.op("gpsimd", lambda g: g.memset(ones16[:], 1.0), inc="pool")
            wld = p.c("d_gpsimd")
            for t in range(NT5):
                t0 = t * TS5
                guard = [("pe", p.c("pe")), ("act", p.c("act")), ("dve", p.c("dve"))]
                xw = p.dma("sync", po_view(xbuf, TS5), po_view(x1_fm, T)[:, :, t0:t0 + TS5],
                           waits=guard)
                for po in range(PO):
                    p.op("tensor", (lambda e, po=po: e.matmul(
                        ps_s1[:], ONEC, xbuf[:, po * TS5:(po + 1) * TS5],
                        start=(po == 0), stop=(po == PO - 1))),
                        waits=([("d_sync", xw)] if po == 0 else ()),
                        inc=("pe" if po == PO - 1 else None))
                for po in range(PO):
                    aw = p.op("scalar", (lambda e, po=po: e.activation(
                        out=xsq[:], in_=xbuf[:, po * TS5:(po + 1) * TS5], func=AF.Square)),
                        waits=([("d_sync", xw), ("pe", guard[0][1])] if po == 0
                               else [("pe", p.c("pe"))]),
                        inc="act")
                    p.op("tensor", (lambda e, po=po: e.matmul(
                        ps_s2[:], ONEC, xsq[:],
                        start=(po == 0), stop=(po == PO - 1))),
                        waits=[("act", aw)], inc="pe")
                sq_done = p.c("pe")
                p.op("vector", (lambda e, t0=t0: e.tensor_scalar(
                    out=murow2[0:1, t0:t0 + TS5], in0=ps_s1[:], scalar1=1.0 / 1024.0,
                    scalar2=None, op0=ALU.mult)), waits=[("pe", sq_done)])
                p.op("vector", lambda e: e.tensor_scalar(
                    out=rowt[:], in0=ps_s2[:], scalar1=1.0 / 1024.0, scalar2=None, op0=ALU.mult))
                p.op("vector", (lambda e, t0=t0: e.tensor_tensor(
                    out=rowt2[:], in0=murow2[0:1, t0:t0 + TS5], in1=murow2[0:1, t0:t0 + TS5],
                    op=ALU.mult)))
                dv = p.op("vector", lambda e: e.tensor_tensor(
                    out=rowt[:], in0=rowt[:], in1=rowt2[:], op=ALU.subtract), inc="dve")
                av = p.op("scalar", lambda e: e.activation(
                    out=rowt2[:], in_=rowt[:], func=AF.Sqrt, bias=EPSL[0:1, :], scale=1.0),
                    waits=[("dve", dv)], inc="act")
                rcp = p.op("vector", (lambda e, t0=t0: e.reciprocal(
                    out=rstdrow2[0:1, t0:t0 + TS5], in_=rowt2[:])),
                    waits=[("act", av)], inc="dve")
                dv = p.op("vector", (lambda e, t0=t0: e.tensor_tensor(
                    out=rowt2[:], in0=murow2[0:1, t0:t0 + TS5],
                    in1=rstdrow2[0:1, t0:t0 + TS5], op=ALU.mult)),
                    waits=[("dve", rcp)], inc="dve")
                bm = p.op("tensor", (lambda e, t0=t0: e.matmul(
                    ps_a[:], ones_row[:], rstdrow2[0:1, t0:t0 + TS5], start=True, stop=True)),
                    waits=[("dve", dv)], inc="pe")
                ev = p.op("vector", lambda e: e.tensor_copy(out=B1[:], in_=ps_a[:]),
                          waits=[("pe", bm)], inc="dve")
                bm = p.op("tensor", lambda e: e.matmul(
                    ps_a[:], ones_row[:], rowt2[:], start=True, stop=True),
                    waits=[("dve", ev)], inc="pe")
                ev = p.op("vector", lambda e: e.tensor_copy(out=B2[:], in_=ps_a[:]),
                          waits=[("pe", bm)], inc="dve")
                for po in range(PO):
                    p.op("vector", (lambda e, po=po: e.tensor_tensor(
                        out=t1f[:], in0=xbuf[:, po * TS5:(po + 1) * TS5], in1=B1[:],
                        op=ALU.mult)))
                    p.op("vector", (lambda e, po=po: e.tensor_tensor(
                        out=xn2t[:, po * TS5:(po + 1) * TS5], in0=t1f[:], in1=B2[:],
                        op=ALU.subtract)))
                xn_done = p.mark("vector", "dve")
                sw = p.dma("scalar", po_view(sp_xn2.ap(), T)[:, :, t0:t0 + TS5],
                           po_view(xn2t, TS5), waits=[("dve", xn_done)])
                for tc in range(TS5 // P):
                    tg = (t0 + tc * P) // P
                    for kc in range(PO):
                        p.op("tensor", (lambda e, kc=kc, tc=tc: e.matmul(
                            ps_r[:], xn2t[:, kc * TS5 + tc * P: kc * TS5 + (tc + 1) * P],
                            rwsb[:, kc * 16:(kc + 1) * 16],
                            start=(kc == 0), stop=(kc == PO - 1))),
                            waits=([("dve", xn_done), ("d_gpsimd", wld),
                                    ("dve", p.c("dve"))] if kc == 0 else ()))
                    rm = p.mark("tensor", "pe")
                    dv0 = p.op("vector", lambda e: e.tensor_tensor(
                        out=logit[:], in0=ps_r[:], in1=rbs[:], op=ALU.add),
                        waits=[("pe", rm)], inc="dve")
                    rmx = p.op("vector", lambda e: e.reduce_max(col1[:], logit[:], axis=AX.X),
                               inc="dve")
                    p.op("vector", lambda e: e.tensor_scalar(
                        out=col1[:], in0=col1[:], scalar1=-1.0, scalar2=None, op0=ALU.mult),
                        waits=[("dve", rmx)])
                    dv1 = p.mark("vector", "dve")
                    a1 = p.op("scalar", lambda e: e.activation(
                        out=pwork[:], in_=logit[:], func=AF.Exp, bias=col1[:, 0:1], scale=1.0),
                        waits=[("dve", dv1)], inc="act")
                    p.op("vector", lambda e: e.tensor_copy(out=logit[:], in_=pwork[:]),
                         waits=[("act", a1)])
                    p.op("gpsimd", lambda e: e.memset(macc[:], 0.0),
                         waits=[("dve", p.c("dve"))], inc="pool")
                    p.op("gpsimd", lambda e: e.memset(col2[:], 0.0), inc="pool")
                    p.nop("vector", waits=[("pool", p.c("pool"))])
                    for it in range(4):
                        rmx = p.op("vector", lambda e: e.reduce_max(col1[:], pwork[:], axis=AX.X),
                                   inc="dve")
                        p.op("vector", lambda e: e.tensor_tensor(
                            out=col2[:], in0=col2[:], in1=col1[:], op=ALU.add),
                            waits=[("dve", rmx)])
                        p.op("vector", lambda e: e.tensor_scalar(
                            out=eqt[:], in0=ones16[:], scalar1=col1[:, 0:1],
                            scalar2=None, op0=ALU.mult))
                        p.op("vector", lambda e: e.tensor_tensor(
                            out=eqt[:], in0=pwork[:], in1=eqt[:], op=ALU.subtract))
                        dsg = p.mark("vector", "dve")
                        asg = p.op("scalar", lambda e: e.activation(
                            out=eqt2[:], in_=eqt[:], func=AF.Sign),
                            waits=[("dve", dsg)], inc="act")
                        p.op("vector", lambda e: e.tensor_scalar(
                            out=eqt2[:], in0=eqt2[:], scalar1=1.0, scalar2=None, op0=ALU.add),
                            waits=[("act", asg)])
                        p.op("vector", lambda e: e.tensor_tensor(
                            out=macc[:], in0=macc[:], in1=eqt2[:], op=ALU.max))
                        p.op("vector", lambda e: e.scalar_tensor_tensor(
                            out=pwork[:], in0=eqt2[:], scalar=-1e30, in1=pwork[:],
                            op0=ALU.mult, op1=ALU.add))
                    rcp = p.op("vector", lambda e: e.reciprocal(col1[:], col2[:]), inc="dve")
                    p.op("vector", lambda e: e.tensor_tensor(
                        out=logit[:], in0=logit[:], in1=macc[:], op=ALU.mult),
                        waits=[("dve", rcp)])
                    p.op("vector", (lambda e, tg=tg: e.tensor_scalar(
                        out=c_tm[:, tg * NEXP:(tg + 1) * NEXP], in0=logit[:],
                        scalar1=col1[:, 0:1], scalar2=None, op0=ALU.mult)), inc="dve")
                p.nop("scalar", waits=[("d_scalar", p.c("d_scalar"))])
            p.emit("ph5" in PHASES_L1)

        _s6 = contextlib.ExitStack()
        with _s6:
            q_fm = _s6.enter_context(nc.sbuf_tensor("q_fm_y", [P, PO * T], BF16))
            _s6i = contextlib.ExitStack()
            wqb = _s6i.enter_context(nc.sbuf_tensor("wqb_y", [P, PO * D], BF16))
            wkb = _s6i.enter_context(nc.sbuf_tensor("wkb_y", [P, PO * D], BF16))
            wvb = _s6i.enter_context(nc.sbuf_tensor("wvb_y", [P, PO * D], BF16))
            qraw = _s6i.enter_context(nc.sbuf_tensor("qraw_y", [P, 8 * 256], BF16))
            k_fm = _s6i.enter_context(nc.sbuf_tensor("k_fm_y", [P, PO * T], BF16))
            vtc = _s6i.enter_context(nc.sbuf_tensor("vtc_y", [P, NTC * NH * 65], BF16))
            xn2t6 = _s6i.enter_context(nc.sbuf_tensor("xn2t6_y", [P, PO * TS5], BF16))
            pre = _s6i.enter_context(nc.sbuf_tensor("pre_y", [P, PO * TS5], F32))
            sqt = _s6i.enter_context(nc.sbuf_tensor("sqt_y", [P, TS5], F32))
            nsq = _s6i.enter_context(nc.sbuf_tensor("nsq_y", [P, 2 * TS5], F32))
            nsq2 = _s6i.enter_context(nc.sbuf_tensor("nsq2_y", [P, 2 * TS5], F32))
            ps_a = _s6i.enter_context(nc.psum_tensor("ps_a6_y", [P, TS5], F32))
            ps_t = _s6i.enter_context(nc.psum_tensor("ps_t6_y", [P, P], BF16))
            ps_t2 = _s6i.enter_context(nc.psum_tensor("ps_t6b_y", [P, P], BF16))

            p = Prog(nc, sems, cnt)
            for (wsrc, wdst) in ((qw_r, wqb), (kw_r, wkb), (vw_r, wvb)):
                lw = p.dma("gpsimd", qraw[:], wsrc, waits=[("dve", p.c("dve"))])
                p.nop("vector", waits=[("d_gpsimd", lw)])
                for kc in range(PO):
                    ci, hc = kc // 2, kc % 2
                    for comp in range(4):
                        w, sg = QUAT_TAB[(ci, comp)]
                        p.op("vector", (lambda e, kc=kc, comp=comp, w=w, sg=sg, hc=hc,
                                        wdst=wdst: e.tensor_scalar(
                            out=wdst[:, kc * D + comp * 256: kc * D + (comp + 1) * 256],
                            in0=qraw[:, (w * 2 + hc) * 256:(w * 2 + hc + 1) * 256],
                            scalar1=sg, scalar2=None, op0=ALU.mult)))
                p.mark("vector", "dve")
            p.op("gpsimd", lambda e: e.memset(
                vtc.ap().rearrange("p (tc h o) -> p tc h o", tc=NTC, h=NH)[:, :, :, 64:65], 1.0),
                inc="pool")
            wasm = p.c("dve")
            for t in range(NT5):
                t0 = t * TS5
                guard = [("pe", p.c("pe")), ("act", p.c("act")), ("dve", max(p.c("dve"), wasm))]
                lx = p.dma("sync", po_view(xn2t6, TS5), po_view(sp_xn2.ap(), T)[:, :, t0:t0 + TS5],
                           waits=guard)
                for (pi, wdst, dst) in ((0, wqb, q_fm), (1, wkb, k_fm), (2, wvb, None)):
                    for po in range(PO):
                        for kc in range(PO):
                            p.op("tensor", (lambda e, wdst=wdst, po=po, kc=kc: e.matmul(
                                ps_a[:], wdst[:, kc * D + po * P: kc * D + (po + 1) * P],
                                xn2t6[:, kc * TS5:(kc + 1) * TS5],
                                start=(kc == 0), stop=(kc == PO - 1))),
                                waits=([("d_sync", lx), ("dve", max(wasm, p.c("dve"))),
                                        ("act", p.c("act"))]
                                       if po == 0 and kc == 0 else ()))
                        mmv = p.mark("tensor", "pe")
                        aw = p.op("scalar", (lambda e, pi=pi, po=po: e.activation(
                            out=pre[:, po * TS5:(po + 1) * TS5], in_=ps_a[:], func=AF.Identity,
                            bias=bcols[:, pi * PO + po: pi * PO + po + 1], scale=1.0)),
                            waits=[("pe", mmv)], inc="act")
                        p.nop("tensor", waits=[("act", aw)])
                    pre_done = p.c("act")
                    if pi < 2:
                        for half in range(2):
                            for ci2 in range(4):
                                po = 2 * ci2 + half
                                aw = p.op("scalar", (lambda e, po=po: e.activation(
                                    out=sqt[:], in_=pre[:, po * TS5:(po + 1) * TS5],
                                    func=AF.Square)),
                                    waits=([("act", pre_done), ("dve", p.c("dve"))]
                                           if half == 0 and ci2 == 0
                                           else [("dve", p.c("dve"))]), inc="act")
                                if ci2 == 0:
                                    p.op("vector", (lambda e, half=half: e.tensor_copy(
                                        out=nsq[:, half * TS5:(half + 1) * TS5], in_=sqt[:])),
                                        waits=[("act", aw)], inc="dve")
                                else:
                                    p.op("vector", (lambda e, half=half: e.tensor_tensor(
                                        out=nsq[:, half * TS5:(half + 1) * TS5],
                                        in0=nsq[:, half * TS5:(half + 1) * TS5],
                                        in1=sqt[:], op=ALU.add)),
                                        waits=[("act", aw)], inc="dve")
                        dvn = p.c("dve")
                        a2 = p.op("scalar", lambda e: e.activation(
                            out=nsq2[:], in_=nsq[:], func=AF.Sqrt, bias=EPSQ, scale=1.0),
                            waits=[("dve", dvn)], inc="act")
                        dv2 = p.op("vector", lambda e: e.reciprocal(out=nsq[:], in_=nsq2[:]),
                                   waits=[("act", a2)], inc="dve")
                        if pi == 0:
                            dv2 = p.op("vector", lambda e: e.tensor_scalar(
                                out=nsq[:], in0=nsq[:], scalar1=0.125, scalar2=None,
                                op0=ALU.mult), waits=[("dve", dv2)], inc="dve")
                        for po in range(PO):
                            p.op("vector", (lambda e, po=po, t0=t0, dst=dst: e.tensor_tensor(
                                out=dst[:, po * T + t0: po * T + t0 + TS5],
                                in0=pre[:, po * TS5:(po + 1) * TS5],
                                in1=nsq[:, (po % 2) * TS5:((po % 2) + 1) * TS5],
                                op=ALU.mult)),
                                waits=([("dve", dv2)] if po == 0 else ()))
                        p.mark("vector", "dve")
                    else:
                        for po in range(PO):
                            for tc2 in range(TS5 // P):
                                pst = ps_t if (po * 4 + tc2) % 2 == 0 else ps_t2
                                aw = p.op("scalar", (lambda e, po=po, tc2=tc2: e.activation(
                                    out=sqt.ap().bitcast(BF16)[:, 0:P],
                                    in_=pre[:, po * TS5 + tc2 * P: po * TS5 + (tc2 + 1) * P],
                                    func=AF.Copy)),
                                    waits=[("act", pre_done), ("pe", p.c("pe"))], inc="act")
                                mm = p.op("tensor", (lambda e, pst=pst: e.transpose(
                                    pst[:], sqt.ap().bitcast(BF16)[:, 0:P], ident_bf[:])),
                                    waits=[("act", aw)], inc="pe")
                                tcg = t * (TS5 // P) + tc2
                                p.op("vector", (lambda e, po=po, tcg=tcg, pst=pst: e.tensor_copy(
                                    out=vtc.ap().rearrange(
                                        "p (tc h o) -> p tc h o", tc=NTC, h=NH)[
                                        :, tcg, 2 * po: 2 * po + 2, 0:64],
                                    in_=pst.ap().rearrange("p (a b) -> p a b", a=2))),
                                    waits=[("pe", mm)], inc="dve")
            p.emit("ph6" in PHASES_L1)

            p = Prog(nc, sems, cnt)
            dv = p.c("dve")
            p.dma("gpsimd", ck_in.ap(), k_fm[:], waits=[("dve", dv), ("pool", p.c("pool"))])
            p.dma("gpsimd", cv_in.ap(), vtc[:])
            ccw = p.c("d_gpsimd")
            p.op("gpsimd", lambda e: e.collective_compute(
                "AllGather", ALU.bypass,
                replica_groups=[[0, 1, 2, 3], [4, 5, 6, 7]],
                ins=[ck_in.ap().opt()], outs=[ck_out.ap().opt()]),
                waits=[("d_gpsimd", ccw)], inc="cc")
            p.op("gpsimd", lambda e: e.collective_compute(
                "AllGather", ALU.bypass,
                replica_groups=[[0, 1, 2, 3], [4, 5, 6, 7]],
                ins=[cv_in.ap().opt()], outs=[cv_out.ap().opt()]),
                inc="cc")
            p.emit("ph6" in PHASES_L1)
            _s6i.close()

            _s7 = contextlib.ExitStack()
            with _s7:
                k_sb = _s7.enter_context(nc.sbuf_tensor("k_sb_y", [P, PO * T], BF16))
                v_sb = _s7.enter_context(nc.sbuf_tensor("v_sb_y", [P, NTC * NH * 65], BF16))
                oacc = _s7.enter_context(nc.sbuf_tensor("oacc_y", [P, NH * 4 * 4 * 65], F32))
                probs = _s7.enter_context(nc.sbuf_tensor("probs_y", [P, 2 * 512], BF16))
                lcol = _s7.enter_context(nc.sbuf_tensor("lcol_y", [P, 1], F32))
                ps_sc1 = _s7.enter_context(nc.psum_tensor("ps_sc1_y", [P, 512], F32))
                ps_sc2 = _s7.enter_context(nc.psum_tensor("ps_sc2_y", [P, 512], F32))
                ps_o1 = _s7.enter_context(nc.psum_tensor("ps_o1_y", [P, 4 * 65], F32))
                ps_o2 = _s7.enter_context(nc.psum_tensor("ps_o2_y", [P, 4 * 65], F32))
                QB = 512

                p = Prog(nc, sems, cnt)
                p.nop("sync", waits=[("cc", p.c("cc"))])
                exp_log = []
                for r in range(4):
                    lk = p.dma("sync", k_sb[:], ck_out[r * P:(r + 1) * P, :].opt(),
                               waits=([("cc", p.c("cc"))] if r == 0 else
                                      [("pe", p.c("pe")), ("act", p.c("act"))]))
                    lv = p.dma("sync", v_sb[:], cv_out[r * P:(r + 1) * P, :].opt())
                    lkv = p.c("d_sync")
                    for h in range(NH):
                        hb = (h % 2) * HD
                        hp = h // 2
                        for qb in range(4):
                            exps = []
                            for kt in range(NTC):
                                pss = ps_sc1 if kt % 2 == 0 else ps_sc2
                                wts = [("act", p.c("act"))] if len(exp_log) >= 2 else []
                                if kt == 0 and h == 0 and qb == 0:
                                    wts.append(("d_sync", lkv))
                                if kt == 0:
                                    wts.append(("dve", p.c("dve")))
                                mm = p.op("tensor", (lambda e, hb=hb, hp=hp, kt=kt, qb=qb,
                                                     pss=pss: e.matmul(
                                    pss[:], k_sb[hb:hb + HD, hp * T + kt * P: hp * T + (kt + 1) * P],
                                    q_fm[hb:hb + HD, hp * T + qb * QB: hp * T + (qb + 1) * QB],
                                    start=True, stop=True)),
                                    waits=([("act", exp_log[-2])] if len(exp_log) >= 2 else
                                           wts),
                                    inc="pe")
                                pb = probs[:, (kt % 2) * 512:(kt % 2 + 1) * 512]
                                a1 = p.op("scalar", (lambda e, pss=pss, pb=pb: e.activation(
                                    out=pb, in_=pss[:], func=AF.Exp, bias=JBIAS, scale=1.0)),
                                    waits=[("pe", mm)], inc="act")
                                exp_log.append(a1)
                                pso = ps_o1 if (h * 4 + qb) % 2 == 0 else ps_o2
                                for qt in range(4):
                                    p.op("tensor", (lambda e, pb=pb, qt=qt, kt=kt, h=h,
                                                    pso=pso: e.matmul(
                                        pso[:, qt * 65:(qt + 1) * 65],
                                        pb[:, qt * P:(qt + 1) * P],
                                        v_sb[:, (kt * NH + h) * 65:(kt * NH + h + 1) * 65],
                                        start=(kt == 0 and qt == 0),
                                        stop=(kt == NTC - 1 and qt == 3))),
                                        waits=([("act", a1)] if qt == 0 else ()),
                                        inc=("pe" if kt == NTC - 1 and qt == 3 else None))
                            pvd = p.c("pe")
                            if r == 0:
                                p.op("vector", (lambda e, h=h, qb=qb, pso=pso: e.tensor_copy(
                                    out=oacc[:, (h * 4 + qb) * 260:(h * 4 + qb + 1) * 260],
                                    in_=pso[:])), waits=[("pe", pvd)], inc="dve")
                            else:
                                p.op("vector", (lambda e, h=h, qb=qb, pso=pso: e.tensor_tensor(
                                    out=oacc[:, (h * 4 + qb) * 260:(h * 4 + qb + 1) * 260],
                                    in0=oacc[:, (h * 4 + qb) * 260:(h * 4 + qb + 1) * 260],
                                    in1=pso[:], op=ALU.add)), waits=[("pe", pvd)], inc="dve")
                p.emit("ph7" in PHASES_L1)

                p = Prog(nc, sems, cnt)
                if DEBUG_L1:
                    p.dma("scalar", dbg_oacc, oacc[:], waits=[("dve", p.c("dve"))])
                    p.dma("scalar", dbg_xn2, sp_xn2.ap())
                    p.dma("scalar", dbg_ctm, c_tm[:])
                    p.dma("scalar", dbg_k, ck_in.ap())
                    p.dma("scalar", dbg_v, cv_in.ap())
                    p.nop("sync", waits=[("d_scalar", p.c("d_scalar"))])
                for h in range(NH):
                    for tcg in range(NTC):
                        qb, qt = tcg // 4, tcg % 4
                        base = (h * 4 + qb) * 260 + qt * 65
                        rcp = p.op("vector", (lambda e, base=base: e.reciprocal(
                            out=lcol[:], in_=oacc[:, base + 64: base + 65])), inc="dve")
                        p.op("vector", (lambda e, base=base, tcg=tcg, h=h: e.tensor_scalar(
                            out=q_fm.ap().bitcast(BF16)[:, tcg * D + h * HD: tcg * D + (h + 1) * HD]
                            if False else
                            q_fm[:, tcg * D + h * HD: tcg * D + (h + 1) * HD],
                            in0=oacc[:, base: base + 64],
                            scalar1=lcol[:, 0:1], scalar2=None, op0=ALU.mult)),
                            waits=[("dve", rcp)])
                p.mark("vector", "dve")
                p.emit("ph7" in PHASES_L1)
            p.emit if False else None

            _s7b = contextlib.ExitStack()
            with _s7b:
                afm = _s7b.enter_context(nc.sbuf_tensor("afm_y", [P, PO * T], BF16))
                wob = _s7b.enter_context(nc.sbuf_tensor("wob_y", [P, PO * D], BF16))
                qraw2 = _s7b.enter_context(nc.sbuf_tensor("qraw2_y", [P, 8 * 256], BF16))
                sfm = _s7b.enter_context(nc.sbuf_tensor("sfm_y", [P, PO * TS5], BF16))
                ps_t7 = _s7b.enter_context(nc.psum_tensor("ps_t7_y", [P, P], BF16))
                ps_t7b = _s7b.enter_context(nc.psum_tensor("ps_t7b_y", [P, P], BF16))
                ps_a7 = _s7b.enter_context(nc.psum_tensor("ps_a7_y", [P, TS5], F32))

                p = Prog(nc, sems, cnt)
                if DEBUG_L1:
                    p.dma("scalar", dbg_q, q_fm[:], waits=[("dve", p.c("dve"))])
                lw = p.dma("gpsimd", qraw2[:], ow_r)
                p.nop("vector", waits=[("d_gpsimd", p.c("d_gpsimd"))])
                for kc in range(PO):
                    ci, hc = kc // 2, kc % 2
                    for comp in range(4):
                        w, sg = QUAT_TAB[(ci, comp)]
                        p.op("vector", (lambda e, kc=kc, comp=comp, w=w, sg=sg, hc=hc:
                                        e.tensor_scalar(
                            out=wob[:, kc * D + comp * 256: kc * D + (comp + 1) * 256],
                            in0=qraw2[:, (w * 2 + hc) * 256:(w * 2 + hc + 1) * 256],
                            scalar1=sg, scalar2=None, op0=ALU.mult)))
                wdone = p.mark("vector", "dve")
                for tcg in range(NTC):
                    for j in range(PO):
                        pst = ps_t7 if (tcg * PO + j) % 2 == 0 else ps_t7b
                        mm = p.op("tensor", (lambda e, tcg=tcg, j=j, pst=pst: e.transpose(
                            pst[:], q_fm[:, tcg * D + j * P: tcg * D + (j + 1) * P],
                            ident_bf[:])),
                            waits=[("dve", p.c("dve"))], inc="pe")
                        p.op("vector", (lambda e, tcg=tcg, j=j, pst=pst: e.tensor_copy(
                            out=afm[:, j * T + tcg * P: j * T + (tcg + 1) * P], in_=pst[:])),
                            waits=[("pe", mm)], inc="dve")
                tr_done = p.c("dve")
                for t in range(NT5):
                    t0 = t * TS5
                    for po in range(PO):
                        for kc in range(PO):
                            p.op("tensor", (lambda e, po=po, kc=kc, t0=t0: e.matmul(
                                ps_a7[:], wob[:, kc * D + po * P: kc * D + (po + 1) * P],
                                afm[:, kc * T + t0: kc * T + t0 + TS5],
                                start=(kc == 0), stop=(kc == PO - 1))),
                                waits=([("dve", tr_done)] if t == 0 and po == 0 and kc == 0
                                       else ()))
                        mmv = p.mark("tensor", "pe")
                        aw = p.op("scalar", (lambda e, po=po: e.activation(
                            out=sfm[:, po * TS5:(po + 1) * TS5], in_=ps_a7[:],
                            func=AF.Identity, bias=bcols[:, 3 * PO + po: 3 * PO + po + 1],
                            scale=1.0)),
                            waits=[("pe", mmv), ("d_scalar", p.c("d_scalar"))], inc="act")
                        p.nop("tensor", waits=[("act", aw)])
                    p.dma("scalar", po_view(sp_s.ap(), T)[:, :, t0:t0 + TS5],
                          po_view(sfm, TS5), waits=[("act", p.c("act"))])
                p.nop("sync", waits=[("d_scalar", p.c("d_scalar"))])
                p.emit("ph7" in PHASES_L1)

        _s8 = contextlib.ExitStack()
        with _s8:
            zacc = _s8.enter_context(nc.sbuf_tensor("zacc8_y", [P, NTC * D], F32))
            _s8i = contextlib.ExitStack()
            w1e = _s8i.enter_context(nc.sbuf_tensor("w1e_y", [P, PO * MI], BF16))
            w1e2 = _s8i.enter_context(nc.sbuf_tensor("w1e2_y", [P, PO * MI], BF16))
            w2e = _s8i.enter_context(nc.sbuf_tensor("w2e_y", [P, (MI // P) * D], BF16))
            midb = _s8i.enter_context(nc.sbuf_tensor("midb_y", [P, (MI // P) * 512], BF16))
            xn2t8 = _s8i.enter_context(nc.sbuf_tensor("xn2t8_y", [P, PO * 512], BF16))
            xn2t8b = _s8i.enter_context(nc.sbuf_tensor("xn2t8b_y", [P, PO * 512], BF16))
            b1c = _s8i.enter_context(nc.sbuf_tensor("b1c_y", [P, MI // P], F32))
            b1c2 = _s8i.enter_context(nc.sbuf_tensor("b1c2_y", [P, MI // P], F32))
            b2row = _s8i.enter_context(nc.sbuf_tensor("b2row_y", [1, D], F32))
            b2row2 = _s8i.enter_context(nc.sbuf_tensor("b2row2_y", [1, D], F32))
            ps_f1 = _s8i.enter_context(nc.psum_tensor("ps_f18_y", [P, 512], F32))
            ps_f1b = _s8i.enter_context(nc.psum_tensor("ps_f18b_y", [P, 512], F32))
            ps_z = [_s8i.enter_context(nc.psum_tensor(f"ps_z{i}", [P, 512], F32))
                    for i in range(4)]

            p = Prog(nc, sems, cnt)
            if DEBUG_L1:
                p.dma("scalar", dbg_s, sp_s.ap())
                p.nop("sync", waits=[("d_scalar", p.c("d_scalar"))])
            NCHm = MI // P
            w1b = [w1e, w1e2]
            b1b = [b1c, b1c2]
            b2b = [b2row, b2row2]
            xnb = [xn2t8, xn2t8b]
            psf = [ps_f1, ps_f1b]
            w1rdy = [0, 0]
            b1rdy = [0, 0]
            b2rdy = [0, 0]
            xrdy = [0, 0]
            fc1_end = [0] * (NEXP + 1)
            z_end = [0] * (NEXP + 1)
            gelu_end = [0] * (NEXP + 1)
            evac_mark = {}
            prev_tail = [0, 0]

            def load_w1(ei, wait_pe):
                sb = ei % 2
                w1rdy[sb] = p.dma(
                    "gpsimd", w1b[sb].ap().rearrange("p (k n) -> p k n", k=PO),
                    w1m[ei].rearrange("(k p) n -> p k n", p=P),
                    waits=[("pe", wait_pe)])

            def load_b(ei, wait_act, wait_pe):
                sb = ei % 2
                b1rdy[sb] = p.dma("sync", b1b[sb][:], b1m[ei], waits=[("act", wait_act)])
                b2rdy[sb] = p.dma("sync", b2b[sb][:], b2m[0:1, ei * D:(ei + 1) * D],
                                  waits=[("pe", wait_pe)])

            load_w1(0, 0)
            load_b(0, 0, 0)
            for e_i in range(NEXP):
                sb = e_i % 2
                lw2 = p.dma("gpsimd", w2e.ap().rearrange("p (k m) -> p k m", k=NCHm),
                            w2m[e_i].rearrange("(k p) m -> p k m", p=P),
                            waits=[("pe", z_end[e_i])])
                if e_i + 1 < NEXP:
                    load_w1(e_i + 1, fc1_end[e_i])
                    load_b(e_i + 1, gelu_end[e_i], z_end[e_i])
                for ts in range(4):
                    t0 = ts * 512
                    xb = ts % 2
                    if e_i == 0 and ts < 2:
                        xrdy[xb] = p.dma("sync", po_view(xnb[xb], 512),
                                         po_view(sp_xn2.ap(), T)[:, :, t0:t0 + 512])
                    gelu_marks = []
                    for nch in range(NCHm):
                        pf = psf[nch % 2]
                        wts = []
                        if nch == 0 and ts == 0:
                            wts = [("d_gpsimd", w1rdy[sb]), ("d_sync", b1rdy[sb])]
                        if nch == 0:
                            wts.append(("d_sync", xrdy[xb]))
                        if nch >= 2:
                            wts.append(("act", gelu_marks[nch - 2]))
                        else:
                            wts.append(("act", prev_tail[nch]))
                        for kc in range(PO):
                            p.op("tensor", (lambda e, nch=nch, kc=kc, sb=sb, xb=xb, pf=pf: e.matmul(
                                pf[:], w1b[sb][:, kc * MI + nch * P: kc * MI + (nch + 1) * P],
                                xnb[xb][:, kc * 512:(kc + 1) * 512],
                                start=(kc == 0), stop=(kc == PO - 1))),
                                waits=(wts if kc == 0 else ()))
                        mmv = p.mark("tensor", "pe")
                        aw = p.op("scalar", (lambda e, nch=nch, sb=sb, pf=pf: e.activation(
                            out=midb[:, nch * 512:(nch + 1) * 512], in_=pf[:], func=AF.Gelu,
                            bias=b1b[sb][:, nch:nch + 1], scale=1.0)),
                            waits=[("pe", mmv)], inc="act")
                        gelu_marks.append(aw)
                    prev_tail = [gelu_marks[-2], gelu_marks[-1]]
                    mid_done = p.c("act")
                    fc1_end[e_i] = p.c("pe")
                    if e_i < NEXP or ts < 3:
                        nts = ts + 2
                        ne = e_i
                        if nts >= 4:
                            nts -= 4
                            ne += 1
                        if ne < NEXP:
                            xrdy[nts % 2] = p.dma(
                                "sync", po_view(xnb[nts % 2], 512),
                                po_view(sp_xn2.ap(), T)[:, :, nts * 512: nts * 512 + 512],
                                waits=[("pe", fc1_end[e_i])])
                    for mh in range(2):
                        for t128 in range(4):
                            psz = ps_z[t128]
                            wts = [("dve", evac_mark.get(t128, 0))]
                            if mh == 0 and t128 == 0:
                                wts += [("act", mid_done), ("d_sync", b2rdy[sb]),
                                        ("d_gpsimd", lw2)]
                            p.op("tensor", (lambda e, psz=psz, mh=mh, sb=sb: e.matmul(
                                psz[:], ones_row[:],
                                b2b[sb][0:1, mh * 512: mh * 512 + 512],
                                start=True, stop=False)), waits=wts)
                            for nch in range(NCHm):
                                p.op("tensor", (lambda e, nch=nch, t128=t128, mh=mh,
                                                psz=psz: e.matmul(
                                    psz[:],
                                    midb[:, nch * 512 + t128 * P: nch * 512 + (t128 + 1) * P],
                                    w2e[:, nch * D + mh * 512: nch * D + mh * 512 + 512],
                                    start=False, stop=(nch == NCHm - 1))))
                            mmz = p.mark("tensor", "pe")
                            tg2 = ts * 4 + t128
                            if e_i == 0:
                                evac_mark[t128] = p.op("vector", (lambda e, tg2=tg2, mh=mh,
                                                                  psz=psz: e.tensor_scalar(
                                    out=zacc[:, tg2 * D + mh * 512: tg2 * D + mh * 512 + 512],
                                    in0=psz[:],
                                    scalar1=c_tm[:, tg2 * NEXP + e_i: tg2 * NEXP + e_i + 1],
                                    scalar2=None, op0=ALU.mult)),
                                    waits=[("pe", mmz)], inc="dve")
                            else:
                                evac_mark[t128] = p.op("vector", (lambda e, tg2=tg2, mh=mh,
                                                                  psz=psz, e_i=e_i:
                                                                  e.scalar_tensor_tensor(
                                    out=zacc[:, tg2 * D + mh * 512: tg2 * D + mh * 512 + 512],
                                    in0=psz[:],
                                    scalar=c_tm[:, tg2 * NEXP + e_i: tg2 * NEXP + e_i + 1],
                                    in1=zacc[:, tg2 * D + mh * 512: tg2 * D + mh * 512 + 512],
                                    op0=ALU.mult, op1=ALU.add)),
                                    waits=[("pe", mmz)], inc="dve")
                    z_end[e_i] = p.c("pe")
                    gelu_end[e_i] = p.c("act")
            p.emit("ph8" in PHASES_L1)
            _s8i.close()

            _s9 = contextlib.ExitStack()
            with _s9:
                gz = _s9.enter_context(nc.sbuf_tensor("gz_y", [P, NTC * D], BF16))
                gfm = _s9.enter_context(nc.sbuf_tensor("gfm_y", [P, PO * T], BF16))
                sfm9 = _s9.enter_context(nc.sbuf_tensor("sfm9_y", [P, PO * TS5], BF16))
                x9 = _s9.enter_context(nc.sbuf_tensor("x9_y", [P, TS5], F32))
                o9 = _s9.enter_context(nc.sbuf_tensor("o9_y", [P, TS5], F32))
                g9 = _s9.enter_context(nc.sbuf_tensor("g9_y", [P, TS5], F32))
                ps_t9 = _s9.enter_context(nc.psum_tensor("ps_t9_y", [P, P], BF16))
                ps_t9b = _s9.enter_context(nc.psum_tensor("ps_t9b_y", [P, P], BF16))

                p = Prog(nc, sems, cnt)
                if DEBUG_L1:
                    p.dma("scalar", dbg_z, zacc[:])
                    p.nop("sync", waits=[("d_scalar", p.c("d_scalar"))])
                for tcg in range(NTC):
                    p.op("scalar", (lambda e, tcg=tcg: e.activation(
                        out=gz[:, tcg * D:(tcg + 1) * D], in_=zacc[:, tcg * D:(tcg + 1) * D],
                        func=AF.Silu)), inc="act")
                silu_done = p.c("act")
                for tcg in range(NTC):
                    for j in range(PO):
                        pst = ps_t9 if (tcg * PO + j) % 2 == 0 else ps_t9b
                        mm = p.op("tensor", (lambda e, tcg=tcg, j=j, pst=pst: e.transpose(
                            pst[:], gz[:, tcg * D + j * P: tcg * D + (j + 1) * P],
                            ident_bf[:])),
                            waits=([("act", silu_done)] if tcg == 0 and j == 0 else
                                   [("dve", p.c("dve"))]), inc="pe")
                        p.op("vector", (lambda e, tcg=tcg, j=j, pst=pst: e.tensor_copy(
                            out=gfm[:, j * T + tcg * P: j * T + (tcg + 1) * P], in_=pst[:])),
                            waits=[("pe", mm)], inc="dve")
                tr9 = p.c("dve")
                for t in range(NT5):
                    t0 = t * TS5
                    ls = p.dma("sync", po_view(sfm9, TS5),
                               po_view(sp_s.ap(), T)[:, :, t0:t0 + TS5],
                               waits=[("dve", p.c("dve"))])
                    for po in range(PO):
                        lx = p.dma("scalar", x9[:], po_view(x1_fm, T)[:, po, t0:t0 + TS5],
                                   waits=[("dve", p.c("dve"))])
                        a1 = p.op("scalar", (lambda e, po=po, t0=t0: e.activation(
                            out=g9[:], in_=gfm[:, po * T + t0: po * T + t0 + TS5],
                            func=AF.Copy)), waits=[("dve", tr9)], inc="act")
                        a2 = p.op("scalar", (lambda e, po=po: e.activation(
                            out=o9[:], in_=sfm9[:, po * TS5:(po + 1) * TS5], func=AF.Copy)),
                            waits=[("d_sync", ls)], inc="act")
                        p.op("vector", lambda e: e.tensor_tensor(
                            out=o9[:], in0=o9[:], in1=g9[:], op=ALU.mult),
                            waits=[("act", a2), ("d_scalar", lx)])
                        dv = p.op("vector", lambda e: e.tensor_tensor(
                            out=o9[:], in0=o9[:], in1=x9[:], op=ALU.add), inc="dve")
                        sw = p.dma("scalar", po_view(out_fm, T)[:, po, t0:t0 + TS5], o9[:],
                                   waits=[("dve", dv)])
                        p.nop("scalar", waits=[("d_scalar", sw)])
                        p.nop("vector", waits=[("d_scalar", sw)])
                p.nop("sync", waits=[("d_scalar", p.c("d_scalar"))])
                p.emit("ph9" in PHASES_L1)



def build_l1(detect_races=True):
    nc = bass.Bass("TRN2", target_bir_lowering=False, detect_race_conditions=detect_races)
    A = _decl_l1(nc)
    sems = {}
    cnt = {}
    stack = contextlib.ExitStack()
    for k in ("d_sync", "d_gpsimd", "d_scalar", "pe", "act", "dve", "pool", "cc"):
        sems[k] = stack.enter_context(nc.semaphore(k))
    with stack:
        _emit_l1(nc, sems, cnt, A)
    return nc


def build_full(detect_races=True):
    nc = bass.Bass("TRN2", target_bir_lowering=False, detect_race_conditions=detect_races)
    A0 = _decl_l0(nc, "Internal")
    A1 = _decl_l1(nc, x1_ap=A0["x1_out"])
    sems = {}
    cnt = {}
    stack = contextlib.ExitStack()
    for k in ("d_sync", "d_gpsimd", "d_scalar", "pe", "act", "dve", "pool", "cc"):
        sems[k] = stack.enter_context(nc.semaphore(k))
    with stack:
        _emit_l0(nc, sems, cnt, {"ph0", "ph1a", "ph1b", "ph2", "ph3a", "ph3b", "ph4"}, A0)
        _emit_l1(nc, sems, cnt, A1)
    return nc


PHASES_L1 = {"ph5", "ph6", "ph7", "ph8", "ph9"}
DEBUG_L1 = False


def prep_l1_inputs(inputs, x1_fm_per_core):
    def quat_w_layout(w):
        return np.ascontiguousarray(
            np.asarray(w).reshape(4, 2, P, 256).transpose(2, 0, 1, 3).reshape(P, 2048))

    base = {
        "qw_r": quat_w_layout(inputs["L1_attn_qW"]),
        "kw_r": quat_w_layout(inputs["L1_attn_kW"]),
        "vw_r": quat_w_layout(inputs["L1_attn_vW"]),
        "ow_r": quat_w_layout(inputs["L1_attn_oW"]),
        "qb_r": np.ascontiguousarray(np.asarray(inputs["L1_attn_qb"]).reshape(1, 1024)),
        "kb_r": np.ascontiguousarray(np.asarray(inputs["L1_attn_kb"]).reshape(1, 1024)),
        "vb_r": np.ascontiguousarray(np.asarray(inputs["L1_attn_vb"]).reshape(1, 1024)),
        "ob_r": np.ascontiguousarray(np.asarray(inputs["L1_attn_ob"]).reshape(1, 1024)),
        "rw": np.ascontiguousarray(np.asarray(inputs["L1_ffn_router_w"])),
        "jcol": np.ascontiguousarray(np.asarray(inputs["L1_attn_J"]).reshape(PO, P).T),
        "rb": np.ascontiguousarray(np.asarray(inputs["L1_ffn_router_b"]).reshape(1, 16)),
        "w1m": np.ascontiguousarray(np.asarray(inputs["L1_ffn_w1"])),
        "b1m": np.ascontiguousarray(
            np.asarray(inputs["L1_ffn_b1"]).reshape(NEXP, MI // P, P).transpose(0, 2, 1)),
        "w2m": np.ascontiguousarray(np.asarray(inputs["L1_ffn_w2"])),
        "b2m": np.ascontiguousarray(np.asarray(inputs["L1_ffn_b2"]).reshape(1, NEXP * D)),
    }
    maps = []
    for core in range(8):
        m = dict(base)
        if x1_fm_per_core[core] is not None:
            m["x1_fm"] = np.ascontiguousarray(x1_fm_per_core[core])
        maps.append(m)
    return maps


def kernel(**inputs):
    nc = build_full()
    maps0 = prep_l0_inputs(inputs)
    maps1 = prep_l1_inputs(inputs, [None] * 8)
    maps = []
    for c in range(8):
        m = dict(maps0[c])
        m.update({k: v for k, v in maps1[c].items() if k != "x1_fm"})
        maps.append(m)
    res = run_bass_kernel_spmd(nc, maps, core_ids=list(range(8)))
    out = np.zeros((2, 8192, 1024), np.float32)
    for core in range(8):
        b, r = core // 4, core % 4
        out[b, r * T:(r + 1) * T, :] = unfm_layout(res.results[core]["out_fm"])
    return out


if __name__ == "__main__":
    pass


